# version 02
# baseline (speedup 1.0000x reference)
import contextlib
import numpy as np
import ml_dtypes
import concourse.bass as bass
import concourse.mybir as mybir
from concourse.bass_utils import run_bass_kernel_spmd

F32 = mybir.dt.float32
BF16 = mybir.dt.bfloat16
AF = mybir.ActivationFunctionType
ALU = mybir.AluOpType
AX = mybir.AxisListType.X
BF = ml_dtypes.bfloat16

MAGIC = 12582912.0
D = 1024
H = 16
NDCH = 8
FF = 4096
NJCH = 32


class Sem:
    def __init__(self, es, nc, name):
        self.h = es.enter_context(nc.semaphore(name))
        self.n = 0

    def inc(self, instr, k=1):
        instr.then_inc(self.h, k)
        self.n += k
        return self.n


class DS:

    def __init__(self, es, nc, name, nbuf=2):
        self.s = [Sem(es, nc, f"{name}{i}") for i in range(nbuf)]
        self.nbuf = nbuf
        self.g = {}

    def sem(self, idx):
        return self.s[idx % self.nbuf]

    def mark(self, idx):
        self.g[idx] = self.sem(idx).n

    def wait(self, eng, idx):
        if idx >= 0 and idx in self.g:
            eng.wait_ge(self.sem(idx).h, self.g[idx])

    def wait_all(self, eng):
        for s in self.s:
            eng.wait_ge(s.h, s.n)


PHASES = 5
NOGELU = 0
M1B2 = 0
M2LITE = 0
M1LITE = 0
NOB = 0
M1NOXM = 0
M1NOXQ = 0
M1NOSP = 0
M1EVC = 0
M1NOHM = 0


def build_core_program(NC=32, ncores=8):
    NT = NC * 128
    NTT = NT // 512
    assert NTT * 512 == NT and NTT >= 1
    nc = bass.Bass("TRN2", target_bir_lowering=False)

    x_h = nc.dram_tensor("x", [NT, D], F32, kind="ExternalInput")
    wq_h = nc.dram_tensor("wqT", [D, D], F32, kind="ExternalInput")
    wk_h = nc.dram_tensor("wkT", [D, D], F32, kind="ExternalInput")
    wv_h = nc.dram_tensor("wvT", [D, D], F32, kind="ExternalInput")
    wo_h = nc.dram_tensor("woT", [D, D], F32, kind="ExternalInput")
    w1_h = nc.dram_tensor("w1T", [D, FF], F32, kind="ExternalInput")
    w2_h = nc.dram_tensor("w2T", [FF, D], F32, kind="ExternalInput")
    b1_h = nc.dram_tensor("b1fm", [128, NJCH], F32, kind="ExternalInput")
    b2_h = nc.dram_tensor("b2row", [1, D], F32, kind="ExternalInput")
    crep_h = nc.dram_tensor("crep", [NT, 512], F32, kind="ExternalInput")
    srep_h = nc.dram_tensor("srep", [NT, 512], F32, kind="ExternalInput")
    ltri_h = nc.dram_tensor("ltri", [128, 128], BF16, kind="ExternalInput")
    mtri_h = nc.dram_tensor("mtri", [128, 512], F32, kind="ExternalInput")
    ones_r_h = nc.dram_tensor("ones_r", [1, 128], BF16, kind="ExternalInput")
    ones_c_h = nc.dram_tensor("ones_c", [128, 1], BF16, kind="ExternalInput")
    ones_rf_h = nc.dram_tensor("ones_rf", [1, 128], F32, kind="ExternalInput")
    ident_h = nc.dram_tensor("identf", [128, 128], F32, kind="ExternalInput")
    mke_h = nc.dram_tensor("mke", [128, 1], F32, kind="ExternalInput")
    mko_h = nc.dram_tensor("mko", [128, 1], F32, kind="ExternalInput")
    out_h = nc.dram_tensor("out", [NT, D], F32, kind="ExternalOutput")

    xn_sp = nc.dram_tensor("xn_sp", [NDCH, 128, NT], BF16)
    kfm_sp = nc.dram_tensor("kfm_sp", [NT, D], BF16)
    ktm_sp = nc.dram_tensor("ktm_sp", [NT, D], BF16)
    v_sp = nc.dram_tensor("v_sp", [NT, D], BF16)
    pkv_sp = nc.dram_tensor("pkv_sp", [NT, 512], BF16)
    pks_sp = nc.dram_tensor("pks_sp", [NC, D], BF16)
    part_sp = nc.dram_tensor("part_sp", [NT, D], F32)
    h_sp = nc.dram_tensor("h_sp", [NC, FF, 128], F32)
    wq2_sp = nc.dram_tensor("wq2_sp", [FF, D], BF16)
    ar3i = nc.dram_tensor("ar3i", [1, 128 * 512 + D], F32)
    ar3o = nc.dram_tensor("ar3o", [1, 128 * 512 + D], F32)
    ar1i = nc.dram_tensor("ar1i", [1, 1], F32)
    ar1o = nc.dram_tensor("ar1o", [1, 1], F32)
    ar2i = nc.dram_tensor("ar2i", [1, 1], F32)
    ar2o = nc.dram_tensor("ar2o", [1, 1], F32)

    es = contextlib.ExitStack()
    E = es.enter_context

    war = E(nc.sbuf_tensor("war", [128, 4 * NDCH * D], BF16))
    WK, WV, WQ, WO = 0, 8192, 16384, 24576

    ltri = E(nc.sbuf_tensor("ltri_s", [128, 128], BF16))
    mtri = E(nc.sbuf_tensor("mtri_s", [128, 512], F32))
    ones_r = E(nc.sbuf_tensor("ones_r_s", [1, 128], BF16))
    ones_c = E(nc.sbuf_tensor("ones_c_s", [128, 1], BF16))
    ones_rf = E(nc.sbuf_tensor("ones_rf_s", [1, 128], F32))
    identf = E(nc.sbuf_tensor("identf_s", [128, 128], F32))
    mke = E(nc.sbuf_tensor("mke_s", [128, 1], F32))
    mko = E(nc.sbuf_tensor("mko_s", [128, 1], F32))
    b1fm = E(nc.sbuf_tensor("b1fm_s", [128, NJCH], F32))
    b2b = E(nc.sbuf_tensor("b2b_s", [1, D], BF16))
    zeros_b = E(nc.sbuf_tensor("zeros_b", [128, 512], BF16))

    lnsum = [E(nc.sbuf_tensor(f"lnsum{i}", [128, 1], F32)) for i in range(2)]
    lnss = [E(nc.sbuf_tensor(f"lnss{i}", [128, 1], F32)) for i in range(2)]
    lnm = [E(nc.sbuf_tensor(f"lnm{i}", [128, 1], F32)) for i in range(2)]
    lnmsq = [E(nc.sbuf_tensor(f"lnmsq{i}", [128, 1], F32)) for i in range(2)]
    lnvar = [E(nc.sbuf_tensor(f"lnvar{i}", [128, 1], F32)) for i in range(2)]
    lnrstd = [E(nc.sbuf_tensor(f"lnrstd{i}", [128, 1], F32)) for i in range(2)]
    lnnm = [E(nc.sbuf_tensor(f"lnnm{i}", [128, 1], F32)) for i in range(2)]
    tmx = [E(nc.sbuf_tensor(f"tmx{i}", [128, 1], F32)) for i in range(2)]
    thm = E(nc.sbuf_tensor("thm", [128, 4], F32))
    thm4 = E(nc.sbuf_tensor("thm4", [128, 1], F32))
    mxacc = E(nc.sbuf_tensor("mxacc", [128, 1], F32))
    hmx = E(nc.sbuf_tensor("hmx", [128, 1], F32))
    w1mx = E(nc.sbuf_tensor("w1mx", [128, 1], F32))
    w2mx = E(nc.sbuf_tensor("w2mx", [128, 1], F32))
    den_s = [E(nc.sbuf_tensor(f"den{i}", [128, H], F32)) for i in range(2)]
    arst = E(nc.sbuf_tensor("arst", [128, 512], F32))
    arks = E(nc.sbuf_tensor("arks", [1, D], F32))
    kvinf = E(nc.sbuf_tensor("kvinf", [128, 512], F32))
    ksuminf = E(nc.sbuf_tensor("ksuminf", [1, D], F32))
    kvin_b = E(nc.sbuf_tensor("kvin_b", [128, 512], BF16))
    ksumin_b = E(nc.sbuf_tensor("ksumin_b", [1, D], BF16))
    g1 = E(nc.sbuf_tensor("g1", [1, 16], F32))
    scrow = E(nc.sbuf_tensor("scrow", [1, 8], F32))
    scrow2 = E(nc.sbuf_tensor("scrow2", [1, 8], F32))
    scb = E(nc.sbuf_tensor("scb", [128, 8], F32))
    scb2 = E(nc.sbuf_tensor("scb2", [128, 8], F32))
    SC_RSX, SC_SWS1, SC_RSW1 = 0, 1, 2
    SC2_RSH, SC2_S2B, SC2_RSW2 = 0, 1, 2
    b1s = E(nc.sbuf_tensor("b1s", [128, NJCH], F32))
    constf = E(nc.sbuf_tensor("constf", [128, 4], F32))

    A16_COLS = 32768
    A32_COLS = 12288
    a16 = E(nc.sbuf_tensor("a16", [128, A16_COLS], BF16))
    a32 = E(nc.sbuf_tensor("a32", [128, A32_COLS], F32))

    def carve(arena, cur, cols, n=None, rows=128):
        if n is None:
            ap = arena[0:rows, cur[0]:cur[0] + cols]
            cur[0] += cols
            return ap
        return [carve(arena, cur, cols, None, rows) for _ in range(n)]

    cS16, cS32 = [0], [0]
    crepb = carve(a16, cS16, 1024, 2)
    ropes = carve(a16, cS16, 2048)
    phi1 = carve(a16, cS16, 1024, 2)
    phi2 = carve(a16, cS16, 1024, 2)
    rowslot = carve(a16, cS16, 1024, 2, rows=1)
    S16 = cS16[0]
    crepf = carve(a32, cS32, 1024, 2)
    wqscr = carve(a32, cS32, 4096)
    w1f = carve(a32, cS32, 4096)
    S32 = cS32[0]

    c16, c32 = [S16], [S32 - 8192]
    xbuf = carve(a32, c32, 1024, 2)
    xn32 = carve(a32, c32, 1024, 2)
    sqscr = carve(a32, c32, 1024, 2)
    xntm = carve(a16, c16, 1024, 2)
    xnfmb = carve(a16, c16, 1024, 2)
    ktmp = carve(a16, c16, 1024, 2)
    krop = carve(a16, c16, 1024, 2)
    ktil = krop
    vbuf = carve(a16, c16, 1024, 2)
    kfmb = carve(a16, c16, 1024, 2)
    pkvst = carve(a16, c16, 512, 2)
    pksst = rowslot
    assert c16[0] <= A16_COLS and c32[0] <= A32_COLS, (c16, c32)

    c16, c32 = [S16], [S32]
    xrel = carve(a32, c32, 1024)
    opart = carve(a32, c32, 1024)
    xnB = carve(a16, c16, 1024, 2)
    ktmB = carve(a16, c16, 1024, 2)
    kfmB = carve(a16, c16, 1024, 2)
    vB = carve(a16, c16, 1024, 2)
    qtmp = carve(a16, c16, 1024, 2)
    qrop = carve(a16, c16, 1024, 2)
    qtil = qrop
    qhat = qrop
    qfmB = carve(a16, c16, 1024, 2)
    ptot = carve(a16, c16, 512, 2)
    prow = rowslot
    kcb = carve(a16, c16, 1024, 2)
    rdrep = carve(a16, c16, 1024)
    stmask = carve(a16, c16, 2048)
    ytm = carve(a16, c16, 1024, 2)
    yfm = ktmB
    assert c16[0] <= A16_COLS and c32[0] <= A32_COLS, (c16, c32)

    c16, c32 = [S16], [S32 - 4096]
    w2f = carve(a32, c32, 1024, 2)
    hb_off = c32[0]
    hb = carve(a32, c32, 512, 4)
    hball = a32[:, hb_off:hb_off + 2048]
    qscr = carve(a32, c32, 2048)
    xnm1 = carve(a16, c16, 4096, 2)
    xqb = carve(a16, c16, 4096, 2)
    wq2c = carve(a16, c16, 1024, 2)
    assert c16[0] <= A16_COLS and c32[0] <= A32_COLS, (c16, c32)

    c16, c32 = [S16], [S32 - 8192]
    hblf = carve(a32, c32, 4096, 2)
    oprel = carve(a32, c32, 1024)
    fin = carve(a32, c32, 1024)
    hqb = carve(a16, c16, 4096, 2)
    assert c16[0] <= A16_COLS and c32[0] <= A32_COLS, (c16, c32)

    psA = [E(nc.psum_tensor(f"psA{i}", [128, 512], F32)) for i in range(2)]
    psB = [E(nc.psum_tensor(f"psB{i}", [128, 512], F32)) for i in range(2)]
    kvps = E(nc.psum_tensor("kvps", [128, 512], F32))
    ksps = E(nc.psum_tensor("ksps", [1, D], F32))
    misc = E(nc.psum_tensor("miscps", [128, 136], F32))
    smps = misc[0:1, 0:128]
    bcps = misc[:, 128:136]

    d_ld = DS(es, nc, "d_ld")
    d_xt = DS(es, nc, "d_xt")
    d_kt = DS(es, nc, "d_kt")
    d_qt = DS(es, nc, "d_qt")
    d_yt = DS(es, nc, "d_yt")
    d_rl = DS(es, nc, "d_rl")
    d_xr = DS(es, nc, "d_xr")
    d_w1 = DS(es, nc, "d_w1")
    d_w2 = DS(es, nc, "d_w2")
    d_xm = DS(es, nc, "d_xm")
    d_hb = DS(es, nc, "d_hb")
    d_q2 = DS(es, nc, "d_q2")
    w_a = DS(es, nc, "w_a")
    w_pt = DS(es, nc, "w_pt")
    w_h = DS(es, nc, "w_h")
    w_w2 = DS(es, nc, "w_w2")
    w_out = DS(es, nc, "w_out")
    s_gw = Sem(es, nc, "s_gw")
    s_gb = Sem(es, nc, "s_gb")
    s_cc = Sem(es, nc, "s_cc")
    s_cn = Sem(es, nc, "s_cn")
    smt = Sem(es, nc, "smt")
    sms = Sem(es, nc, "sms")
    smv = Sem(es, nc, "smv")

    blk = E(nc.Block())
    cnt = {}

    def wt(eng, sem, key):
        if key in cnt:
            eng.wait_ge(sem.h, cnt[key])

    def _finish():
        @blk.gpsimd
        def _(g):
            g.wait_ge(smt.h, smt.n)
            g.wait_ge(sms.h, sms.n)
            g.wait_ge(smv.h, smv.n)
            for dsx in (d_ld, d_xt, d_kt, d_qt, d_yt, d_rl, d_xr, d_w1,
                        d_w2, d_xm, d_hb, d_q2, w_a, w_pt, w_h, w_w2, w_out):
                dsx.wait_all(g)
        es.close()
        return nc

    def r3(t):
        return t.rearrange("p (h d) -> p h d", d=64)

    def c3(t):
        return t.rearrange("p (h j) -> p h j", j=32)

    @blk.sync
    def _(s):
        for dst, src in ((ltri, ltri_h), (mtri, mtri_h), (ones_r, ones_r_h),
                         (ones_c, ones_c_h), (ones_rf, ones_rf_h), (identf, ident_h),
                         (mke, mke_h), (mko, mko_h), (b1fm, b1_h)):
            s_cn.inc(s.dma_start(out=dst[:, :], in_=src.ap()), 16)

    @blk.gpsimd
    def _(g):
        for off, src in ((WK, wk_h), (WV, wv_h), (WQ, wq_h), (WO, wo_h)):
            s_gw.inc(g.dma_start(out=war[:, off:off + 8192],
                                 in_=src.ap().rearrange("(j p) d -> p j d", p=128)), 16)
        s_gw.inc(g.dma_start(out=b2b[:, :], in_=b2_h.ap()), 16)

    for i, val in enumerate((1e-5, -1.0, MAGIC, -MAGIC)):
        nc.const_aps.register(F32, val, constf[:, i:i + 1])

    @blk.vector
    def _(v):
        for i, val in enumerate((1e-5, -1.0, MAGIC, -MAGIC)):
            v.memset(constf[:, i:i + 1], val)
        v.memset(zeros_b[:, :], 0.0)
        v.memset(scrow[:, :], 0.0)
        v.memset(scrow2[:, :], 0.0)
        v.memset(mxacc[:, :], 0.0)
        v.memset(hmx[:, :], 0.0)
        v.memset(w1mx[:, :], 0.0)
        v.memset(w2mx[:, :], 0.0)
        smv.inc(v.drain())
        cnt["v_init"] = smv.n

    for c in range(NC):
        p = c % 2

        @blk.sync
        def _(s, c=c, p=p):
            if c >= 2:
                wt(s, sms, ("a_s_xn", c - 2))
                wt(s, smv, ("a_v_cast", c - 2))
            d_ld.sem(c).inc(s.dma_start(out=xbuf[p][:, :],
                                        in_=x_h.ap()[c * 128:(c + 1) * 128, :]), 16)
            d_ld.sem(c).inc(s.dma_start(out=crepf[p][:, 0:512],
                                        in_=crep_h.ap()[c * 128:(c + 1) * 128, :]), 16)
            d_ld.sem(c).inc(s.dma_start(out=crepf[p][:, 512:1024],
                                        in_=srep_h.ap()[c * 128:(c + 1) * 128, :]), 16)
            d_ld.mark(c)

        @blk.vector
        def _(v, c=c, p=p):
            d_ld.wait(v, c)
            v.reduce_sum(lnsum[p][:, :], xbuf[p][:, :], AX)
            smv.inc(v.drain())
            cnt[("a_v_sum", c)] = smv.n

        @blk.scalar
        def _(sc, c=c, p=p):
            d_ld.wait(sc, c)
            sms.inc(sc.activation(sqscr[p][:, :], xbuf[p][:, :], AF.Square,
                                  accum_out=lnss[p][:, 0:1]))
            cnt[("a_s_ss", c)] = sms.n

        @blk.vector
        def _(v, c=c, p=p):
            wt(v, sms, ("a_s_ss", c))
            v.tensor_scalar_mul(lnm[p][:, :], lnsum[p][:, :], 1.0 / D)
            v.drain()
            v.tensor_mul(lnmsq[p][:, :], lnm[p][:, :], lnm[p][:, :])
            v.drain()
            v.scalar_tensor_tensor(lnvar[p][:, :], lnss[p][:, :], 1.0 / D,
                                   lnmsq[p][:, :], ALU.mult, ALU.subtract)
            smv.inc(v.drain())
            cnt[("a_v_var", c)] = smv.n

        @blk.scalar
        def _(sc, c=c, p=p):
            wt(sc, smv, ("a_v_var", c))
            sms.inc(sc.activation(lnvar[p][:, :], lnvar[p][:, :], AF.Sqrt, bias=1e-5))
            cnt[("a_s_rstd", c)] = sms.n

        @blk.vector
        def _(v, c=c, p=p):
            wt(v, sms, ("a_s_rstd", c))
            v.reciprocal(lnrstd[p][:, :], lnvar[p][:, :])
            v.drain()
            v.scalar_tensor_tensor(lnnm[p][:, :], lnm[p][:, :], -1.0,
                                   lnrstd[p][:, :], ALU.mult, ALU.mult)
            smv.inc(v.drain())
            cnt[("a_v_nm", c)] = smv.n

        @blk.scalar
        def _(sc, c=c, p=p):
            wt(sc, smv, ("a_v_nm", c))
            sms.inc(sc.activation(xn32[p][:, :], xbuf[p][:, :], AF.Identity,
                                  scale=lnrstd[p][:, 0:1], bias=lnnm[p][:, 0:1]))
            cnt[("a_s_xn", c)] = sms.n

        @blk.vector
        def _(v, c=c, p=p):
            wt(v, sms, ("a_s_xn", c))
            v.reduce_max(tmx[p][:, :], xn32[p][:, :], AX, apply_absolute_value=True)
            v.drain()
            v.tensor_max(mxacc[:, :], mxacc[:, :], tmx[p][:, :])
            v.drain()
            v.tensor_copy(xntm[p][:, :], xn32[p][:, :])
            v.tensor_copy(crepb[p][:, :], crepf[p][:, :])
            smv.inc(v.drain())
            cnt[("a_v_cast", c)] = smv.n

        @blk.sync
        def _(s, c=c, p=p):
            wt(s, smv, ("a_v_cast", c))
            if c >= 2:
                wt(s, smt, ("a_t_proj", c - 2))
                w_a.wait(s, c - 2)
            for j in range(NDCH):
                d_xt.sem(c).inc(s.dma_start(out=xnfmb[p][:, j * 128:(j + 1) * 128],
                                            in_=xntm[p][:, j * 128:(j + 1) * 128],
                                            transpose=True), 16)
            d_xt.mark(c)


        @blk.tensor
        def _(t, c=c, p=p):
            d_xt.wait(t, c)
            if c == 0:
                t.wait_ge(s_gw.h, s_gw.n)
                t.wait_ge(s_cn.h, s_cn.n)
            if c >= 1:
                wt(t, sms, ("a_s_evkv", c - 1))
            for (ps, woff) in ((psA, WV), (psB, WK)):
                for i in range(2):
                    for j in range(NDCH):
                        t.matmul(ps[i][:, :], xnfmb[p][:, j * 128:(j + 1) * 128],
                                 war[:, woff + j * D + i * 512: woff + j * D + i * 512 + 512],
                                 start=(j == 0), stop=(j == NDCH - 1))
            smt.inc(t.drain())
            cnt[("a_t_proj", c)] = smt.n

        @blk.scalar
        def _(sc, c=c, p=p):
            wt(sc, smt, ("a_t_proj", c))
            if c >= 2:
                w_a.wait(sc, c - 2)
            sc.copy(vbuf[p][:, 0:512], psA[0][:, :])
            sc.copy(vbuf[p][:, 512:1024], psA[1][:, :])
            sc.copy(ktmp[p][:, 0:512], psB[0][:, :])
            sms.inc(sc.copy(ktmp[p][:, 512:1024], psB[1][:, :]))
            cnt[("a_s_evkv", c)] = sms.n

        @blk.vector
        def _(v, c=c, p=p):
            wt(v, sms, ("a_s_evkv", c))
            if c >= 2:
                wt(v, smt, ("a_t_upd", c - 2))
                d_kt.wait(v, c - 2)
                w_a.wait(v, c - 2)
            k1 = r3(ktmp[p])[:, :, 0:32]
            k2 = r3(ktmp[p])[:, :, 32:64]
            o1 = r3(krop[p])[:, :, 0:32]
            o2 = r3(krop[p])[:, :, 32:64]
            cc_ = c3(crepb[p][:, 0:512])
            ss_ = c3(crepb[p][:, 512:1024])
            sA, sB2 = c3(ropes[:, 0:512]), c3(ropes[:, 512:1024])
            sC, sD = c3(ropes[:, 1024:1536]), c3(ropes[:, 1536:2048])
            v.drain()
            v.tensor_tensor(sA, k1, cc_, ALU.mult)
            v.tensor_tensor(sB2, k2, ss_, ALU.mult)
            v.tensor_tensor(sC, k1, ss_, ALU.mult)
            v.tensor_tensor(sD, k2, cc_, ALU.mult)
            v.drain()
            v.tensor_tensor(o1, sA, sB2, ALU.subtract)
            v.tensor_tensor(o2, sC, sD, ALU.add)
            smv.inc(v.drain())
            cnt[("a_v_rope", c)] = smv.n

        @blk.scalar
        def _(sc, c=c, p=p):
            wt(sc, smv, ("a_v_rope", c))
            sc.activation(phi1[p][:, :], krop[p][:, :], AF.Relu, scale=-1.0)
            sc.drain()
            sms.inc(sc.activation(phi2[p][:, :], phi1[p][:, :], AF.Exp, scale=-1.0))
            cnt[("a_s_phi", c)] = sms.n

        @blk.vector
        def _(v, c=c, p=p):
            wt(v, sms, ("a_s_phi", c))
            v.scalar_tensor_tensor(ktil[p][:, :], krop[p][:, :], 0.0,
                                   phi2[p][:, :], ALU.max, ALU.add)
            smv.inc(v.drain())
            cnt[("a_v_ktil", c)] = smv.n
            if c >= 1:
                wt(v, smt, ("a_t_upd", c - 1))

                smv.inc(v.tensor_copy(pkvst[p][:, :], kvps[:, :]))
                cnt[("a_v_pref", c)] = smv.n

        @blk.scalar
        def _(sc, c=c, p=p):
            if c >= 1:
                wt(sc, smt, ("a_t_upd", c - 1))

                sms.inc(sc.copy(pksst[p][:, :], ksps[:, :]))
                cnt[("a_s_prefks", c)] = sms.n


        @blk.sync
        def _(s, c=c, p=p):
            wt(s, smv, ("a_v_ktil", c))
            if c >= 2:
                w_a.wait(s, c - 2)
            for j in range(NDCH):
                d_kt.sem(c).inc(s.dma_start(out=kfmb[p][:, j * 128:(j + 1) * 128],
                                            in_=ktil[p][:, j * 128:(j + 1) * 128],
                                            transpose=True), 16)
            d_kt.mark(c)

        @blk.gpsimd
        def _(g, c=c, p=p):
            d_xt.wait(g, c)
            w_a.sem(c).inc(g.dma_start(
                out=xn_sp.ap()[:, :, c * 128:(c + 1) * 128].transpose([1, 0, 2]),
                in_=xnfmb[p][:, :].rearrange("p (j t) -> p j t", t=128)), 16)
            d_kt.wait(g, c)
            w_a.sem(c).inc(g.dma_start(out=kfm_sp.ap()[c * 128:(c + 1) * 128, :],
                                       in_=kfmb[p][:, :]), 16)
            w_a.sem(c).inc(g.dma_start(out=ktm_sp.ap()[c * 128:(c + 1) * 128, :],
                                       in_=ktil[p][:, :]), 16)
            wt(g, sms, ("a_s_evkv", c))
            w_a.sem(c).inc(g.dma_start(out=v_sp.ap()[c * 128:(c + 1) * 128, :],
                                       in_=vbuf[p][:, :]), 16)
            if c >= 1:
                wt(g, smv, ("a_v_pref", c))
                w_a.sem(c).inc(g.dma_start(out=pkv_sp.ap()[c * 128:(c + 1) * 128, :],
                                           in_=pkvst[p][:, :]), 16)
                wt(g, sms, ("a_s_prefks", c))
                w_a.sem(c).inc(g.dma_start(out=pks_sp.ap()[c:c + 1, :],
                                           in_=pksst[p][:, :]), 16)
            w_a.mark(c)

        @blk.tensor
        def _(t, c=c, p=p):
            wt(t, smv, ("a_v_ktil", c))
            if c == 0:
                wt(t, smv, "v_init")
                t.matmul(kvps[:, :], ltri[:, :], zeros_b[:, :], start=True, stop=False,
                         skip_group_check=True)
            else:
                wt(t, smv, ("a_v_pref", c))
                wt(t, sms, ("a_s_prefks", c))
            for h in range(H):
                hp, hh = h % 2, h // 2
                t.matmul(kvps[64 * hp:64 * hp + 64, 64 * hh:64 * hh + 64],
                         ktil[p][:, h * 64:(h + 1) * 64], vbuf[p][:, h * 64:(h + 1) * 64],
                         start=False, stop=False, tile_position=(0, 64 * hp),
                         skip_group_check=True)
            for i in range(2):
                t.matmul(ksps[:, i * 512:(i + 1) * 512], ones_c[:, :],
                         ktil[p][:, i * 512:(i + 1) * 512],
                         start=(c == 0), stop=(c == NC - 1), skip_group_check=True)
            smt.inc(t.drain())
            cnt[("a_t_upd", c)] = smt.n

    if PHASES < 2:
        return _finish()

    @blk.vector
    def _(v):
        wt(v, smt, ("a_t_upd", NC - 1))
        v.tensor_scalar_mul(arst[:, :], kvps[:, :], mke[:, 0:1])
        smv.inc(v.drain())
        cnt["ar_st"] = smv.n

    @blk.scalar
    def _(sc):
        wt(sc, smt, ("a_t_upd", NC - 1))
        sms.inc(sc.activation(arks[:, :], ksps[:, :], AF.Copy, scale=mke[0:1, 0:1]))
        cnt["ar_ks"] = sms.n

    @blk.tensor
    def _(t):
        wt(t, smv, ("a_v_cast", NC - 1))
        smt.inc(t.transpose(smps, mxacc[:, :], identf[:, :]))
        cnt["t_mxtp"] = smt.n

    @blk.vector
    def _(v):
        wt(v, smt, "t_mxtp")
        v.reduce_max(g1[0:1, 0:1], smps, AX)
        smv.inc(v.drain())
        cnt["v_mxg"] = smv.n

    @blk.gpsimd
    def _(g):
        w_a.wait_all(g)
        wt(g, smv, "ar_st")
        wt(g, sms, "ar_ks")
        kvv = ar3i.ap()[0:1, 0:65536].rearrange("o (p c) -> o p c", p=128).squeeze(0)
        s_gb.inc(g.dma_start(out=kvv, in_=arst[:, :]), 16)
        s_gb.inc(g.dma_start(out=ar3i.ap()[0:1, 65536:65536 + D], in_=arks[:, :]), 16)
        g.wait_ge(s_gb.h, s_gb.n)
        grp3 = [[2 * i, 2 * i + 1] for i in range(ncores // 2)] if ncores > 1 else [[0]]
        s_cc.inc(g.collective_compute("AllReduce", ALU.add, replica_groups=grp3,
                                      ins=[ar3i.ap().opt()], outs=[ar3o.ap().opt()]))
        g.wait_ge(s_cc.h, s_cc.n)
        kvo = ar3o.ap()[0:1, 0:65536].rearrange("o (p c) -> o p c", p=128).squeeze(0)
        s_gb.inc(g.dma_start(out=kvinf[:, :], in_=kvo), 16)
        s_gb.inc(g.dma_start(out=ksuminf[:, :], in_=ar3o.ap()[0:1, 65536:65536 + D]), 16)
        cnt["g_ar3"] = s_gb.n
        g.wait_ge(s_gb.h, s_gb.n)
        wt(g, smv, "v_mxg")
        s_gb.inc(g.dma_start(out=ar1i.ap(), in_=g1[0:1, 0:1]), 16)
        g.wait_ge(s_gb.h, s_gb.n)
        s_cc.inc(g.collective_compute("AllReduce", ALU.max,
                                      replica_groups=[list(range(ncores))],
                                      ins=[ar1i.ap().opt()], outs=[ar1o.ap().opt()]))
        g.wait_ge(s_cc.h, s_cc.n)
        s_gb.inc(g.dma_start(out=g1[0:1, 1:2], in_=ar1o.ap()), 16)
        cnt["g_ar1"] = s_gb.n
        g.wait_ge(s_gb.h, s_gb.n)

    @blk.vector
    def _(v):
        wt(v, s_gb, "g_ar3")
        v.tensor_scalar_mul(kvin_b[:, :], kvinf[:, :], mko[:, 0:1])
        v.tensor_scalar_mul(ksumin_b[:, :], ksuminf[:, :], mko[0:1, 0:1])
        smv.inc(v.drain())
        cnt["v_kvin"] = smv.n
        wt(v, s_gb, "g_ar1")
        v.tensor_scalar_max(g1[0:1, 2:3], g1[0:1, 1:2], 1e-6)
        v.drain()
        v.tensor_scalar_mul(g1[0:1, 3:4], g1[0:1, 2:3], 1.0 / 127.0)
        v.drain()
        v.reciprocal(scrow[0:1, SC_RSX:SC_RSX + 1], g1[0:1, 3:4])
        smv.inc(v.drain())
        cnt["v_sc1a"] = smv.n

    if PHASES < 3:
        return _finish()

    def emit_w1_max(d):
        @blk.sync
        def _(s, d=d):
            if d >= 1:
                wt(s, smv, ("w1mx", d - 1))
            d_w1.sem(d).inc(s.dma_start(out=w1f[:, :],
                                        in_=w1_h.ap()[d * 128:(d + 1) * 128, :]), 16)
            d_w1.mark(d)

        @blk.vector
        def _(v, d=d):
            d_w1.wait(v, d)
            v.reduce_max(tmx[d % 2][:, :], w1f[:, :], AX, apply_absolute_value=True)
            v.drain()
            v.tensor_max(w1mx[:, :], w1mx[:, :], tmx[d % 2][:, :])
            smv.inc(v.drain())
            cnt[("w1mx", d)] = smv.n

    def emit_w1_scale():
        @blk.tensor
        def _(t):
            wt(t, smv, ("w1mx", NDCH - 1))
            smt.inc(t.transpose(smps, w1mx[:, :], identf[:, :]))
            cnt["t_w1tp"] = smt.n

        @blk.vector
        def _(v):
            wt(v, smt, "t_w1tp")
            v.reduce_max(g1[0:1, 4:5], smps, AX)
            v.drain()
            v.tensor_scalar_max(g1[0:1, 5:6], g1[0:1, 4:5], 1e-6)
            v.drain()
            v.tensor_scalar_mul(g1[0:1, 6:7], g1[0:1, 5:6], 1.0 / 127.0)
            v.drain()
            v.reciprocal(scrow[0:1, SC_RSW1:SC_RSW1 + 1], g1[0:1, 6:7])
            v.drain()
            wt(v, smv, "v_sc1a")
            v.tensor_mul(scrow[0:1, SC_SWS1:SC_SWS1 + 1], g1[0:1, 3:4], g1[0:1, 6:7])
            smv.inc(v.drain())
            cnt["v_sc1"] = smv.n

        @blk.tensor
        def _(t):
            wt(t, smv, "v_sc1")
            smt.inc(t.matmul(bcps, ones_rf[:, :], scrow[:, :], start=True, stop=True))
            cnt["t_bc1"] = smt.n

        @blk.vector
        def _(v):
            wt(v, smt, "t_bc1")
            v.tensor_copy(scb[:, :], bcps)
            v.drain()
            v.tensor_scalar_mul(b1s[:, :], b1fm[:, :], scb[:, SC_SWS1:SC_SWS1 + 1])
            smv.inc(v.drain())
            cnt["v_bc1"] = smv.n

    def emit_w1_quant(d, guard):
        @blk.sync
        def _(s, d=d):
            if d >= 1:
                wt(s, sms, ("w1q", d - 1))
            else:
                wt(s, smv, ("w1mx", NDCH - 1))
            d_w1.sem(NDCH + d).inc(s.dma_start(out=w1f[:, :],
                                               in_=w1_h.ap()[d * 128:(d + 1) * 128, :]), 16)
            d_w1.mark(NDCH + d)

        @blk.scalar
        def _(sc, d=d, guard=guard):
            d_w1.wait(sc, NDCH + d)
            wt(sc, smv, "v_bc1")
            if guard is not None:
                sc.wait_ge(smt.h, guard)
            sc.activation(wqscr[:, :], w1f[:, :], AF.Copy,
                          scale=scb[:, SC_RSW1:SC_RSW1 + 1], bias=MAGIC)
            sc.drain()
            sms.inc(sc.add(war[:, d * FF:(d + 1) * FF], wqscr[:, :], -MAGIC))
            cnt[("w1q", d)] = sms.n

    def emit_w2_max(jw):
        @blk.sync
        def _(s, jw=jw):
            if jw >= 2:
                wt(s, smv, ("w2mx", jw - 2))
            else:
                wt(s, sms, ("w1q", NDCH - 1))
            d_w2.sem(jw).inc(s.dma_start(out=w2f[jw % 2][:, :],
                                         in_=w2_h.ap()[jw * 128:(jw + 1) * 128, :]), 16)
            d_w2.mark(jw)

        @blk.vector
        def _(v, jw=jw):
            d_w2.wait(v, jw)
            v.reduce_max(tmx[jw % 2][:, :], w2f[jw % 2][:, :], AX, apply_absolute_value=True)
            v.drain()
            v.tensor_max(w2mx[:, :], w2mx[:, :], tmx[jw % 2][:, :])
            smv.inc(v.drain())
            cnt[("w2mx", jw)] = smv.n

    def emit_w2_scale():
        @blk.tensor
        def _(t):
            wt(t, smv, ("w2mx", NJCH - 1))
            smt.inc(t.transpose(smps, w2mx[:, :], identf[:, :]))
            cnt["t_w2tp"] = smt.n

        @blk.vector
        def _(v):
            wt(v, smt, "t_w2tp")
            v.reduce_max(g1[0:1, 7:8], smps, AX)
            v.drain()
            v.tensor_scalar_max(g1[0:1, 8:9], g1[0:1, 7:8], 1e-6)
            v.drain()
            v.tensor_scalar_mul(g1[0:1, 9:10], g1[0:1, 8:9], 1.0 / 127.0)
            v.drain()
            v.reciprocal(scrow2[0:1, SC2_RSW2:SC2_RSW2 + 1], g1[0:1, 9:10])
            smv.inc(v.drain())
            cnt["v_sc2w"] = smv.n

        @blk.tensor
        def _(t):
            wt(t, smv, "v_sc2w")
            smt.inc(t.matmul(bcps, ones_rf[:, :], scrow2[:, :], start=True, stop=True))
            cnt["t_bc2w"] = smt.n

        @blk.vector
        def _(v):
            wt(v, smt, "t_bc2w")
            v.tensor_copy(scb2[:, :], bcps)
            smv.inc(v.drain())
            cnt["v_bc2w"] = smv.n

    def emit_w2_quant(jw):
        @blk.sync
        def _(s, jw=jw):
            if jw >= 2:
                w_w2.wait(s, jw - 2)
                wt(s, sms, ("w2q", jw - 2))
            else:
                wt(s, smv, ("w2mx", NJCH - 2 + jw))
            d_w2.sem(NJCH + jw).inc(s.dma_start(out=w2f[jw % 2][:, :],
                                                in_=w2_h.ap()[jw * 128:(jw + 1) * 128, :]), 16)
            d_w2.mark(NJCH + jw)

        @blk.scalar
        def _(sc, jw=jw):
            d_w2.wait(sc, NJCH + jw)
            wt(sc, smv, "v_bc2w")
            sc.activation(w2f[jw % 2][:, :], w2f[jw % 2][:, :], AF.Copy,
                          scale=scb2[:, SC2_RSW2:SC2_RSW2 + 1], bias=MAGIC)
            sc.drain()
            sms.inc(sc.add(wq2c[jw % 2][:, :], w2f[jw % 2][:, :], -MAGIC))
            cnt[("w2q", jw)] = sms.n
            sc.drain()
            w_w2.sem(jw).inc(sc.dma_start(out=wq2_sp.ap()[jw * 128:(jw + 1) * 128, :],
                                          in_=wq2c[jw % 2][:, :]), 16)
            w_w2.mark(jw)

    w1max_sched = {4 + d: d for d in range(NDCH)} if NC >= 24 else {}
    w1scale_at = 13 if NC >= 24 else None
    w1q_sched = {14 + d: d for d in range(4)} if NC >= 24 else {}

    for c in (range(0) if NOB else range(NC)):
        p = c % 2

        @blk.sync
        def _(s, c=c, p=p):
            if c >= 2:
                wt(s, smt, ("b_t_y", c - 2))
                wt(s, smt, ("b_t_kc", c - 2))
                wt(s, smt, ("b_t_q", c - 2))
                wt(s, smt, ("b_t_st", c - 2, 1))
                wt(s, smv, ("b_v_opart", c - 2))
                wt(s, smv, ("b_v_ptot", c - 2))
            if c == 0:
                w_a.wait_all(s)
                s.wait_ge(s_gb.h, 96)
            for dst, sp in ((kfmB[p], kfm_sp), (ktmB[p], ktm_sp), (vB[p], v_sp)):
                d_rl.sem(c).inc(s.dma_start(out=dst[:, :],
                                            in_=sp.ap()[c * 128:(c + 1) * 128, :]), 16)
            d_rl.sem(c).inc(s.dma_start(out=crepf[p][:, 0:512],
                                        in_=crep_h.ap()[c * 128:(c + 1) * 128, :]), 16)
            d_rl.sem(c).inc(s.dma_start(out=crepf[p][:, 512:1024],
                                        in_=srep_h.ap()[c * 128:(c + 1) * 128, :]), 16)
            d_rl.sem(c).inc(s.dma_start(out=xnB[p][:, :],
                                        in_=xn_sp.ap()[:, :, c * 128:(c + 1) * 128]
                                        .transpose([1, 0, 2])), 16)
            if c >= 1:
                d_rl.sem(c).inc(s.dma_start(out=ptot[p][:, :],
                                            in_=pkv_sp.ap()[c * 128:(c + 1) * 128, :]), 16)
                d_rl.sem(c).inc(s.dma_start(out=prow[p][:, :],
                                            in_=pks_sp.ap()[c:c + 1, :]), 16)
            d_rl.mark(c)

        @blk.tensor
        def _(t, c=c, p=p):
            d_rl.wait(t, c)
            if c >= 1:
                wt(t, sms, ("b_s_evq", c - 1))
                wt(t, smv, ("b_v_opart", c - 1))
            if c == 0:
                wt(t, sms, ("a_s_evkv", NC - 1))
            for i in range(2):
                for j in range(NDCH):
                    t.matmul(psA[i][:, :], xnB[p][:, j * 128:(j + 1) * 128],
                             war[:, WQ + j * D + i * 512: WQ + j * D + i * 512 + 512],
                             start=(j == 0), stop=(j == NDCH - 1))
            smt.inc(t.drain())
            cnt[("b_t_q", c)] = smt.n

        @blk.scalar
        def _(sc, c=c, p=p):
            wt(sc, smt, ("b_t_q", c))
            sc.copy(qtmp[p][:, 0:512], psA[0][:, :])
            sms.inc(sc.copy(qtmp[p][:, 512:1024], psA[1][:, :]))
            cnt[("b_s_evq", c)] = sms.n

        @blk.vector
        def _(v, c=c, p=p):
            d_rl.wait(v, c)
            v.tensor_copy(crepb[p][:, :], crepf[p][:, :])
            wt(v, sms, ("b_s_evq", c))
            if c >= 2:
                d_qt.wait(v, c - 2)
            q1 = r3(qtmp[p])[:, :, 0:32]
            q2 = r3(qtmp[p])[:, :, 32:64]
            o1 = r3(qrop[p])[:, :, 0:32]
            o2 = r3(qrop[p])[:, :, 32:64]
            cc_ = c3(crepb[p][:, 0:512])
            ss_ = c3(crepb[p][:, 512:1024])
            sA, sB2 = c3(ropes[:, 0:512]), c3(ropes[:, 512:1024])
            sC, sD = c3(ropes[:, 1024:1536]), c3(ropes[:, 1536:2048])
            v.drain()
            v.tensor_tensor(sA, q1, cc_, ALU.mult)
            v.tensor_tensor(sB2, q2, ss_, ALU.mult)
            v.tensor_tensor(sC, q1, ss_, ALU.mult)
            v.tensor_tensor(sD, q2, cc_, ALU.mult)
            v.drain()
            v.tensor_tensor(o1, sA, sB2, ALU.subtract)
            v.tensor_tensor(o2, sC, sD, ALU.add)
            smv.inc(v.drain())
            cnt[("b_v_rope", c)] = smv.n

        @blk.scalar
        def _(sc, c=c, p=p):
            wt(sc, smv, ("b_v_rope", c))
            sc.activation(phi1[p][:, :], qrop[p][:, :], AF.Relu, scale=-1.0)
            sc.drain()
            sms.inc(sc.activation(phi2[p][:, :], phi1[p][:, :], AF.Exp, scale=-1.0))
            cnt[("b_s_phi", c)] = sms.n

        @blk.vector
        def _(v, c=c, p=p):
            wt(v, sms, ("b_s_phi", c))
            v.scalar_tensor_tensor(qtil[p][:, :], qrop[p][:, :], 0.0,
                                   phi2[p][:, :], ALU.max, ALU.add)
            smv.inc(v.drain())
            cnt[("b_v_qtil", c)] = smv.n
            if c == 0:
                wt(v, smv, "v_kvin")
            if c >= 2:
                wt(v, smt, ("b_t_y", c - 2))
                wt(v, smt, ("b_t_kc", c - 2))
            d_rl.wait(v, c)
            if c >= 1:
                v.tensor_add(ptot[p][:, :], ptot[p][:, :], kvin_b[:, :])
                v.tensor_add(prow[p][:, :], prow[p][:, :], ksumin_b[:, :])
            else:
                v.tensor_copy(ptot[p][:, :], kvin_b[:, :])
                v.tensor_copy(prow[p][:, :], ksumin_b[:, :])
            smv.inc(v.drain())
            cnt[("b_v_ptot", c)] = smv.n

        @blk.tensor
        def _(t, c=c, p=p):
            wt(t, smv, ("b_v_ptot", c))
            if c >= 1:
                wt(t, sms, ("b_s_evy", c - 1))
            for i in range(2):
                t.matmul(psB[i][:, :], ltri[:, :], ktmB[p][:, i * 512:(i + 1) * 512],
                         start=True, stop=False)
                t.matmul(psB[i][:, :], ones_r[:, :], prow[p][0:1, i * 512:(i + 1) * 512],
                         start=False, stop=True)
            smt.inc(t.drain())
            cnt[("b_t_kc", c)] = smt.n

        @blk.scalar
        def _(sc, c=c, p=p):
            wt(sc, smt, ("b_t_kc", c))
            sc.copy(kcb[p][:, 0:512], psB[0][:, :])
            sms.inc(sc.copy(kcb[p][:, 512:1024], psB[1][:, :]))
            cnt[("b_s_kcb", c)] = sms.n

        @blk.vector
        def _(v, c=c, p=p):
            wt(v, sms, ("b_s_kcb", c))
            v.tensor_mul(kcb[p][:, :], qtil[p][:, :], kcb[p][:, :])
            v.drain()
            v.tensor_reduce(den_s[p][:, :].rearrange("p (h o) -> p h o", o=1),
                            kcb[p][:, :].rearrange("p (h d) -> p h d", d=64),
                            AX, ALU.add)
            v.drain()
            v.tensor_scalar_max(den_s[p][:, :], den_s[p][:, :], 1e-6)
            v.drain()
            v.reciprocal(den_s[p][:, :], den_s[p][:, :])
            v.drain()
            v.tensor_copy(rdrep[:, :].rearrange("p (h d) -> p h d", d=64),
                          den_s[p][:, 0:H].unsqueeze(2).broadcast_to([128, H, 64]))
            v.drain()
            v.tensor_mul(qhat[p][:, :], qtil[p][:, :], rdrep[:, :])
            smv.inc(v.drain())
            cnt[("b_v_qhat", c)] = smv.n

        @blk.sync
        def _(s, c=c, p=p):
            wt(s, smv, ("b_v_qhat", c))
            if c >= 2:
                wt(s, smt, ("b_t_y", c - 2))
                wt(s, smt, ("b_t_st", c - 2, 1))
            for j in range(NDCH):
                d_qt.sem(c).inc(s.dma_start(out=qfmB[p][:, j * 128:(j + 1) * 128],
                                            in_=qhat[p][:, j * 128:(j + 1) * 128],
                                            transpose=True), 16)
            d_qt.mark(c)

        for r in range(2):
            @blk.tensor
            def _(t, c=c, p=p, r=r):
                d_qt.wait(t, c)
                if r == 0 and c >= 1:
                    wt(t, smv, ("b_v_mask", c - 1, 1))
                if r == 1:
                    wt(t, smv, ("b_v_mask", c, 0))
                for hh in (-1, 0, 2, 4, 6, -1, 1, 3, 5, 7):
                    if hh < 0:
                        t.matmul(kvps[0:8, 0:8], ltri[:, 0:8], zeros_b[:, 0:8],
                                 start=True, stop=True, skip_group_check=True)
                        continue
                    h = r * 8 + hh
                    hp, pr = h % 2, h // 2
                    t.matmul(psA[hh // 4][:, (hh % 4) * 128:(hh % 4) * 128 + 128],
                             kfmB[p][64 * hp:64 * hp + 64, pr * 128:(pr + 1) * 128],
                             qfmB[p][64 * hp:64 * hp + 64, pr * 128:(pr + 1) * 128],
                             start=(hh in (0, 4)), stop=(hh in (3, 7)),
                             tile_position=(64 * hp, 0), skip_group_check=True)
                smt.inc(t.drain())
                cnt[("b_t_st", c, r)] = smt.n

            @blk.vector
            def _(v, c=c, p=p, r=r):
                wt(v, smt, ("b_t_st", c, r))
                if r == 0 and c >= 1:
                    wt(v, smt, ("b_t_y", c - 1))
                for b in range(2):
                    v.tensor_mul(stmask[:, r * 1024 + b * 512: r * 1024 + b * 512 + 512],
                                 psA[b][:, :], mtri[:, :])
                smv.inc(v.drain())
                cnt[("b_v_mask", c, r)] = smv.n

            @blk.tensor
            def _(t, c=c, p=p, r=r):
                wt(t, smv, ("b_v_mask", c, r))
                for hh in range(8):
                    h = r * 8 + hh
                    hp, pr = h % 2, h // 2
                    yb = psB[(h * 64) // 512]
                    ycol = (h * 64) % 512
                    t.matmul(yb[:, ycol:ycol + 64],
                             stmask[:, h * 128:(h + 1) * 128],
                             vB[p][:, h * 64:(h + 1) * 64],
                             start=(h % 8 == 0), stop=False, skip_group_check=True)
                    t.matmul(yb[:, ycol:ycol + 64],
                             qfmB[p][64 * hp:64 * hp + 64, pr * 128:(pr + 1) * 128],
                             ptot[p][64 * hp:64 * hp + 64, 64 * pr:64 * pr + 64],
                             start=False, stop=(h % 8 == 7),
                             tile_position=(64 * hp, 0), skip_group_check=True)
                smt.inc(t.drain())
                cnt[("b_t_y", c) if r == 1 else ("b_t_y0", c)] = smt.n

        @blk.scalar
        def _(sc, c=c, p=p):
            wt(sc, smt, ("b_t_y", c))
            sc.copy(ytm[p][:, 0:512], psB[0][:, :])
            sms.inc(sc.copy(ytm[p][:, 512:1024], psB[1][:, :]))
            cnt[("b_s_evy", c)] = sms.n

        @blk.sync
        def _(s, c=c, p=p):
            wt(s, sms, ("b_s_evy", c))
            if c >= 2:
                wt(s, smt, ("b_t_wo", c - 2))
            for j in range(NDCH):
                d_yt.sem(c).inc(s.dma_start(out=yfm[p][:, j * 128:(j + 1) * 128],
                                            in_=ytm[p][:, j * 128:(j + 1) * 128],
                                            transpose=True), 16)
            d_yt.mark(c)
            wt(s, smv, ("b_v_opart", c - 1))
            d_xr.sem(c).inc(s.dma_start(out=xrel[:, :],
                                        in_=x_h.ap()[c * 128:(c + 1) * 128, :]), 16)
            d_xr.mark(c)

        @blk.tensor
        def _(t, c=c, p=p):
            d_yt.wait(t, c)
            if c >= 1:
                wt(t, smv, ("b_v_opart", c - 1))
            for i in range(2):
                for j in range(NDCH):
                    t.matmul(psA[i][:, :], yfm[p][:, j * 128:(j + 1) * 128],
                             war[:, WO + j * D + i * 512: WO + j * D + i * 512 + 512],
                             start=(j == 0), stop=(j == NDCH - 1))
            smt.inc(t.drain())
            cnt[("b_t_wo", c)] = smt.n

        @blk.vector
        def _(v, c=c, p=p):
            wt(v, smt, ("b_t_wo", c))
            d_xr.wait(v, c)
            if c >= 1:
                w_pt.wait(v, c - 1)
            for i in range(2):
                v.tensor_add(opart[:, i * 512:(i + 1) * 512], psA[i][:, :],
                             xrel[:, i * 512:(i + 1) * 512])
            smv.inc(v.drain())
            cnt[("b_v_opart", c)] = smv.n

        @blk.scalar
        def _(sc, c=c, p=p):
            wt(sc, smv, ("b_v_opart", c))
            w_pt.sem(c).inc(sc.dma_start(out=part_sp.ap()[c * 128:(c + 1) * 128, :],
                                         in_=opart[:, :]), 16)
            w_pt.mark(c)

        if c in w1max_sched:
            emit_w1_max(w1max_sched[c])
        if w1scale_at == c:
            emit_w1_scale()
        if c in w1q_sched:
            emit_w1_quant(w1q_sched[c], None)

    if NC < 24 or NOB:
        for d in range(NDCH):
            emit_w1_max(d)
        emit_w1_scale()
        for d in range(4):
            emit_w1_quant(d, cnt.get(("b_t_wo", NC - 1)))
    for d in range(4, NDCH):
        emit_w1_quant(d, cnt.get(("b_t_wo", NC - 1)))

    if PHASES < 3.5:
        return _finish()

    w2max_sched = {}
    w2q_sched = {}
    w2scale_at = None
    if NTT >= 8:
        for tt in range(4):
            w2max_sched[tt] = [tt * 8 + jj for jj in range(8)]
        w2scale_at = 4
        for tt in (5, 6):
            w2q_sched[tt] = [(tt - 5) * 16 + jj for jj in range(16)]

    for tt in range(NTT):
        p = tt % 2

        @blk.sync
        def _(s, tt=tt, p=p):
            if tt >= 2:
                wt(s, smv, ("m1_xq", tt - 2))
            if tt == 0:
                wt(s, smt, ("b_t_y", NC - 1))
                wt(s, smt, ("b_t_wo", NC - 1))
                wt(s, smv, ("b_v_opart", NC - 1))
                wt(s, sms, ("b_s_evy", NC - 1))
            if not M1NOXM:
                d_xm.sem(tt).inc(s.dma_start(out=xnm1[p][:, :],
                                             in_=xn_sp.ap()[:, :, tt * 512:(tt + 1) * 512]
                                             .transpose([1, 0, 2])), 16)
            d_xm.mark(tt)

        @blk.vector
        def _(v, tt=tt, p=p):
            d_xm.wait(v, tt)
            if tt == 0:
                wt(v, sms, ("w1q", NDCH - 1))
                w_pt.wait(v, NC - 1)
                w_pt.wait(v, NC - 2)
            if tt >= 2:
                wt(v, smt, ("m1_t", tt - 2, NJCH - 1))
            if not M1NOXQ:
                for hf in range(2):
                    v.tensor_scalar(qscr[:, :], xnm1[p][:, hf * 2048:(hf + 1) * 2048],
                                    scb[:, SC_RSX:SC_RSX + 1], MAGIC, ALU.mult, ALU.add)
                    v.drain()
                    v.tensor_scalar(xqb[p][:, hf * 2048:(hf + 1) * 2048], qscr[:, :],
                                    MAGIC, None, ALU.subtract)
                    v.drain()
            smv.inc(v.drain())
            cnt[("m1_xq", tt)] = smv.n

        for j in range(NJCH):
            bank = j % (2 if M1B2 else 4)

            @blk.tensor
            def _(t, tt=tt, p=p, j=j, bank=bank):
                wt(t, smv, ("m1_xq", tt))
                idx = tt * NJCH + j
                nb = 2 if M1B2 else 4
                if idx == 0:
                    wt(t, sms, ("w1q", NDCH - 1))
                if idx >= nb:
                    wt(t, sms, ("m1_h", idx - nb))
                tgt = psA[bank] if bank < 2 else psB[bank - 2]
                for d in range(1 if M1LITE else NDCH):
                    t.matmul(tgt[:, :], war[:, d * FF + j * 128: d * FF + j * 128 + 128],
                             xqb[p][:, d * 512:(d + 1) * 512],
                             start=(d == 0), stop=(d == (0 if M1LITE else NDCH - 1)))
                smt.inc(t.drain())
                cnt[("m1_t", tt, j)] = smt.n

            @blk.scalar
            def _(sc, tt=tt, p=p, j=j, bank=bank):
                wt(sc, smt, ("m1_t", tt, j))
                idx = tt * NJCH + j
                nb = 2 if M1B2 else 4
                if idx >= nb:
                    w_h.wait(sc, idx // 4 - 1)
                    wt(sc, smv, ("m1_hmx", idx - nb))
                tgt = psA[bank] if bank < 2 else psB[bank - 2]
                fn = AF.Identity if NOGELU else AF.Gelu
                if M1EVC:
                    sms.inc(sc.copy(hb[idx % 4][:, :], tgt[:, :]))
                else:
                    sms.inc(sc.activation(hb[idx % 4][:, :], tgt[:, :], fn,
                                          scale=scb[:, SC_SWS1:SC_SWS1 + 1],
                                          bias=b1s[:, j:j + 1]))
                cnt[("m1_h", idx)] = sms.n
                if j % 4 == 3:
                    gidx = idx // 4
                    sc.drain()
                    if not M1NOSP:
                        for jj in range(j - 3, j + 1):
                            w_h.sem(gidx).inc(sc.dma_start(
                                out=h_sp.ap()[4 * tt:4 * tt + 4, jj * 128:(jj + 1) * 128, :]
                                .transpose([1, 0, 2]),
                                in_=hb[jj % 4][:, :].rearrange("p (b t) -> p b t", t=128)), 16)
                    w_h.mark(gidx)

            @blk.vector
            def _(v, tt=tt, j=j):
                idx = tt * NJCH + j
                wt(v, sms, ("m1_h", idx))
                if not M1NOHM:
                    v.reduce_max(thm[:, idx % 4:idx % 4 + 1], hb[idx % 4][:, :], AX,
                                 apply_absolute_value=True)
                    if idx % 4 == 3:
                        v.drain()
                        v.reduce_max(thm4[:, :], thm[:, :], AX)
                        v.drain()
                        v.tensor_max(hmx[:, :], hmx[:, :], thm4[:, :])
                smv.inc(v.drain())
                cnt[("m1_hmx", idx)] = smv.n

        if tt in w2max_sched:
            for jw in w2max_sched[tt]:
                emit_w2_max(jw)
        if w2scale_at == tt:
            emit_w2_scale()
        if tt in w2q_sched:
            for jw in w2q_sched[tt]:
                emit_w2_quant(jw)

    if PHASES < 3.8:
        return _finish()

    if w2scale_at is None:
        for jw in range(NJCH):
            emit_w2_max(jw)
        emit_w2_scale()
        for jw in range(NJCH):
            emit_w2_quant(jw)

    @blk.tensor
    def _(t):
        wt(t, smv, ("m1_hmx", NTT * NJCH - 1))
        smt.inc(t.transpose(smps, hmx[:, :], identf[:, :]))
        cnt["t_htp"] = smt.n

    @blk.vector
    def _(v):
        wt(v, smt, "t_htp")
        v.reduce_max(g1[0:1, 10:11], smps, AX)
        smv.inc(v.drain())
        cnt["v_hg"] = smv.n

    @blk.gpsimd
    def _(g):
        w_h.wait_all(g)
        w_pt.wait_all(g)
        w_w2.wait_all(g)
        wt(g, smv, "v_hg")
        s_gb.inc(g.dma_start(out=ar2i.ap(), in_=g1[0:1, 10:11]), 16)
        g.wait_ge(s_gb.h, s_gb.n)
        s_cc.inc(g.collective_compute("AllReduce", ALU.max,
                                      replica_groups=[list(range(ncores))],
                                      ins=[ar2i.ap().opt()], outs=[ar2o.ap().opt()]))
        g.wait_ge(s_cc.h, s_cc.n)
        s_gb.inc(g.dma_start(out=g1[0:1, 11:12], in_=ar2o.ap()), 16)
        cnt["g_ar2"] = s_gb.n
        g.wait_ge(s_gb.h, s_gb.n)

    @blk.vector
    def _(v):
        wt(v, s_gb, "g_ar2")
        v.tensor_scalar_max(g1[0:1, 12:13], g1[0:1, 11:12], 1e-6)
        v.drain()
        v.tensor_scalar_mul(g1[0:1, 13:14], g1[0:1, 12:13], 1.0 / 127.0)
        v.drain()
        v.reciprocal(scrow2[0:1, SC2_RSH:SC2_RSH + 1], g1[0:1, 13:14])
        v.drain()
        v.tensor_mul(scrow2[0:1, SC2_S2B:SC2_S2B + 1], g1[0:1, 13:14], g1[0:1, 9:10])
        smv.inc(v.drain())
        cnt["v_sc2"] = smv.n

    @blk.tensor
    def _(t):
        wt(t, smv, "v_sc2")
        smt.inc(t.matmul(bcps, ones_rf[:, :], scrow2[:, :], start=True, stop=True))
        cnt["t_bc2"] = smt.n

    @blk.vector
    def _(v):
        wt(v, smt, "t_bc2")
        v.tensor_copy(scb2[:, :], bcps)
        smv.inc(v.drain())
        cnt["v_bc2"] = smv.n

    @blk.sync
    def _(s):
        wt(s, smt, ("m1_t", NTT - 1, NJCH - 1))
        wt(s, sms, ("w2q", NJCH - 1))
        s.wait_ge(s_gb.h, 128)
        w_w2.wait_all(s)
        for part in range(4):
            d_q2.sem(0).inc(s.dma_start(
                out=war[:, part * 8192:(part + 1) * 8192],
                in_=wq2_sp.ap()[part * 1024:(part + 1) * 1024, :]
                .rearrange("(j p) i -> p j i", p=128)), 16)
        d_q2.mark(0)

    if PHASES < 5:
        return _finish()

    for tc in range(NC):
        p = tc % 2

        @blk.sync
        def _(s, tc=tc, p=p):
            if tc >= 2:
                wt(s, smv, ("m2_hq", tc - 2))
            if tc >= 1:
                wt(s, smv, ("m2_fin", tc - 1))
            if tc == 0:
                w_h.wait_all(s)
                w_pt.wait_all(s)
                wt(s, sms, ("w2q", NJCH - 1))
                wt(s, smv, ("m1_hmx", NTT * NJCH - 1))
                wt(s, smv, ("m1_xq", NTT - 1))
            d_hb.sem(tc).inc(s.dma_start(out=hblf[p][:, :],
                                         in_=h_sp.ap()[tc, :, :]
                                         .rearrange("(j p) t -> p j t", p=128)), 16)
            d_hb.sem(tc).inc(s.dma_start(out=oprel[:, :],
                                         in_=part_sp.ap()[tc * 128:(tc + 1) * 128, :]), 16)
            d_hb.mark(tc)

        @blk.vector
        def _(v, tc=tc, p=p):
            d_hb.wait(v, tc)
            if tc == 0:
                wt(v, smv, "v_bc2")
            if tc >= 2:
                wt(v, smt, ("m2_t", tc - 2))
            v.tensor_scalar(hblf[p][:, :], hblf[p][:, :], scb2[:, SC2_RSH:SC2_RSH + 1],
                            MAGIC, ALU.mult, ALU.add)
            v.drain()
            v.tensor_scalar(hqb[p][:, :], hblf[p][:, :], MAGIC, None, ALU.subtract)
            smv.inc(v.drain())
            cnt[("m2_hq", tc)] = smv.n

        @blk.tensor
        def _(t, tc=tc, p=p):
            wt(t, smv, ("m2_hq", tc))
            if tc == 0:
                d_q2.wait(t, 0)
                wt(t, sms, ("m1_h", NTT * NJCH - 1))
            if tc >= 1:
                wt(t, smv, ("m2_fin", tc - 1))
            for i in range(2):
                for j in range(1 if M2LITE else NJCH):
                    t.matmul(psA[i][:, :], hqb[p][:, j * 128:(j + 1) * 128],
                             war[:, j * D + i * 512: j * D + i * 512 + 512],
                             start=(j == 0), stop=False)
                t.matmul(psA[i][:, :], ones_r[:, :], b2b[0:1, i * 512:(i + 1) * 512],
                         start=False, stop=True)
            smt.inc(t.drain())
            cnt[("m2_t", tc)] = smt.n

        @blk.vector
        def _(v, tc=tc, p=p):
            wt(v, smt, ("m2_t", tc))
            if tc >= 1:
                w_out.wait(v, tc - 1)
            for i in range(2):
                v.scalar_tensor_tensor(fin[:, i * 512:(i + 1) * 512], psA[i][:, :],
                                       scb2[:, SC2_S2B:SC2_S2B + 1],
                                       oprel[:, i * 512:(i + 1) * 512],
                                       ALU.mult, ALU.add)
            smv.inc(v.drain())
            cnt[("m2_fin", tc)] = smv.n

        @blk.scalar
        def _(sc, tc=tc, p=p):
            wt(sc, smv, ("m2_fin", tc))
            w_out.sem(tc).inc(sc.dma_start(out=out_h.ap()[tc * 128:(tc + 1) * 128, :],
                                           in_=fin[:, :]), 16)
            w_out.mark(tc)

    @blk.gpsimd
    def _(g):
        w_out.wait_all(g)

    es.close()
    return nc


def _host_tables(NT, pos0):
    pos = np.arange(pos0, pos0 + NT, dtype=np.float64)
    inv = 10000.0 ** (-np.arange(32, dtype=np.float64) / 32.0)
    ang = pos[:, None] * inv[None, :]
    crep = np.tile(np.cos(ang).astype(np.float32), (1, H))
    srep = np.tile(np.sin(ang).astype(np.float32), (1, H))
    return crep, srep


def make_in_maps(x, Wq, Wk, Wv, Wo, w1, b1, w2, b2, ncores, NC):
    NT = NC * 128
    wqT = np.ascontiguousarray(np.asarray(Wq, np.float32).T)
    wkT = np.ascontiguousarray(np.asarray(Wk, np.float32).T)
    wvT = np.ascontiguousarray(np.asarray(Wv, np.float32).T)
    woT = np.ascontiguousarray(np.asarray(Wo, np.float32).T)
    w1T = np.ascontiguousarray(np.asarray(w1, np.float32).T)
    w2T = np.ascontiguousarray(np.asarray(w2, np.float32).T)
    b1fm = np.ascontiguousarray(np.asarray(b1, np.float32).reshape(NJCH, 128).T)
    b2row = np.ascontiguousarray(np.asarray(b2, np.float32).reshape(1, D))
    tri = np.triu(np.ones((128, 128), np.float32))
    consts = {
        "wqT": wqT, "wkT": wkT, "wvT": wvT, "woT": woT, "w1T": w1T, "w2T": w2T,
        "b1fm": b1fm, "b2row": b2row,
        "ltri": tri.astype(BF), "mtri": np.tile(tri, (1, 4)).astype(np.float32),
        "ones_r": np.ones((1, 128), BF), "ones_c": np.ones((128, 1), BF),
        "ones_rf": np.ones((1, 128), np.float32),
        "identf": np.eye(128, dtype=np.float32),
    }
    in_maps = []
    for core in range(ncores):
        b, half = core // 2, core % 2
        xs = np.ascontiguousarray(np.asarray(x, np.float32)[b, half * NT:(half + 1) * NT, :])
        crep, srep = _host_tables(NT, half * NT)
        m = dict(consts)
        m.update({
            "x": xs, "crep": crep, "srep": srep,
            "mke": np.full((128, 1), 1.0 - half, np.float32),
            "mko": np.full((128, 1), float(half), np.float32),
        })
        in_maps.append(m)
    return in_maps


def kernel(L0_attn_x, L0_attn_Wq, L0_attn_Wk, L0_attn_Wv, L0_attn_Wo,
           L0_ffn_w1, L0_ffn_b1, L0_ffn_w2, L0_ffn_b2):
    NCORES = 8
    NC = 32
    NT = NC * 128
    x = np.asarray(L0_attn_x, dtype=np.float32)
    B, T, _ = x.shape
    nc = build_core_program(NC=NC, ncores=NCORES)
    in_maps = make_in_maps(x, L0_attn_Wq, L0_attn_Wk, L0_attn_Wv, L0_attn_Wo,
                           L0_ffn_w1, L0_ffn_b1, L0_ffn_w2, L0_ffn_b2, NCORES, NC)
    import time as _time
    last = None
    for _attempt in range(3):
        try:
            res = run_bass_kernel_spmd(nc, in_maps, core_ids=list(range(NCORES)),
                                       trace=True)
            break
        except Exception as e:
            last = e
            if _attempt == 2:
                raise
            _time.sleep(5.0)
    out = np.empty((B, T, D), np.float32)
    for core in range(NCORES):
        b, half = core // 2, core % 2
        out[b, half * NT:(half + 1) * NT, :] = res.results[core]["out"]
    return out


# version 03
# speedup vs baseline: 1.0003x; 1.0003x over previous
import contextlib
import numpy as np
import ml_dtypes
import concourse.bass as bass
import concourse.mybir as mybir
from concourse.bass_utils import run_bass_kernel_spmd

F32 = mybir.dt.float32
BF16 = mybir.dt.bfloat16
AF = mybir.ActivationFunctionType
ALU = mybir.AluOpType
AX = mybir.AxisListType.X
BF = ml_dtypes.bfloat16

MAGIC = 12582912.0
D = 1024
H = 16
NDCH = 8
FF = 4096
NJCH = 32


class Sem:
    def __init__(self, es, nc, name):
        self.h = es.enter_context(nc.semaphore(name))
        self.n = 0

    def inc(self, instr, k=1):
        instr.then_inc(self.h, k)
        self.n += k
        return self.n


class DS:

    def __init__(self, es, nc, name, nbuf=2):
        self.s = [Sem(es, nc, f"{name}{i}") for i in range(nbuf)]
        self.nbuf = nbuf
        self.g = {}

    def sem(self, idx):
        return self.s[idx % self.nbuf]

    def mark(self, idx):
        self.g[idx] = self.sem(idx).n

    def wait(self, eng, idx):
        if idx >= 0 and idx in self.g:
            eng.wait_ge(self.sem(idx).h, self.g[idx])

    def wait_all(self, eng):
        for s in self.s:
            eng.wait_ge(s.h, s.n)


PHASES = 5
NOGELU = 0
M1B2 = 0
M2LITE = 0
M1LITE = 0
NOB = 0
M1NOXM = 0
M1NOXQ = 0
M1NOSP = 0
M1EVC = 0
M1NOHM = 0


def build_core_program(NC=32, ncores=8):
    NT = NC * 128
    NTT = NT // 512
    assert NTT * 512 == NT and NTT >= 1
    nc = bass.Bass("TRN2", target_bir_lowering=False)

    x_h = nc.dram_tensor("x", [NT, D], F32, kind="ExternalInput")
    wq_h = nc.dram_tensor("wqT", [D, D], F32, kind="ExternalInput")
    wk_h = nc.dram_tensor("wkT", [D, D], F32, kind="ExternalInput")
    wv_h = nc.dram_tensor("wvT", [D, D], F32, kind="ExternalInput")
    wo_h = nc.dram_tensor("woT", [D, D], F32, kind="ExternalInput")
    w1_h = nc.dram_tensor("w1T", [D, FF], F32, kind="ExternalInput")
    w2_h = nc.dram_tensor("w2T", [FF, D], F32, kind="ExternalInput")
    b1_h = nc.dram_tensor("b1fm", [128, NJCH], F32, kind="ExternalInput")
    b2_h = nc.dram_tensor("b2row", [1, D], F32, kind="ExternalInput")
    crep_h = nc.dram_tensor("crep", [NT, 512], F32, kind="ExternalInput")
    srep_h = nc.dram_tensor("srep", [NT, 512], F32, kind="ExternalInput")
    ltri_h = nc.dram_tensor("ltri", [128, 128], BF16, kind="ExternalInput")
    mtri_h = nc.dram_tensor("mtri", [128, 512], F32, kind="ExternalInput")
    ones_r_h = nc.dram_tensor("ones_r", [1, 128], BF16, kind="ExternalInput")
    ones_c_h = nc.dram_tensor("ones_c", [128, 1], BF16, kind="ExternalInput")
    ones_rf_h = nc.dram_tensor("ones_rf", [1, 128], F32, kind="ExternalInput")
    ident_h = nc.dram_tensor("identf", [128, 128], F32, kind="ExternalInput")
    mke_h = nc.dram_tensor("mke", [128, 1], F32, kind="ExternalInput")
    mko_h = nc.dram_tensor("mko", [128, 1], F32, kind="ExternalInput")
    out_h = nc.dram_tensor("out", [NT, D], F32, kind="ExternalOutput")

    xn_sp = nc.dram_tensor("xn_sp", [NDCH, 128, NT], BF16)
    kfm_sp = nc.dram_tensor("kfm_sp", [NT, D], BF16)
    ktm_sp = nc.dram_tensor("ktm_sp", [NT, D], BF16)
    v_sp = nc.dram_tensor("v_sp", [NT, D], BF16)
    pkv_sp = nc.dram_tensor("pkv_sp", [NT, 512], BF16)
    pks_sp = nc.dram_tensor("pks_sp", [NC, D], BF16)
    part_sp = nc.dram_tensor("part_sp", [NT, D], F32)
    h_sp = nc.dram_tensor("h_sp", [NC, FF, 128], F32)
    wq2_sp = nc.dram_tensor("wq2_sp", [FF, D], BF16)
    ar3i = nc.dram_tensor("ar3i", [1, 128 * 512 + D], F32)
    ar3o = nc.dram_tensor("ar3o", [1, 128 * 512 + D], F32)
    ar1i = nc.dram_tensor("ar1i", [1, 1], F32)
    ar1o = nc.dram_tensor("ar1o", [1, 1], F32)
    ar2i = nc.dram_tensor("ar2i", [1, 1], F32)
    ar2o = nc.dram_tensor("ar2o", [1, 1], F32)

    es = contextlib.ExitStack()
    E = es.enter_context

    war = E(nc.sbuf_tensor("war", [128, 4 * NDCH * D], BF16))
    WK, WV, WQ, WO = 0, 8192, 16384, 24576

    ltri = E(nc.sbuf_tensor("ltri_s", [128, 128], BF16))
    mtri = E(nc.sbuf_tensor("mtri_s", [128, 512], F32))
    ones_r = E(nc.sbuf_tensor("ones_r_s", [1, 128], BF16))
    ones_c = E(nc.sbuf_tensor("ones_c_s", [128, 1], BF16))
    ones_rf = E(nc.sbuf_tensor("ones_rf_s", [1, 128], F32))
    identf = E(nc.sbuf_tensor("identf_s", [128, 128], F32))
    mke = E(nc.sbuf_tensor("mke_s", [128, 1], F32))
    mko = E(nc.sbuf_tensor("mko_s", [128, 1], F32))
    b1fm = E(nc.sbuf_tensor("b1fm_s", [128, NJCH], F32))
    b2b = E(nc.sbuf_tensor("b2b_s", [1, D], BF16))
    zeros_b = E(nc.sbuf_tensor("zeros_b", [128, 512], BF16))

    lnsum = [E(nc.sbuf_tensor(f"lnsum{i}", [128, 1], F32)) for i in range(2)]
    lnss = [E(nc.sbuf_tensor(f"lnss{i}", [128, 1], F32)) for i in range(2)]
    lnm = [E(nc.sbuf_tensor(f"lnm{i}", [128, 1], F32)) for i in range(2)]
    lnmsq = [E(nc.sbuf_tensor(f"lnmsq{i}", [128, 1], F32)) for i in range(2)]
    lnvar = [E(nc.sbuf_tensor(f"lnvar{i}", [128, 1], F32)) for i in range(2)]
    lnrstd = [E(nc.sbuf_tensor(f"lnrstd{i}", [128, 1], F32)) for i in range(2)]
    lnnm = [E(nc.sbuf_tensor(f"lnnm{i}", [128, 1], F32)) for i in range(2)]
    tmx = [E(nc.sbuf_tensor(f"tmx{i}", [128, 1], F32)) for i in range(2)]
    thm = E(nc.sbuf_tensor("thm", [128, 4], F32))
    thm4 = E(nc.sbuf_tensor("thm4", [128, 1], F32))
    mxacc = E(nc.sbuf_tensor("mxacc", [128, 1], F32))
    hmx = E(nc.sbuf_tensor("hmx", [128, 1], F32))
    w1mx = E(nc.sbuf_tensor("w1mx", [128, 1], F32))
    w2mx = E(nc.sbuf_tensor("w2mx", [128, 1], F32))
    den_s = [E(nc.sbuf_tensor(f"den{i}", [128, H], F32)) for i in range(2)]
    arst = E(nc.sbuf_tensor("arst", [128, 512], F32))
    arks = E(nc.sbuf_tensor("arks", [1, D], F32))
    kvinf = E(nc.sbuf_tensor("kvinf", [128, 512], F32))
    ksuminf = E(nc.sbuf_tensor("ksuminf", [1, D], F32))
    kvin_b = E(nc.sbuf_tensor("kvin_b", [128, 512], BF16))
    ksumin_b = E(nc.sbuf_tensor("ksumin_b", [1, D], BF16))
    g1 = E(nc.sbuf_tensor("g1", [1, 16], F32))
    scrow = E(nc.sbuf_tensor("scrow", [1, 8], F32))
    scrow2 = E(nc.sbuf_tensor("scrow2", [1, 8], F32))
    scb = E(nc.sbuf_tensor("scb", [128, 8], F32))
    scb2 = E(nc.sbuf_tensor("scb2", [128, 8], F32))
    SC_RSX, SC_SWS1, SC_RSW1 = 0, 1, 2
    SC2_RSH, SC2_S2B, SC2_RSW2 = 0, 1, 2
    b1s = E(nc.sbuf_tensor("b1s", [128, NJCH], F32))
    constf = E(nc.sbuf_tensor("constf", [128, 4], F32))

    A16_COLS = 32768
    A32_COLS = 12288
    a16 = E(nc.sbuf_tensor("a16", [128, A16_COLS], BF16))
    a32 = E(nc.sbuf_tensor("a32", [128, A32_COLS], F32))

    def carve(arena, cur, cols, n=None, rows=128):
        if n is None:
            ap = arena[0:rows, cur[0]:cur[0] + cols]
            cur[0] += cols
            return ap
        return [carve(arena, cur, cols, None, rows) for _ in range(n)]

    cS16, cS32 = [0], [0]
    crepb = carve(a16, cS16, 1024, 2)
    ropes = carve(a16, cS16, 2048)
    phi1 = carve(a16, cS16, 1024, 2)
    phi2 = carve(a16, cS16, 1024, 2)
    rowslot = carve(a16, cS16, 1024, 2, rows=1)
    S16 = cS16[0]
    crepf = carve(a32, cS32, 1024, 2)
    wqscr = carve(a32, cS32, 4096)
    w1f = carve(a32, cS32, 4096)
    S32 = cS32[0]

    c16, c32 = [S16], [S32 - 8192]
    xbuf = carve(a32, c32, 1024, 2)
    xn32 = carve(a32, c32, 1024, 2)
    sqscr = carve(a32, c32, 1024, 2)
    xntm = carve(a16, c16, 1024, 2)
    xnfmb = carve(a16, c16, 1024, 2)
    ktmp = carve(a16, c16, 1024, 2)
    krop = carve(a16, c16, 1024, 2)
    ktil = krop
    vbuf = carve(a16, c16, 1024, 2)
    kfmb = carve(a16, c16, 1024, 2)
    pkvst = carve(a16, c16, 512, 2)
    pksst = rowslot
    assert c16[0] <= A16_COLS and c32[0] <= A32_COLS, (c16, c32)

    c16, c32 = [S16], [S32]
    xrel = carve(a32, c32, 1024)
    opart = carve(a32, c32, 1024)
    xnB = carve(a16, c16, 1024, 2)
    ktmB = carve(a16, c16, 1024, 2)
    kfmB = carve(a16, c16, 1024, 2)
    vB = carve(a16, c16, 1024, 2)
    qtmp = carve(a16, c16, 1024, 2)
    qrop = carve(a16, c16, 1024, 2)
    qtil = qrop
    qhat = qrop
    qfmB = carve(a16, c16, 1024, 2)
    ptot = carve(a16, c16, 512, 2)
    prow = rowslot
    kcb = carve(a16, c16, 1024, 2)
    rdrep = carve(a16, c16, 1024)
    stmask = carve(a16, c16, 2048)
    ytm = carve(a16, c16, 1024, 2)
    yfm = ktmB
    assert c16[0] <= A16_COLS and c32[0] <= A32_COLS, (c16, c32)

    c16, c32 = [S16], [S32 - 4096]
    w2f = carve(a32, c32, 1024, 2)
    hb_off = c32[0]
    hb = carve(a32, c32, 512, 4)
    hball = a32[:, hb_off:hb_off + 2048]
    qscr = carve(a32, c32, 2048)
    xnm1 = carve(a16, c16, 4096, 2)
    xqb = carve(a16, c16, 4096, 2)
    wq2c = carve(a16, c16, 1024, 2)
    assert c16[0] <= A16_COLS and c32[0] <= A32_COLS, (c16, c32)

    c16, c32 = [S16], [S32 - 8192]
    hblf = carve(a32, c32, 4096, 2)
    oprel = carve(a32, c32, 1024)
    fin = carve(a32, c32, 1024)
    hqb = carve(a16, c16, 4096, 2)
    assert c16[0] <= A16_COLS and c32[0] <= A32_COLS, (c16, c32)

    psA = [E(nc.psum_tensor(f"psA{i}", [128, 512], F32)) for i in range(2)]
    psB = [E(nc.psum_tensor(f"psB{i}", [128, 512], F32)) for i in range(2)]
    kvps = E(nc.psum_tensor("kvps", [128, 512], F32))
    ksps = E(nc.psum_tensor("ksps", [1, D], F32))
    misc = E(nc.psum_tensor("miscps", [128, 136], F32))
    smps = misc[0:1, 0:128]
    bcps = misc[:, 128:136]

    d_ld = DS(es, nc, "d_ld")
    d_xt = DS(es, nc, "d_xt")
    d_kt = DS(es, nc, "d_kt")
    d_qt = DS(es, nc, "d_qt")
    d_yt = DS(es, nc, "d_yt")
    d_rl = DS(es, nc, "d_rl")
    d_xr = DS(es, nc, "d_xr")
    d_w1 = DS(es, nc, "d_w1")
    d_w2 = DS(es, nc, "d_w2")
    d_xm = DS(es, nc, "d_xm")
    d_hb = DS(es, nc, "d_hb")
    d_q2 = DS(es, nc, "d_q2")
    w_a = DS(es, nc, "w_a")
    w_pt = DS(es, nc, "w_pt")
    w_h = DS(es, nc, "w_h")
    w_w2 = DS(es, nc, "w_w2")
    w_out = DS(es, nc, "w_out")
    s_gw = Sem(es, nc, "s_gw")
    s_gb = Sem(es, nc, "s_gb")
    s_cc = Sem(es, nc, "s_cc")
    s_cn = Sem(es, nc, "s_cn")
    smt = Sem(es, nc, "smt")
    sms = Sem(es, nc, "sms")
    smv = Sem(es, nc, "smv")

    blk = E(nc.Block())
    cnt = {}

    def wt(eng, sem, key):
        if key in cnt:
            eng.wait_ge(sem.h, cnt[key])

    def _finish():
        @blk.gpsimd
        def _(g):
            g.wait_ge(smt.h, smt.n)
            g.wait_ge(sms.h, sms.n)
            g.wait_ge(smv.h, smv.n)
            for dsx in (d_ld, d_xt, d_kt, d_qt, d_yt, d_rl, d_xr, d_w1,
                        d_w2, d_xm, d_hb, d_q2, w_a, w_pt, w_h, w_w2, w_out):
                dsx.wait_all(g)
        es.close()
        return nc

    def r3(t):
        return t.rearrange("p (h d) -> p h d", d=64)

    def c3(t):
        return t.rearrange("p (h j) -> p h j", j=32)

    @blk.sync
    def _(s):
        for dst, src in ((ltri, ltri_h), (mtri, mtri_h), (ones_r, ones_r_h),
                         (ones_c, ones_c_h), (ones_rf, ones_rf_h), (identf, ident_h),
                         (mke, mke_h), (mko, mko_h), (b1fm, b1_h)):
            s_cn.inc(s.dma_start(out=dst[:, :], in_=src.ap()), 16)

    @blk.gpsimd
    def _(g):
        for off, src in ((WK, wk_h), (WV, wv_h), (WQ, wq_h), (WO, wo_h)):
            s_gw.inc(g.dma_start(out=war[:, off:off + 8192],
                                 in_=src.ap().rearrange("(j p) d -> p j d", p=128)), 16)
        s_gw.inc(g.dma_start(out=b2b[:, :], in_=b2_h.ap()), 16)

    for i, val in enumerate((1e-5, -1.0, MAGIC, -MAGIC)):
        nc.const_aps.register(F32, val, constf[:, i:i + 1])

    @blk.vector
    def _(v):
        for i, val in enumerate((1e-5, -1.0, MAGIC, -MAGIC)):
            v.memset(constf[:, i:i + 1], val)
        v.memset(zeros_b[:, :], 0.0)
        v.memset(scrow[:, :], 0.0)
        v.memset(scrow2[:, :], 0.0)
        v.memset(mxacc[:, :], 0.0)
        v.memset(hmx[:, :], 0.0)
        v.memset(w1mx[:, :], 0.0)
        v.memset(w2mx[:, :], 0.0)
        smv.inc(v.drain())
        cnt["v_init"] = smv.n

    for c in range(NC):
        p = c % 2

        @blk.sync
        def _(s, c=c, p=p):
            if c >= 2:
                wt(s, sms, ("a_s_xn", c - 2))
                wt(s, smv, ("a_v_cast", c - 2))
            d_ld.sem(c).inc(s.dma_start(out=xbuf[p][:, :],
                                        in_=x_h.ap()[c * 128:(c + 1) * 128, :]), 16)
            d_ld.sem(c).inc(s.dma_start(out=crepf[p][:, 0:512],
                                        in_=crep_h.ap()[c * 128:(c + 1) * 128, :]), 16)
            d_ld.sem(c).inc(s.dma_start(out=crepf[p][:, 512:1024],
                                        in_=srep_h.ap()[c * 128:(c + 1) * 128, :]), 16)
            d_ld.mark(c)

        @blk.vector
        def _(v, c=c, p=p):
            d_ld.wait(v, c)
            v.reduce_sum(lnsum[p][:, :], xbuf[p][:, :], AX)
            smv.inc(v.drain())
            cnt[("a_v_sum", c)] = smv.n

        @blk.scalar
        def _(sc, c=c, p=p):
            d_ld.wait(sc, c)
            sms.inc(sc.activation(sqscr[p][:, :], xbuf[p][:, :], AF.Square,
                                  accum_out=lnss[p][:, 0:1]))
            cnt[("a_s_ss", c)] = sms.n

        @blk.vector
        def _(v, c=c, p=p):
            wt(v, sms, ("a_s_ss", c))
            v.tensor_scalar_mul(lnm[p][:, :], lnsum[p][:, :], 1.0 / D)
            v.drain()
            v.tensor_mul(lnmsq[p][:, :], lnm[p][:, :], lnm[p][:, :])
            v.drain()
            v.scalar_tensor_tensor(lnvar[p][:, :], lnss[p][:, :], 1.0 / D,
                                   lnmsq[p][:, :], ALU.mult, ALU.subtract)
            smv.inc(v.drain())
            cnt[("a_v_var", c)] = smv.n

        @blk.scalar
        def _(sc, c=c, p=p):
            wt(sc, smv, ("a_v_var", c))
            sms.inc(sc.activation(lnvar[p][:, :], lnvar[p][:, :], AF.Sqrt, bias=1e-5))
            cnt[("a_s_rstd", c)] = sms.n

        @blk.vector
        def _(v, c=c, p=p):
            wt(v, sms, ("a_s_rstd", c))
            v.reciprocal(lnrstd[p][:, :], lnvar[p][:, :])
            v.drain()
            v.scalar_tensor_tensor(lnnm[p][:, :], lnm[p][:, :], -1.0,
                                   lnrstd[p][:, :], ALU.mult, ALU.mult)
            smv.inc(v.drain())
            cnt[("a_v_nm", c)] = smv.n

        @blk.scalar
        def _(sc, c=c, p=p):
            wt(sc, smv, ("a_v_nm", c))
            sms.inc(sc.activation(xn32[p][:, :], xbuf[p][:, :], AF.Identity,
                                  scale=lnrstd[p][:, 0:1], bias=lnnm[p][:, 0:1]))
            cnt[("a_s_xn", c)] = sms.n

        @blk.vector
        def _(v, c=c, p=p):
            wt(v, sms, ("a_s_xn", c))
            v.reduce_max(tmx[p][:, :], xn32[p][:, :], AX, apply_absolute_value=True)
            v.drain()
            v.tensor_max(mxacc[:, :], mxacc[:, :], tmx[p][:, :])
            v.drain()
            v.tensor_copy(xntm[p][:, :], xn32[p][:, :])
            v.tensor_copy(crepb[p][:, :], crepf[p][:, :])
            smv.inc(v.drain())
            cnt[("a_v_cast", c)] = smv.n

        @blk.sync
        def _(s, c=c, p=p):
            wt(s, smv, ("a_v_cast", c))
            if c >= 2:
                wt(s, smt, ("a_t_proj", c - 2))
                w_a.wait(s, c - 2)
            for j in range(NDCH):
                d_xt.sem(c).inc(s.dma_start(out=xnfmb[p][:, j * 128:(j + 1) * 128],
                                            in_=xntm[p][:, j * 128:(j + 1) * 128],
                                            transpose=True), 16)
            d_xt.mark(c)


        @blk.tensor
        def _(t, c=c, p=p):
            d_xt.wait(t, c)
            if c == 0:
                t.wait_ge(s_gw.h, s_gw.n)
                t.wait_ge(s_cn.h, s_cn.n)
            if c >= 1:
                wt(t, sms, ("a_s_evkv", c - 1))
            for (ps, woff) in ((psA, WV), (psB, WK)):
                for i in range(2):
                    for j in range(NDCH):
                        t.matmul(ps[i][:, :], xnfmb[p][:, j * 128:(j + 1) * 128],
                                 war[:, woff + j * D + i * 512: woff + j * D + i * 512 + 512],
                                 start=(j == 0), stop=(j == NDCH - 1))
            smt.inc(t.drain())
            cnt[("a_t_proj", c)] = smt.n

        @blk.scalar
        def _(sc, c=c, p=p):
            wt(sc, smt, ("a_t_proj", c))
            if c >= 2:
                w_a.wait(sc, c - 2)
            sc.copy(vbuf[p][:, 0:512], psA[0][:, :])
            sc.copy(vbuf[p][:, 512:1024], psA[1][:, :])
            sc.copy(ktmp[p][:, 0:512], psB[0][:, :])
            sms.inc(sc.copy(ktmp[p][:, 512:1024], psB[1][:, :]))
            cnt[("a_s_evkv", c)] = sms.n

        @blk.vector
        def _(v, c=c, p=p):
            wt(v, sms, ("a_s_evkv", c))
            if c >= 2:
                wt(v, smt, ("a_t_upd", c - 2))
                d_kt.wait(v, c - 2)
                w_a.wait(v, c - 2)
            k1 = r3(ktmp[p])[:, :, 0:32]
            k2 = r3(ktmp[p])[:, :, 32:64]
            o1 = r3(krop[p])[:, :, 0:32]
            o2 = r3(krop[p])[:, :, 32:64]
            cc_ = c3(crepb[p][:, 0:512])
            ss_ = c3(crepb[p][:, 512:1024])
            sA, sB2 = c3(ropes[:, 0:512]), c3(ropes[:, 512:1024])
            sC, sD = c3(ropes[:, 1024:1536]), c3(ropes[:, 1536:2048])
            v.drain()
            v.tensor_tensor(sA, k1, cc_, ALU.mult)
            v.tensor_tensor(sB2, k2, ss_, ALU.mult)
            v.tensor_tensor(sC, k1, ss_, ALU.mult)
            v.tensor_tensor(sD, k2, cc_, ALU.mult)
            v.drain()
            v.tensor_tensor(o1, sA, sB2, ALU.subtract)
            v.tensor_tensor(o2, sC, sD, ALU.add)
            smv.inc(v.drain())
            cnt[("a_v_rope", c)] = smv.n

        @blk.scalar
        def _(sc, c=c, p=p):
            wt(sc, smv, ("a_v_rope", c))
            sc.activation(phi1[p][:, :], krop[p][:, :], AF.Relu, scale=-1.0)
            sc.drain()
            sms.inc(sc.activation(phi2[p][:, :], phi1[p][:, :], AF.Exp, scale=-1.0))
            cnt[("a_s_phi", c)] = sms.n

        @blk.vector
        def _(v, c=c, p=p):
            wt(v, sms, ("a_s_phi", c))
            v.scalar_tensor_tensor(ktil[p][:, :], krop[p][:, :], 0.0,
                                   phi2[p][:, :], ALU.max, ALU.add)
            smv.inc(v.drain())
            cnt[("a_v_ktil", c)] = smv.n
            if c >= 1:
                wt(v, smt, ("a_t_upd", c - 1))

                smv.inc(v.tensor_copy(pkvst[p][:, :], kvps[:, :]))
                cnt[("a_v_pref", c)] = smv.n

        @blk.scalar
        def _(sc, c=c, p=p):
            if c >= 1:
                wt(sc, smt, ("a_t_upd", c - 1))

                sms.inc(sc.copy(pksst[p][:, :], ksps[:, :]))
                cnt[("a_s_prefks", c)] = sms.n


        @blk.sync
        def _(s, c=c, p=p):
            wt(s, smv, ("a_v_ktil", c))
            if c >= 2:
                w_a.wait(s, c - 2)
            for j in range(NDCH):
                d_kt.sem(c).inc(s.dma_start(out=kfmb[p][:, j * 128:(j + 1) * 128],
                                            in_=ktil[p][:, j * 128:(j + 1) * 128],
                                            transpose=True), 16)
            d_kt.mark(c)

        @blk.gpsimd
        def _(g, c=c, p=p):
            d_xt.wait(g, c)
            w_a.sem(c).inc(g.dma_start(
                out=xn_sp.ap()[:, :, c * 128:(c + 1) * 128].transpose([1, 0, 2]),
                in_=xnfmb[p][:, :].rearrange("p (j t) -> p j t", t=128)), 16)
            d_kt.wait(g, c)
            w_a.sem(c).inc(g.dma_start(out=kfm_sp.ap()[c * 128:(c + 1) * 128, :],
                                       in_=kfmb[p][:, :]), 16)
            w_a.sem(c).inc(g.dma_start(out=ktm_sp.ap()[c * 128:(c + 1) * 128, :],
                                       in_=ktil[p][:, :]), 16)
            wt(g, sms, ("a_s_evkv", c))
            w_a.sem(c).inc(g.dma_start(out=v_sp.ap()[c * 128:(c + 1) * 128, :],
                                       in_=vbuf[p][:, :]), 16)
            if c >= 1:
                wt(g, smv, ("a_v_pref", c))
                w_a.sem(c).inc(g.dma_start(out=pkv_sp.ap()[c * 128:(c + 1) * 128, :],
                                           in_=pkvst[p][:, :]), 16)
                wt(g, sms, ("a_s_prefks", c))
                w_a.sem(c).inc(g.dma_start(out=pks_sp.ap()[c:c + 1, :],
                                           in_=pksst[p][:, :]), 16)
            w_a.mark(c)

        @blk.tensor
        def _(t, c=c, p=p):
            wt(t, smv, ("a_v_ktil", c))
            if c == 0:
                wt(t, smv, "v_init")
                t.matmul(kvps[:, :], ltri[:, :], zeros_b[:, :], start=True, stop=False,
                         skip_group_check=True)
            else:
                wt(t, smv, ("a_v_pref", c))
                wt(t, sms, ("a_s_prefks", c))
            for h in range(H):
                hp, hh = h % 2, h // 2
                t.matmul(kvps[64 * hp:64 * hp + 64, 64 * hh:64 * hh + 64],
                         ktil[p][:, h * 64:(h + 1) * 64], vbuf[p][:, h * 64:(h + 1) * 64],
                         start=False, stop=False, tile_position=(0, 64 * hp),
                         skip_group_check=True)
            for i in range(2):
                t.matmul(ksps[:, i * 512:(i + 1) * 512], ones_c[:, :],
                         ktil[p][:, i * 512:(i + 1) * 512],
                         start=(c == 0), stop=(c == NC - 1), skip_group_check=True)
            smt.inc(t.drain())
            cnt[("a_t_upd", c)] = smt.n

    if PHASES < 2:
        return _finish()

    @blk.vector
    def _(v):
        wt(v, smt, ("a_t_upd", NC - 1))
        v.tensor_scalar_mul(arst[:, :], kvps[:, :], mke[:, 0:1])
        smv.inc(v.drain())
        cnt["ar_st"] = smv.n

    @blk.scalar
    def _(sc):
        wt(sc, smt, ("a_t_upd", NC - 1))
        sms.inc(sc.activation(arks[:, :], ksps[:, :], AF.Copy, scale=mke[0:1, 0:1]))
        cnt["ar_ks"] = sms.n

    @blk.tensor
    def _(t):
        wt(t, smv, ("a_v_cast", NC - 1))
        smt.inc(t.transpose(smps, mxacc[:, :], identf[:, :]))
        cnt["t_mxtp"] = smt.n

    @blk.vector
    def _(v):
        wt(v, smt, "t_mxtp")
        v.reduce_max(g1[0:1, 0:1], smps, AX)
        smv.inc(v.drain())
        cnt["v_mxg"] = smv.n

    @blk.gpsimd
    def _(g):
        w_a.wait_all(g)
        wt(g, smv, "ar_st")
        wt(g, sms, "ar_ks")
        kvv = ar3i.ap()[0:1, 0:65536].rearrange("o (p c) -> o p c", p=128).squeeze(0)
        s_gb.inc(g.dma_start(out=kvv, in_=arst[:, :]), 16)
        s_gb.inc(g.dma_start(out=ar3i.ap()[0:1, 65536:65536 + D], in_=arks[:, :]), 16)
        g.wait_ge(s_gb.h, s_gb.n)
        grp3 = [[2 * i, 2 * i + 1] for i in range(ncores // 2)] if ncores > 1 else [[0]]
        s_cc.inc(g.collective_compute("AllReduce", ALU.add, replica_groups=grp3,
                                      ins=[ar3i.ap().opt()], outs=[ar3o.ap().opt()]))
        g.wait_ge(s_cc.h, s_cc.n)
        kvo = ar3o.ap()[0:1, 0:65536].rearrange("o (p c) -> o p c", p=128).squeeze(0)
        s_gb.inc(g.dma_start(out=kvinf[:, :], in_=kvo), 16)
        s_gb.inc(g.dma_start(out=ksuminf[:, :], in_=ar3o.ap()[0:1, 65536:65536 + D]), 16)
        cnt["g_ar3"] = s_gb.n
        g.wait_ge(s_gb.h, s_gb.n)
        wt(g, smv, "v_mxg")
        s_gb.inc(g.dma_start(out=ar1i.ap(), in_=g1[0:1, 0:1]), 16)
        g.wait_ge(s_gb.h, s_gb.n)
        s_cc.inc(g.collective_compute("AllReduce", ALU.max,
                                      replica_groups=[list(range(ncores))],
                                      ins=[ar1i.ap().opt()], outs=[ar1o.ap().opt()]))
        g.wait_ge(s_cc.h, s_cc.n)
        s_gb.inc(g.dma_start(out=g1[0:1, 1:2], in_=ar1o.ap()), 16)
        cnt["g_ar1"] = s_gb.n
        g.wait_ge(s_gb.h, s_gb.n)

    @blk.vector
    def _(v):
        wt(v, s_gb, "g_ar3")
        v.tensor_scalar_mul(kvin_b[:, :], kvinf[:, :], mko[:, 0:1])
        v.tensor_scalar_mul(ksumin_b[:, :], ksuminf[:, :], mko[0:1, 0:1])
        smv.inc(v.drain())
        cnt["v_kvin"] = smv.n
        wt(v, s_gb, "g_ar1")
        v.tensor_scalar_max(g1[0:1, 2:3], g1[0:1, 1:2], 1e-6)
        v.drain()
        v.tensor_scalar_mul(g1[0:1, 3:4], g1[0:1, 2:3], 1.0 / 127.0)
        v.drain()
        v.reciprocal(scrow[0:1, SC_RSX:SC_RSX + 1], g1[0:1, 3:4])
        smv.inc(v.drain())
        cnt["v_sc1a"] = smv.n

    if PHASES < 3:
        return _finish()

    def emit_w1_max(d):
        @blk.sync
        def _(s, d=d):
            if d >= 1:
                wt(s, smv, ("w1mx", d - 1))
            d_w1.sem(d).inc(s.dma_start(out=w1f[:, :],
                                        in_=w1_h.ap()[d * 128:(d + 1) * 128, :]), 16)
            d_w1.mark(d)

        @blk.vector
        def _(v, d=d):
            d_w1.wait(v, d)
            v.reduce_max(tmx[d % 2][:, :], w1f[:, :], AX, apply_absolute_value=True)
            v.drain()
            v.tensor_max(w1mx[:, :], w1mx[:, :], tmx[d % 2][:, :])
            smv.inc(v.drain())
            cnt[("w1mx", d)] = smv.n

    def emit_w1_scale():
        @blk.tensor
        def _(t):
            wt(t, smv, ("w1mx", NDCH - 1))
            smt.inc(t.transpose(smps, w1mx[:, :], identf[:, :]))
            cnt["t_w1tp"] = smt.n

        @blk.vector
        def _(v):
            wt(v, smt, "t_w1tp")
            v.reduce_max(g1[0:1, 4:5], smps, AX)
            v.drain()
            v.tensor_scalar_max(g1[0:1, 5:6], g1[0:1, 4:5], 1e-6)
            v.drain()
            v.tensor_scalar_mul(g1[0:1, 6:7], g1[0:1, 5:6], 1.0 / 127.0)
            v.drain()
            v.reciprocal(scrow[0:1, SC_RSW1:SC_RSW1 + 1], g1[0:1, 6:7])
            v.drain()
            wt(v, smv, "v_sc1a")
            v.tensor_mul(scrow[0:1, SC_SWS1:SC_SWS1 + 1], g1[0:1, 3:4], g1[0:1, 6:7])
            smv.inc(v.drain())
            cnt["v_sc1"] = smv.n

        @blk.tensor
        def _(t):
            wt(t, smv, "v_sc1")
            smt.inc(t.matmul(bcps, ones_rf[:, :], scrow[:, :], start=True, stop=True))
            cnt["t_bc1"] = smt.n

        @blk.vector
        def _(v):
            wt(v, smt, "t_bc1")
            v.tensor_copy(scb[:, :], bcps)
            v.drain()
            v.tensor_scalar_mul(b1s[:, :], b1fm[:, :], scb[:, SC_SWS1:SC_SWS1 + 1])
            smv.inc(v.drain())
            cnt["v_bc1"] = smv.n

    def emit_w1_quant(d, guard):
        @blk.sync
        def _(s, d=d):
            if d >= 1:
                wt(s, sms, ("w1q", d - 1))
            else:
                wt(s, smv, ("w1mx", NDCH - 1))
            d_w1.sem(NDCH + d).inc(s.dma_start(out=w1f[:, :],
                                               in_=w1_h.ap()[d * 128:(d + 1) * 128, :]), 16)
            d_w1.mark(NDCH + d)

        @blk.scalar
        def _(sc, d=d, guard=guard):
            d_w1.wait(sc, NDCH + d)
            wt(sc, smv, "v_bc1")
            if guard is not None:
                sc.wait_ge(smt.h, guard)
            sc.activation(wqscr[:, :], w1f[:, :], AF.Copy,
                          scale=scb[:, SC_RSW1:SC_RSW1 + 1], bias=MAGIC)
            sc.drain()
            sms.inc(sc.add(war[:, d * FF:(d + 1) * FF], wqscr[:, :], -MAGIC))
            cnt[("w1q", d)] = sms.n

    def emit_w2_max(jw):
        @blk.sync
        def _(s, jw=jw):
            if jw >= 2:
                wt(s, smv, ("w2mx", jw - 2))
            else:
                wt(s, sms, ("w1q", NDCH - 1))
            d_w2.sem(jw).inc(s.dma_start(out=w2f[jw % 2][:, :],
                                         in_=w2_h.ap()[jw * 128:(jw + 1) * 128, :]), 16)
            d_w2.mark(jw)

        @blk.vector
        def _(v, jw=jw):
            d_w2.wait(v, jw)
            v.reduce_max(tmx[jw % 2][:, :], w2f[jw % 2][:, :], AX, apply_absolute_value=True)
            v.drain()
            v.tensor_max(w2mx[:, :], w2mx[:, :], tmx[jw % 2][:, :])
            smv.inc(v.drain())
            cnt[("w2mx", jw)] = smv.n

    def emit_w2_scale():
        @blk.tensor
        def _(t):
            wt(t, smv, ("w2mx", NJCH - 1))
            smt.inc(t.transpose(smps, w2mx[:, :], identf[:, :]))
            cnt["t_w2tp"] = smt.n

        @blk.vector
        def _(v):
            wt(v, smt, "t_w2tp")
            v.reduce_max(g1[0:1, 7:8], smps, AX)
            v.drain()
            v.tensor_scalar_max(g1[0:1, 8:9], g1[0:1, 7:8], 1e-6)
            v.drain()
            v.tensor_scalar_mul(g1[0:1, 9:10], g1[0:1, 8:9], 1.0 / 127.0)
            v.drain()
            v.reciprocal(scrow2[0:1, SC2_RSW2:SC2_RSW2 + 1], g1[0:1, 9:10])
            smv.inc(v.drain())
            cnt["v_sc2w"] = smv.n

        @blk.tensor
        def _(t):
            wt(t, smv, "v_sc2w")
            smt.inc(t.matmul(bcps, ones_rf[:, :], scrow2[:, :], start=True, stop=True))
            cnt["t_bc2w"] = smt.n

        @blk.vector
        def _(v):
            wt(v, smt, "t_bc2w")
            v.tensor_copy(scb2[:, :], bcps)
            smv.inc(v.drain())
            cnt["v_bc2w"] = smv.n

    def emit_w2_quant(jw):
        @blk.sync
        def _(s, jw=jw):
            if jw >= 2:
                w_w2.wait(s, jw - 2)
                wt(s, sms, ("w2q", jw - 2))
            else:
                wt(s, smv, ("w2mx", NJCH - 2 + jw))
            d_w2.sem(NJCH + jw).inc(s.dma_start(out=w2f[jw % 2][:, :],
                                                in_=w2_h.ap()[jw * 128:(jw + 1) * 128, :]), 16)
            d_w2.mark(NJCH + jw)

        @blk.scalar
        def _(sc, jw=jw):
            d_w2.wait(sc, NJCH + jw)
            wt(sc, smv, "v_bc2w")
            sc.activation(w2f[jw % 2][:, :], w2f[jw % 2][:, :], AF.Copy,
                          scale=scb2[:, SC2_RSW2:SC2_RSW2 + 1], bias=MAGIC)
            sc.drain()
            sms.inc(sc.add(wq2c[jw % 2][:, :], w2f[jw % 2][:, :], -MAGIC))
            cnt[("w2q", jw)] = sms.n
            sc.drain()
            w_w2.sem(jw).inc(sc.dma_start(out=wq2_sp.ap()[jw * 128:(jw + 1) * 128, :],
                                          in_=wq2c[jw % 2][:, :]), 16)
            w_w2.mark(jw)

    w1max_sched = {4 + d: d for d in range(NDCH)} if NC >= 24 else {}
    w1scale_at = 13 if NC >= 24 else None
    w1q_sched = {14 + d: d for d in range(4)} if NC >= 24 else {}

    for c in (range(0) if NOB else range(NC)):
        p = c % 2

        @blk.sync
        def _(s, c=c, p=p):
            if c >= 2:
                wt(s, smt, ("b_t_y", c - 2))
                wt(s, smt, ("b_t_kc", c - 2))
                wt(s, smt, ("b_t_q", c - 2))
                wt(s, smt, ("b_t_st", c - 2, 1))
                wt(s, smv, ("b_v_opart", c - 2))
                wt(s, smv, ("b_v_ptot", c - 2))
            if c == 0:
                w_a.wait_all(s)
                s.wait_ge(s_gb.h, 96)
            for dst, sp in ((kfmB[p], kfm_sp), (ktmB[p], ktm_sp), (vB[p], v_sp)):
                d_rl.sem(c).inc(s.dma_start(out=dst[:, :],
                                            in_=sp.ap()[c * 128:(c + 1) * 128, :]), 16)
            d_rl.sem(c).inc(s.dma_start(out=crepf[p][:, 0:512],
                                        in_=crep_h.ap()[c * 128:(c + 1) * 128, :]), 16)
            d_rl.sem(c).inc(s.dma_start(out=crepf[p][:, 512:1024],
                                        in_=srep_h.ap()[c * 128:(c + 1) * 128, :]), 16)
            d_rl.sem(c).inc(s.dma_start(out=xnB[p][:, :],
                                        in_=xn_sp.ap()[:, :, c * 128:(c + 1) * 128]
                                        .transpose([1, 0, 2])), 16)
            if c >= 1:
                d_rl.sem(c).inc(s.dma_start(out=ptot[p][:, :],
                                            in_=pkv_sp.ap()[c * 128:(c + 1) * 128, :]), 16)
                d_rl.sem(c).inc(s.dma_start(out=prow[p][:, :],
                                            in_=pks_sp.ap()[c:c + 1, :]), 16)
            d_rl.mark(c)

        @blk.tensor
        def _(t, c=c, p=p):
            d_rl.wait(t, c)
            if c >= 1:
                wt(t, sms, ("b_s_evq", c - 1))
                wt(t, smv, ("b_v_opart", c - 1))
            if c == 0:
                wt(t, sms, ("a_s_evkv", NC - 1))
            for i in range(2):
                for j in range(NDCH):
                    t.matmul(psA[i][:, :], xnB[p][:, j * 128:(j + 1) * 128],
                             war[:, WQ + j * D + i * 512: WQ + j * D + i * 512 + 512],
                             start=(j == 0), stop=(j == NDCH - 1))
            smt.inc(t.drain())
            cnt[("b_t_q", c)] = smt.n

        @blk.scalar
        def _(sc, c=c, p=p):
            wt(sc, smt, ("b_t_q", c))
            sc.copy(qtmp[p][:, 0:512], psA[0][:, :])
            sms.inc(sc.copy(qtmp[p][:, 512:1024], psA[1][:, :]))
            cnt[("b_s_evq", c)] = sms.n

        @blk.vector
        def _(v, c=c, p=p):
            d_rl.wait(v, c)
            v.tensor_copy(crepb[p][:, :], crepf[p][:, :])
            wt(v, sms, ("b_s_evq", c))
            if c >= 2:
                d_qt.wait(v, c - 2)
            q1 = r3(qtmp[p])[:, :, 0:32]
            q2 = r3(qtmp[p])[:, :, 32:64]
            o1 = r3(qrop[p])[:, :, 0:32]
            o2 = r3(qrop[p])[:, :, 32:64]
            cc_ = c3(crepb[p][:, 0:512])
            ss_ = c3(crepb[p][:, 512:1024])
            sA, sB2 = c3(ropes[:, 0:512]), c3(ropes[:, 512:1024])
            sC, sD = c3(ropes[:, 1024:1536]), c3(ropes[:, 1536:2048])
            v.drain()
            v.tensor_tensor(sA, q1, cc_, ALU.mult)
            v.tensor_tensor(sB2, q2, ss_, ALU.mult)
            v.tensor_tensor(sC, q1, ss_, ALU.mult)
            v.tensor_tensor(sD, q2, cc_, ALU.mult)
            v.drain()
            v.tensor_tensor(o1, sA, sB2, ALU.subtract)
            v.tensor_tensor(o2, sC, sD, ALU.add)
            smv.inc(v.drain())
            cnt[("b_v_rope", c)] = smv.n

        @blk.scalar
        def _(sc, c=c, p=p):
            wt(sc, smv, ("b_v_rope", c))
            sc.activation(phi1[p][:, :], qrop[p][:, :], AF.Relu, scale=-1.0)
            sc.drain()
            sms.inc(sc.activation(phi2[p][:, :], phi1[p][:, :], AF.Exp, scale=-1.0))
            cnt[("b_s_phi", c)] = sms.n

        @blk.vector
        def _(v, c=c, p=p):
            wt(v, sms, ("b_s_phi", c))
            v.scalar_tensor_tensor(qtil[p][:, :], qrop[p][:, :], 0.0,
                                   phi2[p][:, :], ALU.max, ALU.add)
            smv.inc(v.drain())
            cnt[("b_v_qtil", c)] = smv.n
            if c == 0:
                wt(v, smv, "v_kvin")
            if c >= 2:
                wt(v, smt, ("b_t_y", c - 2))
                wt(v, smt, ("b_t_kc", c - 2))
            d_rl.wait(v, c)
            if c >= 1:
                v.tensor_add(ptot[p][:, :], ptot[p][:, :], kvin_b[:, :])
                v.tensor_add(prow[p][:, :], prow[p][:, :], ksumin_b[:, :])
            else:
                v.tensor_copy(ptot[p][:, :], kvin_b[:, :])
                v.tensor_copy(prow[p][:, :], ksumin_b[:, :])
            smv.inc(v.drain())
            cnt[("b_v_ptot", c)] = smv.n

        @blk.tensor
        def _(t, c=c, p=p):
            wt(t, smv, ("b_v_ptot", c))
            if c >= 1:
                wt(t, sms, ("b_s_evy", c - 1))
            for i in range(2):
                t.matmul(psB[i][:, :], ltri[:, :], ktmB[p][:, i * 512:(i + 1) * 512],
                         start=True, stop=False)
                t.matmul(psB[i][:, :], ones_r[:, :], prow[p][0:1, i * 512:(i + 1) * 512],
                         start=False, stop=True)
            smt.inc(t.drain())
            cnt[("b_t_kc", c)] = smt.n

        @blk.scalar
        def _(sc, c=c, p=p):
            wt(sc, smt, ("b_t_kc", c))
            sc.copy(kcb[p][:, 0:512], psB[0][:, :])
            sms.inc(sc.copy(kcb[p][:, 512:1024], psB[1][:, :]))
            cnt[("b_s_kcb", c)] = sms.n

        @blk.vector
        def _(v, c=c, p=p):
            wt(v, sms, ("b_s_kcb", c))
            v.tensor_mul(kcb[p][:, :], qtil[p][:, :], kcb[p][:, :])
            v.drain()
            v.tensor_reduce(den_s[p][:, :].rearrange("p (h o) -> p h o", o=1),
                            kcb[p][:, :].rearrange("p (h d) -> p h d", d=64),
                            AX, ALU.add)
            v.drain()
            v.tensor_scalar_max(den_s[p][:, :], den_s[p][:, :], 1e-6)
            v.drain()
            v.reciprocal(den_s[p][:, :], den_s[p][:, :])
            v.drain()
            v.tensor_copy(rdrep[:, :].rearrange("p (h d) -> p h d", d=64),
                          den_s[p][:, 0:H].unsqueeze(2).broadcast_to([128, H, 64]))
            v.drain()
            v.tensor_mul(qhat[p][:, :], qtil[p][:, :], rdrep[:, :])
            smv.inc(v.drain())
            cnt[("b_v_qhat", c)] = smv.n

        @blk.sync
        def _(s, c=c, p=p):
            wt(s, smv, ("b_v_qhat", c))
            if c >= 2:
                wt(s, smt, ("b_t_y", c - 2))
                wt(s, smt, ("b_t_st", c - 2, 1))
            for j in range(NDCH):
                d_qt.sem(c).inc(s.dma_start(out=qfmB[p][:, j * 128:(j + 1) * 128],
                                            in_=qhat[p][:, j * 128:(j + 1) * 128],
                                            transpose=True), 16)
            d_qt.mark(c)

        for r in range(2):
            @blk.tensor
            def _(t, c=c, p=p, r=r):
                d_qt.wait(t, c)
                if r == 0 and c >= 1:
                    wt(t, smv, ("b_v_mask", c - 1, 1))
                if r == 1:
                    wt(t, smv, ("b_v_mask", c, 0))
                for hh in (-1, 0, 2, 4, 6, -1, 1, 3, 5, 7):
                    if hh < 0:
                        t.matmul(kvps[0:8, 0:8], ltri[:, 0:8], zeros_b[:, 0:8],
                                 start=True, stop=True, skip_group_check=True)
                        continue
                    h = r * 8 + hh
                    hp, pr = h % 2, h // 2
                    t.matmul(psA[hh // 4][:, (hh % 4) * 128:(hh % 4) * 128 + 128],
                             kfmB[p][64 * hp:64 * hp + 64, pr * 128:(pr + 1) * 128],
                             qfmB[p][64 * hp:64 * hp + 64, pr * 128:(pr + 1) * 128],
                             start=(hh in (0, 4)), stop=(hh in (3, 7)),
                             tile_position=(64 * hp, 0), skip_group_check=True)
                smt.inc(t.drain())
                cnt[("b_t_st", c, r)] = smt.n

            @blk.vector
            def _(v, c=c, p=p, r=r):
                wt(v, smt, ("b_t_st", c, r))
                if r == 0 and c >= 1:
                    wt(v, smt, ("b_t_y", c - 1))
                for b in range(2):
                    v.tensor_mul(stmask[:, r * 1024 + b * 512: r * 1024 + b * 512 + 512],
                                 psA[b][:, :], mtri[:, :])
                smv.inc(v.drain())
                cnt[("b_v_mask", c, r)] = smv.n

            @blk.tensor
            def _(t, c=c, p=p, r=r):
                wt(t, smv, ("b_v_mask", c, r))
                for hh in range(8):
                    h = r * 8 + hh
                    hp, pr = h % 2, h // 2
                    yb = psB[(h * 64) // 512]
                    ycol = (h * 64) % 512
                    t.matmul(yb[:, ycol:ycol + 64],
                             stmask[:, h * 128:(h + 1) * 128],
                             vB[p][:, h * 64:(h + 1) * 64],
                             start=(h % 8 == 0), stop=False, skip_group_check=True)
                    t.matmul(yb[:, ycol:ycol + 64],
                             qfmB[p][64 * hp:64 * hp + 64, pr * 128:(pr + 1) * 128],
                             ptot[p][64 * hp:64 * hp + 64, 64 * pr:64 * pr + 64],
                             start=False, stop=(h % 8 == 7),
                             tile_position=(64 * hp, 0), skip_group_check=True)
                smt.inc(t.drain())
                cnt[("b_t_y", c) if r == 1 else ("b_t_y0", c)] = smt.n

        @blk.scalar
        def _(sc, c=c, p=p):
            wt(sc, smt, ("b_t_y", c))
            sc.copy(ytm[p][:, 0:512], psB[0][:, :])
            sms.inc(sc.copy(ytm[p][:, 512:1024], psB[1][:, :]))
            cnt[("b_s_evy", c)] = sms.n

        @blk.sync
        def _(s, c=c, p=p):
            wt(s, sms, ("b_s_evy", c))
            if c >= 2:
                wt(s, smt, ("b_t_wo", c - 2))
            for j in range(NDCH):
                d_yt.sem(c).inc(s.dma_start(out=yfm[p][:, j * 128:(j + 1) * 128],
                                            in_=ytm[p][:, j * 128:(j + 1) * 128],
                                            transpose=True), 16)
            d_yt.mark(c)
            wt(s, smv, ("b_v_opart", c - 1))
            d_xr.sem(c).inc(s.dma_start(out=xrel[:, :],
                                        in_=x_h.ap()[c * 128:(c + 1) * 128, :]), 16)
            d_xr.mark(c)

        @blk.tensor
        def _(t, c=c, p=p):
            d_yt.wait(t, c)
            if c >= 1:
                wt(t, smv, ("b_v_opart", c - 1))
            for i in range(2):
                for j in range(NDCH):
                    t.matmul(psA[i][:, :], yfm[p][:, j * 128:(j + 1) * 128],
                             war[:, WO + j * D + i * 512: WO + j * D + i * 512 + 512],
                             start=(j == 0), stop=(j == NDCH - 1))
            smt.inc(t.drain())
            cnt[("b_t_wo", c)] = smt.n

        @blk.vector
        def _(v, c=c, p=p):
            wt(v, smt, ("b_t_wo", c))
            d_xr.wait(v, c)
            if c >= 1:
                w_pt.wait(v, c - 1)
            for i in range(2):
                v.tensor_add(opart[:, i * 512:(i + 1) * 512], psA[i][:, :],
                             xrel[:, i * 512:(i + 1) * 512])
            smv.inc(v.drain())
            cnt[("b_v_opart", c)] = smv.n

        @blk.scalar
        def _(sc, c=c, p=p):
            wt(sc, smv, ("b_v_opart", c))
            w_pt.sem(c).inc(sc.dma_start(out=part_sp.ap()[c * 128:(c + 1) * 128, :],
                                         in_=opart[:, :]), 16)
            w_pt.mark(c)

        if c in w1max_sched:
            emit_w1_max(w1max_sched[c])
        if w1scale_at == c:
            emit_w1_scale()
        if c in w1q_sched:
            emit_w1_quant(w1q_sched[c], None)

    if NC < 24 or NOB:
        for d in range(NDCH):
            emit_w1_max(d)
        emit_w1_scale()
        for d in range(4):
            emit_w1_quant(d, cnt.get(("b_t_wo", NC - 1)))
    for d in range(4, NDCH):
        emit_w1_quant(d, cnt.get(("b_t_wo", NC - 1)))

    if PHASES < 3.5:
        return _finish()

    w2max_sched = {}
    w2q_sched = {}
    w2scale_at = None
    if NTT >= 8:
        for tt in range(4):
            w2max_sched[tt] = [tt * 8 + jj for jj in range(8)]
        w2scale_at = 4
        for tt in (5, 6):
            w2q_sched[tt] = [(tt - 5) * 16 + jj for jj in range(16)]

    for tt in range(NTT):
        p = tt % 2

        @blk.sync
        def _(s, tt=tt, p=p):
            if tt >= 2:
                wt(s, smv, ("m1_xq", tt - 2))
            if tt == 0:
                wt(s, smt, ("b_t_y", NC - 1))
                wt(s, smt, ("b_t_wo", NC - 1))
                wt(s, smv, ("b_v_opart", NC - 1))
                wt(s, sms, ("b_s_evy", NC - 1))
            if not M1NOXM:
                d_xm.sem(tt).inc(s.dma_start(out=xnm1[p][:, :],
                                             in_=xn_sp.ap()[:, :, tt * 512:(tt + 1) * 512]
                                             .transpose([1, 0, 2])), 16)
            d_xm.mark(tt)

        @blk.vector
        def _(v, tt=tt, p=p):
            d_xm.wait(v, tt)
            if tt == 0:
                wt(v, sms, ("w1q", NDCH - 1))
                w_pt.wait(v, NC - 1)
                w_pt.wait(v, NC - 2)
            if tt >= 2:
                wt(v, smt, ("m1_t", tt - 2, NJCH - 1))
            if not M1NOXQ:
                for hf in range(2):
                    v.tensor_scalar(qscr[:, :], xnm1[p][:, hf * 2048:(hf + 1) * 2048],
                                    scb[:, SC_RSX:SC_RSX + 1], MAGIC, ALU.mult, ALU.add)
                    v.drain()
                    v.tensor_scalar(xqb[p][:, hf * 2048:(hf + 1) * 2048], qscr[:, :],
                                    MAGIC, None, ALU.subtract)
                    v.drain()
            smv.inc(v.drain())
            cnt[("m1_xq", tt)] = smv.n

        for j in range(NJCH):
            bank = j % (2 if M1B2 else 4)

            @blk.tensor
            def _(t, tt=tt, p=p, j=j, bank=bank):
                wt(t, smv, ("m1_xq", tt))
                idx = tt * NJCH + j
                nb = 2 if M1B2 else 4
                if idx == 0:
                    wt(t, sms, ("w1q", NDCH - 1))
                if idx >= nb:
                    wt(t, sms, ("m1_h", idx - nb))
                tgt = psA[bank] if bank < 2 else psB[bank - 2]
                for d in range(1 if M1LITE else NDCH):
                    t.matmul(tgt[:, :], war[:, d * FF + j * 128: d * FF + j * 128 + 128],
                             xqb[p][:, d * 512:(d + 1) * 512],
                             start=(d == 0), stop=(d == (0 if M1LITE else NDCH - 1)))
                smt.inc(t.drain())
                cnt[("m1_t", tt, j)] = smt.n

            @blk.scalar
            def _(sc, tt=tt, p=p, j=j, bank=bank):
                wt(sc, smt, ("m1_t", tt, j))
                idx = tt * NJCH + j
                nb = 2 if M1B2 else 4
                if idx >= nb:
                    w_h.wait(sc, idx // 4 - 1)
                    wt(sc, smv, ("m1_hmx", idx - nb))
                tgt = psA[bank] if bank < 2 else psB[bank - 2]
                fn = AF.Identity if NOGELU else AF.Gelu
                if M1EVC:
                    sms.inc(sc.copy(hb[idx % 4][:, :], tgt[:, :]))
                else:
                    sms.inc(sc.activation(hb[idx % 4][:, :], tgt[:, :], fn,
                                          scale=scb[:, SC_SWS1:SC_SWS1 + 1],
                                          bias=b1s[:, j:j + 1]))
                cnt[("m1_h", idx)] = sms.n
                if j % 4 == 3:
                    gidx = idx // 4
                    sc.drain()
                    if not M1NOSP:
                        for jj in range(j - 3, j + 1):
                            w_h.sem(gidx).inc(sc.dma_start(
                                out=h_sp.ap()[4 * tt:4 * tt + 4, jj * 128:(jj + 1) * 128, :]
                                .transpose([1, 0, 2]),
                                in_=hb[jj % 4][:, :].rearrange("p (b t) -> p b t", t=128)), 16)
                    w_h.mark(gidx)

            @blk.vector
            def _(v, tt=tt, j=j):
                idx = tt * NJCH + j
                wt(v, sms, ("m1_h", idx))
                if not M1NOHM:
                    v.reduce_max(thm[:, idx % 4:idx % 4 + 1], hb[idx % 4][:, :], AX,
                                 apply_absolute_value=True)
                    if idx % 4 == 3:
                        v.drain()
                        v.reduce_max(thm4[:, :], thm[:, :], AX)
                        v.drain()
                        v.tensor_max(hmx[:, :], hmx[:, :], thm4[:, :])
                smv.inc(v.drain())
                cnt[("m1_hmx", idx)] = smv.n

        if tt in w2max_sched:
            for jw in w2max_sched[tt]:
                emit_w2_max(jw)
        if w2scale_at == tt:
            emit_w2_scale()
        if tt in w2q_sched:
            for jw in w2q_sched[tt]:
                emit_w2_quant(jw)

    if PHASES < 3.8:
        return _finish()

    if w2scale_at is None:
        for jw in range(NJCH):
            emit_w2_max(jw)
        emit_w2_scale()
        for jw in range(NJCH):
            emit_w2_quant(jw)

    @blk.tensor
    def _(t):
        wt(t, smv, ("m1_hmx", NTT * NJCH - 1))
        smt.inc(t.transpose(smps, hmx[:, :], identf[:, :]))
        cnt["t_htp"] = smt.n

    @blk.vector
    def _(v):
        wt(v, smt, "t_htp")
        v.reduce_max(g1[0:1, 10:11], smps, AX)
        smv.inc(v.drain())
        cnt["v_hg"] = smv.n

    @blk.gpsimd
    def _(g):
        w_h.wait_all(g)
        w_pt.wait_all(g)
        w_w2.wait_all(g)
        wt(g, smv, "v_hg")
        s_gb.inc(g.dma_start(out=ar2i.ap(), in_=g1[0:1, 10:11]), 16)
        g.wait_ge(s_gb.h, s_gb.n)
        s_cc.inc(g.collective_compute("AllReduce", ALU.max,
                                      replica_groups=[list(range(ncores))],
                                      ins=[ar2i.ap().opt()], outs=[ar2o.ap().opt()]))
        g.wait_ge(s_cc.h, s_cc.n)
        s_gb.inc(g.dma_start(out=g1[0:1, 11:12], in_=ar2o.ap()), 16)
        cnt["g_ar2"] = s_gb.n
        g.wait_ge(s_gb.h, s_gb.n)

    @blk.vector
    def _(v):
        wt(v, s_gb, "g_ar2")
        v.tensor_scalar_max(g1[0:1, 12:13], g1[0:1, 11:12], 1e-6)
        v.drain()
        v.tensor_scalar_mul(g1[0:1, 13:14], g1[0:1, 12:13], 1.0 / 127.0)
        v.drain()
        v.reciprocal(scrow2[0:1, SC2_RSH:SC2_RSH + 1], g1[0:1, 13:14])
        v.drain()
        v.tensor_mul(scrow2[0:1, SC2_S2B:SC2_S2B + 1], g1[0:1, 13:14], g1[0:1, 9:10])
        smv.inc(v.drain())
        cnt["v_sc2"] = smv.n

    @blk.tensor
    def _(t):
        wt(t, smv, "v_sc2")
        smt.inc(t.matmul(bcps, ones_rf[:, :], scrow2[:, :], start=True, stop=True))
        cnt["t_bc2"] = smt.n

    @blk.vector
    def _(v):
        wt(v, smt, "t_bc2")
        v.tensor_copy(scb2[:, :], bcps)
        smv.inc(v.drain())
        cnt["v_bc2"] = smv.n

    @blk.sync
    def _(s):
        wt(s, smt, ("m1_t", NTT - 1, NJCH - 1))
        wt(s, sms, ("w2q", NJCH - 1))
        s.wait_ge(s_gb.h, 128)
        w_w2.wait_all(s)
        for part in range(4):
            d_q2.sem(0).inc(s.dma_start(
                out=war[:, part * 8192:(part + 1) * 8192],
                in_=wq2_sp.ap()[part * 1024:(part + 1) * 1024, :]
                .rearrange("(j p) i -> p j i", p=128)), 16)
        d_q2.mark(0)

    if PHASES < 5:
        return _finish()

    for tc in range(NC):
        p = tc % 2

        @blk.sync
        def _(s, tc=tc, p=p):
            if tc >= 2:
                wt(s, smv, ("m2_hq", tc - 2))
            if tc >= 1:
                wt(s, smv, ("m2_fin", tc - 1))
            if tc == 0:
                w_h.wait_all(s)
                w_pt.wait_all(s)
                wt(s, sms, ("w2q", NJCH - 1))
                wt(s, smv, ("m1_hmx", NTT * NJCH - 1))
                wt(s, smv, ("m1_xq", NTT - 1))
            d_hb.sem(tc).inc(s.dma_start(out=hblf[p][:, :],
                                         in_=h_sp.ap()[tc, :, :]
                                         .rearrange("(j p) t -> p j t", p=128)), 16)
            d_hb.sem(tc).inc(s.dma_start(out=oprel[:, :],
                                         in_=part_sp.ap()[tc * 128:(tc + 1) * 128, :]), 16)
            d_hb.mark(tc)

        @blk.vector
        def _(v, tc=tc, p=p):
            d_hb.wait(v, tc)
            if tc == 0:
                wt(v, smv, "v_bc2")
            if tc >= 2:
                wt(v, smt, ("m2_t", tc - 2))
            v.tensor_scalar(hblf[p][:, :], hblf[p][:, :], scb2[:, SC2_RSH:SC2_RSH + 1],
                            MAGIC, ALU.mult, ALU.add)
            v.drain()
            v.tensor_scalar(hqb[p][:, :], hblf[p][:, :], MAGIC, None, ALU.subtract)
            smv.inc(v.drain())
            cnt[("m2_hq", tc)] = smv.n

        @blk.tensor
        def _(t, tc=tc, p=p):
            wt(t, smv, ("m2_hq", tc))
            if tc == 0:
                d_q2.wait(t, 0)
                wt(t, sms, ("m1_h", NTT * NJCH - 1))
            if tc >= 1:
                wt(t, smv, ("m2_fin", tc - 1))
            for i in range(2):
                for j in range(1 if M2LITE else NJCH):
                    t.matmul(psA[i][:, :], hqb[p][:, j * 128:(j + 1) * 128],
                             war[:, j * D + i * 512: j * D + i * 512 + 512],
                             start=(j == 0), stop=False)
                t.matmul(psA[i][:, :], ones_r[:, :], b2b[0:1, i * 512:(i + 1) * 512],
                         start=False, stop=True)
            smt.inc(t.drain())
            cnt[("m2_t", tc)] = smt.n

        @blk.vector
        def _(v, tc=tc, p=p):
            wt(v, smt, ("m2_t", tc))
            if tc >= 1:
                w_out.wait(v, tc - 1)
            for i in range(2):
                v.scalar_tensor_tensor(fin[:, i * 512:(i + 1) * 512], psA[i][:, :],
                                       scb2[:, SC2_S2B:SC2_S2B + 1],
                                       oprel[:, i * 512:(i + 1) * 512],
                                       ALU.mult, ALU.add)
            smv.inc(v.drain())
            cnt[("m2_fin", tc)] = smv.n

        @blk.scalar
        def _(sc, tc=tc, p=p):
            wt(sc, smv, ("m2_fin", tc))
            w_out.sem(tc).inc(sc.dma_start(out=out_h.ap()[tc * 128:(tc + 1) * 128, :],
                                           in_=fin[:, :]), 16)
            w_out.mark(tc)

    @blk.gpsimd
    def _(g):
        w_out.wait_all(g)

    es.close()
    return nc


def _host_tables(NT, pos0):
    pos = np.arange(pos0, pos0 + NT, dtype=np.float64)
    inv = 10000.0 ** (-np.arange(32, dtype=np.float64) / 32.0)
    ang = pos[:, None] * inv[None, :]
    crep = np.tile(np.cos(ang).astype(np.float32), (1, H))
    srep = np.tile(np.sin(ang).astype(np.float32), (1, H))
    return crep, srep


def make_in_maps(x, Wq, Wk, Wv, Wo, w1, b1, w2, b2, ncores, NC):
    NT = NC * 128
    wqT = np.ascontiguousarray(np.asarray(Wq, np.float32).T)
    wkT = np.ascontiguousarray(np.asarray(Wk, np.float32).T)
    wvT = np.ascontiguousarray(np.asarray(Wv, np.float32).T)
    woT = np.ascontiguousarray(np.asarray(Wo, np.float32).T)
    w1T = np.ascontiguousarray(np.asarray(w1, np.float32).T)
    w2T = np.ascontiguousarray(np.asarray(w2, np.float32).T)
    b1fm = np.ascontiguousarray(np.asarray(b1, np.float32).reshape(NJCH, 128).T)
    b2row = np.ascontiguousarray(np.asarray(b2, np.float32).reshape(1, D))
    tri = np.triu(np.ones((128, 128), np.float32))
    consts = {
        "wqT": wqT, "wkT": wkT, "wvT": wvT, "woT": woT, "w1T": w1T, "w2T": w2T,
        "b1fm": b1fm, "b2row": b2row,
        "ltri": tri.astype(BF), "mtri": np.tile(tri, (1, 4)).astype(np.float32),
        "ones_r": np.ones((1, 128), BF), "ones_c": np.ones((128, 1), BF),
        "ones_rf": np.ones((1, 128), np.float32),
        "identf": np.eye(128, dtype=np.float32),
    }
    in_maps = []
    for core in range(ncores):
        b, half = core // 2, core % 2
        xs = np.ascontiguousarray(np.asarray(x, np.float32)[b, half * NT:(half + 1) * NT, :])
        crep, srep = _host_tables(NT, half * NT)
        m = dict(consts)
        m.update({
            "x": xs, "crep": crep, "srep": srep,
            "mke": np.full((128, 1), 1.0 - half, np.float32),
            "mko": np.full((128, 1), float(half), np.float32),
        })
        in_maps.append(m)
    return in_maps


def kernel(L0_attn_x, L0_attn_Wq, L0_attn_Wk, L0_attn_Wv, L0_attn_Wo,
           L0_ffn_w1, L0_ffn_b1, L0_ffn_w2, L0_ffn_b2):
    NCORES = 8
    NC = 32
    NT = NC * 128
    x = np.asarray(L0_attn_x, dtype=np.float32)
    B, T, _ = x.shape
    nc = build_core_program(NC=NC, ncores=NCORES)
    in_maps = make_in_maps(x, L0_attn_Wq, L0_attn_Wk, L0_attn_Wv, L0_attn_Wo,
                           L0_ffn_w1, L0_ffn_b1, L0_ffn_w2, L0_ffn_b2, NCORES, NC)
    import time as _time
    last = None
    for _attempt in range(6):
        try:
            res = run_bass_kernel_spmd(nc, in_maps, core_ids=list(range(NCORES)),
                                       trace=True)
            break
        except Exception as e:
            last = e
            if _attempt == 5:
                raise
            _time.sleep(10.0)
    out = np.empty((B, T, D), np.float32)
    for core in range(NCORES):
        b, half = core // 2, core % 2
        out[b, half * NT:(half + 1) * NT, :] = res.results[core]["out"]
    return out


# version 04 (final)
# speedup vs baseline: 1.0622x; 1.0619x over previous
import contextlib
import numpy as np
import ml_dtypes
import concourse.bass as bass
import concourse.mybir as mybir
from concourse.bass_utils import run_bass_kernel_spmd

F32 = mybir.dt.float32
BF16 = mybir.dt.bfloat16
AF = mybir.ActivationFunctionType
ALU = mybir.AluOpType
AX = mybir.AxisListType.X
BF = ml_dtypes.bfloat16

MAGIC = 12582912.0
D = 1024
H = 16
NDCH = 8
FF = 4096
NJCH = 32


class Sem:
    def __init__(self, es, nc, name):
        self.h = es.enter_context(nc.semaphore(name))
        self.n = 0

    def inc(self, instr, k=1):
        instr.then_inc(self.h, k)
        self.n += k
        return self.n


class DS:

    def __init__(self, es, nc, name, nbuf=2):
        self.s = [Sem(es, nc, f"{name}{i}") for i in range(nbuf)]
        self.nbuf = nbuf
        self.g = {}

    def sem(self, idx):
        return self.s[idx % self.nbuf]

    def mark(self, idx):
        self.g[idx] = self.sem(idx).n

    def wait(self, eng, idx):
        if idx >= 0 and idx in self.g:
            eng.wait_ge(self.sem(idx).h, self.g[idx])

    def wait_all(self, eng):
        for s in self.s:
            eng.wait_ge(s.h, s.n)


PHASES = 5
NOGELU = 0
M1B2 = 0
M2LITE = 0
M1LITE = 0
NOB = 0
M1NOXM = 0
M1NOXQ = 0
M1NOSP = 0
M1EVC = 0
M1NOHM = 0


def build_core_program(NC=32, ncores=8):
    NT = NC * 128
    NTT = NT // 512
    assert NTT * 512 == NT and NTT >= 1
    nc = bass.Bass("TRN2", target_bir_lowering=False)

    x_h = nc.dram_tensor("x", [NT, D], F32, kind="ExternalInput")
    wq_h = nc.dram_tensor("wqT", [D, D], F32, kind="ExternalInput")
    wk_h = nc.dram_tensor("wkT", [D, D], F32, kind="ExternalInput")
    wv_h = nc.dram_tensor("wvT", [D, D], F32, kind="ExternalInput")
    wo_h = nc.dram_tensor("woT", [D, D], F32, kind="ExternalInput")
    w1_h = nc.dram_tensor("w1T", [D, FF], F32, kind="ExternalInput")
    w2_h = nc.dram_tensor("w2T", [FF, D], F32, kind="ExternalInput")
    b1_h = nc.dram_tensor("b1fm", [128, NJCH], F32, kind="ExternalInput")
    b2_h = nc.dram_tensor("b2row", [1, D], F32, kind="ExternalInput")
    crep_h = nc.dram_tensor("crep", [NT, 512], F32, kind="ExternalInput")
    srep_h = nc.dram_tensor("srep", [NT, 512], F32, kind="ExternalInput")
    ltri_h = nc.dram_tensor("ltri", [128, 128], BF16, kind="ExternalInput")
    mtri_h = nc.dram_tensor("mtri", [128, 512], F32, kind="ExternalInput")
    ones_r_h = nc.dram_tensor("ones_r", [1, 128], BF16, kind="ExternalInput")
    ones_c_h = nc.dram_tensor("ones_c", [128, 1], BF16, kind="ExternalInput")
    ones_rf_h = nc.dram_tensor("ones_rf", [1, 128], F32, kind="ExternalInput")
    ident_h = nc.dram_tensor("identf", [128, 128], F32, kind="ExternalInput")
    mke_h = nc.dram_tensor("mke", [128, 1], F32, kind="ExternalInput")
    mko_h = nc.dram_tensor("mko", [128, 1], F32, kind="ExternalInput")
    out_h = nc.dram_tensor("out", [NT, D], F32, kind="ExternalOutput")

    xn_sp = nc.dram_tensor("xn_sp", [NDCH, 128, NT], BF16)
    kfm_sp = nc.dram_tensor("kfm_sp", [NT, D], BF16)
    ktm_sp = nc.dram_tensor("ktm_sp", [NT, D], BF16)
    v_sp = nc.dram_tensor("v_sp", [NT, D], BF16)
    pkv_sp = nc.dram_tensor("pkv_sp", [NT, 512], BF16)
    pks_sp = nc.dram_tensor("pks_sp", [NC, D], BF16)
    part_sp = nc.dram_tensor("part_sp", [NT, D], F32)
    h_sp = nc.dram_tensor("h_sp", [NC, FF, 128], F32)
    wq2_sp = nc.dram_tensor("wq2_sp", [FF, D], BF16)
    ar3i = nc.dram_tensor("ar3i", [1, 128 * 512 + D], F32)
    ar3o = nc.dram_tensor("ar3o", [1, 128 * 512 + D], F32)
    ar1i = nc.dram_tensor("ar1i", [1, 1], F32)
    ar1o = nc.dram_tensor("ar1o", [1, 1], F32)
    ar2i = nc.dram_tensor("ar2i", [1, 1], F32)
    ar2o = nc.dram_tensor("ar2o", [1, 1], F32)

    es = contextlib.ExitStack()
    E = es.enter_context

    war = E(nc.sbuf_tensor("war", [128, 4 * NDCH * D], BF16))
    WK, WV, WQ, WO = 0, 8192, 16384, 24576

    ltri = E(nc.sbuf_tensor("ltri_s", [128, 128], BF16))
    mtri = E(nc.sbuf_tensor("mtri_s", [128, 512], F32))
    ones_r = E(nc.sbuf_tensor("ones_r_s", [1, 128], BF16))
    ones_c = E(nc.sbuf_tensor("ones_c_s", [128, 1], BF16))
    ones_rf = E(nc.sbuf_tensor("ones_rf_s", [1, 128], F32))
    identf = E(nc.sbuf_tensor("identf_s", [128, 128], F32))
    mke = E(nc.sbuf_tensor("mke_s", [128, 1], F32))
    mko = E(nc.sbuf_tensor("mko_s", [128, 1], F32))
    b1fm = E(nc.sbuf_tensor("b1fm_s", [128, NJCH], F32))
    b2b = E(nc.sbuf_tensor("b2b_s", [1, D], BF16))
    zeros_b = E(nc.sbuf_tensor("zeros_b", [128, 512], BF16))

    lnsum = [E(nc.sbuf_tensor(f"lnsum{i}", [128, 1], F32)) for i in range(2)]
    lnss = [E(nc.sbuf_tensor(f"lnss{i}", [128, 1], F32)) for i in range(2)]
    lnm = [E(nc.sbuf_tensor(f"lnm{i}", [128, 1], F32)) for i in range(2)]
    lnmsq = [E(nc.sbuf_tensor(f"lnmsq{i}", [128, 1], F32)) for i in range(2)]
    lnvar = [E(nc.sbuf_tensor(f"lnvar{i}", [128, 1], F32)) for i in range(2)]
    lnrstd = [E(nc.sbuf_tensor(f"lnrstd{i}", [128, 1], F32)) for i in range(2)]
    lnnm = [E(nc.sbuf_tensor(f"lnnm{i}", [128, 1], F32)) for i in range(2)]
    tmx = [E(nc.sbuf_tensor(f"tmx{i}", [128, 1], F32)) for i in range(2)]
    thm = E(nc.sbuf_tensor("thm", [128, 4], F32))
    thm4 = E(nc.sbuf_tensor("thm4", [128, 1], F32))
    mxacc = E(nc.sbuf_tensor("mxacc", [128, 1], F32))
    hmx = E(nc.sbuf_tensor("hmx", [128, 1], F32))
    w1mx = E(nc.sbuf_tensor("w1mx", [128, 1], F32))
    w2mx = E(nc.sbuf_tensor("w2mx", [128, 1], F32))
    den_s = [E(nc.sbuf_tensor(f"den{i}", [128, H], F32)) for i in range(2)]
    arst = E(nc.sbuf_tensor("arst", [128, 512], F32))
    arks = E(nc.sbuf_tensor("arks", [1, D], F32))
    kvinf = E(nc.sbuf_tensor("kvinf", [128, 512], F32))
    ksuminf = E(nc.sbuf_tensor("ksuminf", [1, D], F32))
    kvin_b = E(nc.sbuf_tensor("kvin_b", [128, 512], BF16))
    ksumin_b = E(nc.sbuf_tensor("ksumin_b", [1, D], BF16))
    g1 = E(nc.sbuf_tensor("g1", [1, 16], F32))
    scrow = E(nc.sbuf_tensor("scrow", [1, 8], F32))
    scrow2 = E(nc.sbuf_tensor("scrow2", [1, 8], F32))
    scb = E(nc.sbuf_tensor("scb", [128, 8], F32))
    scb2 = E(nc.sbuf_tensor("scb2", [128, 8], F32))
    SC_RSX, SC_SWS1, SC_RSW1 = 0, 1, 2
    SC2_RSH, SC2_S2B, SC2_RSW2 = 0, 1, 2
    b1s = E(nc.sbuf_tensor("b1s", [128, NJCH], F32))
    constf = E(nc.sbuf_tensor("constf", [128, 4], F32))

    A16_COLS = 32768
    A32_COLS = 14336
    a16 = E(nc.sbuf_tensor("a16", [128, A16_COLS], BF16))
    a32 = E(nc.sbuf_tensor("a32", [128, A32_COLS], F32))

    def carve(arena, cur, cols, n=None, rows=128):
        if n is None:
            ap = arena[0:rows, cur[0]:cur[0] + cols]
            cur[0] += cols
            return ap
        return [carve(arena, cur, cols, None, rows) for _ in range(n)]

    cS16, cS32 = [0], [0]
    crepb = carve(a16, cS16, 1024, 2)
    ropes = carve(a16, cS16, 2048)
    phi1 = carve(a16, cS16, 1024, 2)
    phi2 = carve(a16, cS16, 1024, 2)
    rowslot = carve(a16, cS16, 1024, 2, rows=1)
    S16 = cS16[0]
    crepf = carve(a32, cS32, 1024, 2)
    wqscr = carve(a32, cS32, 4096)
    w1f = carve(a32, cS32, 4096)
    S32 = cS32[0]

    c16, c32 = [S16], [S32 - 8192]
    xbuf = carve(a32, c32, 1024, 2)
    xn32 = carve(a32, c32, 1024, 2)
    sqscr = carve(a32, c32, 1024, 2)
    xntm = carve(a16, c16, 1024, 2)
    xnfmb = carve(a16, c16, 1024, 2)
    ktmp = carve(a16, c16, 1024, 2)
    krop = carve(a16, c16, 1024, 2)
    ktil = krop
    vbuf = carve(a16, c16, 1024, 2)
    kfmb = carve(a16, c16, 1024, 2)
    pkvst = carve(a16, c16, 512, 2)
    pksst = rowslot
    assert c16[0] <= A16_COLS and c32[0] <= A32_COLS, (c16, c32)

    c16, c32 = [S16], [S32]
    xrel = carve(a32, c32, 1024, 2)
    opart = carve(a32, c32, 1024, 2)
    xnB = carve(a16, c16, 1024, 2)
    ktmB = carve(a16, c16, 1024, 2)
    kfmB = carve(a16, c16, 1024, 2)
    vB = carve(a16, c16, 1024, 2)
    qtmp = carve(a16, c16, 1024, 2)
    qrop = carve(a16, c16, 1024, 2)
    qtil = qrop
    qhat = qrop
    qfmB = carve(a16, c16, 1024, 2)
    ptot = carve(a16, c16, 512, 2)
    prow = rowslot
    kcb = carve(a16, c16, 1024, 2)
    rdrep = carve(a16, c16, 1024)
    stmask = carve(a16, c16, 2048)
    ytm = carve(a16, c16, 1024, 2)
    yfm = ktmB
    assert c16[0] <= A16_COLS and c32[0] <= A32_COLS, (c16, c32)

    c16, c32 = [S16], [S32 - 4096]
    w2f = carve(a32, c32, 1024, 2)
    hb_off = c32[0]
    hb = carve(a32, c32, 512, 4)
    hball = a32[:, hb_off:hb_off + 2048]
    qscr = carve(a32, c32, 4096)
    xnm1 = carve(a16, c16, 4096, 2)
    xqb = carve(a16, c16, 4096, 2)
    wq2c = carve(a16, c16, 1024, 2)
    assert c16[0] <= A16_COLS and c32[0] <= A32_COLS, (c16, c32)

    c16, c32 = [S16], [S32 - 8192]
    hblf = carve(a32, c32, 4096, 2)
    oprel = carve(a32, c32, 1024, 2)
    fin = carve(a32, c32, 1024, 2)
    hqb = carve(a16, c16, 4096, 2)
    assert c16[0] <= A16_COLS and c32[0] <= A32_COLS, (c16, c32)

    psA = [E(nc.psum_tensor(f"psA{i}", [128, 512], F32)) for i in range(2)]
    psB = [E(nc.psum_tensor(f"psB{i}", [128, 512], F32)) for i in range(2)]
    kvps = E(nc.psum_tensor("kvps", [128, 512], F32))
    ksps = E(nc.psum_tensor("ksps", [1, D], F32))
    misc = E(nc.psum_tensor("miscps", [128, 136], F32))
    smps = misc[0:1, 0:128]
    bcps = misc[:, 128:136]

    d_ld = DS(es, nc, "d_ld")
    d_xt = DS(es, nc, "d_xt")
    d_kt = DS(es, nc, "d_kt")
    d_qt = DS(es, nc, "d_qt")
    d_yt = DS(es, nc, "d_yt")
    d_rl = DS(es, nc, "d_rl")
    d_xr = DS(es, nc, "d_xr")
    d_w1 = DS(es, nc, "d_w1")
    d_w2 = DS(es, nc, "d_w2")
    d_xm = DS(es, nc, "d_xm")
    d_hb = DS(es, nc, "d_hb")
    d_q2 = DS(es, nc, "d_q2")
    w_a = DS(es, nc, "w_a")
    w_pt = DS(es, nc, "w_pt")
    w_h = DS(es, nc, "w_h")
    w_w2 = DS(es, nc, "w_w2")
    w_out = DS(es, nc, "w_out")
    s_gw = Sem(es, nc, "s_gw")
    s_gb = Sem(es, nc, "s_gb")
    s_cc = Sem(es, nc, "s_cc")
    s_cn = Sem(es, nc, "s_cn")
    smt = Sem(es, nc, "smt")
    sms = Sem(es, nc, "sms")
    smv = Sem(es, nc, "smv")

    blk = E(nc.Block())
    cnt = {}

    def wt(eng, sem, key):
        if key in cnt:
            eng.wait_ge(sem.h, cnt[key])

    def _finish():
        @blk.gpsimd
        def _(g):
            g.wait_ge(smt.h, smt.n)
            g.wait_ge(sms.h, sms.n)
            g.wait_ge(smv.h, smv.n)
            for dsx in (d_ld, d_xt, d_kt, d_qt, d_yt, d_rl, d_xr, d_w1,
                        d_w2, d_xm, d_hb, d_q2, w_a, w_pt, w_h, w_w2, w_out):
                dsx.wait_all(g)
        es.close()
        return nc

    def r3(t):
        return t.rearrange("p (h d) -> p h d", d=64)

    def c3(t):
        return t.rearrange("p (h j) -> p h j", j=32)

    @blk.sync
    def _(s):
        for dst, src in ((ltri, ltri_h), (mtri, mtri_h), (ones_r, ones_r_h),
                         (ones_c, ones_c_h), (ones_rf, ones_rf_h), (identf, ident_h),
                         (mke, mke_h), (mko, mko_h), (b1fm, b1_h)):
            s_cn.inc(s.dma_start(out=dst[:, :], in_=src.ap()), 16)

    @blk.gpsimd
    def _(g):
        for off, src in ((WK, wk_h), (WV, wv_h), (WQ, wq_h), (WO, wo_h)):
            s_gw.inc(g.dma_start(out=war[:, off:off + 8192],
                                 in_=src.ap().rearrange("(j p) d -> p j d", p=128)), 16)
        s_gw.inc(g.dma_start(out=b2b[:, :], in_=b2_h.ap()), 16)

    for i, val in enumerate((1e-5, -1.0, MAGIC, -MAGIC)):
        nc.const_aps.register(F32, val, constf[:, i:i + 1])

    @blk.vector
    def _(v):
        for i, val in enumerate((1e-5, -1.0, MAGIC, -MAGIC)):
            v.memset(constf[:, i:i + 1], val)
        v.memset(zeros_b[:, :], 0.0)
        v.memset(scrow[:, :], 0.0)
        v.memset(scrow2[:, :], 0.0)
        v.memset(mxacc[:, :], 0.0)
        v.memset(hmx[:, :], 0.0)
        v.memset(w1mx[:, :], 0.0)
        v.memset(w2mx[:, :], 0.0)
        smv.inc(v.drain())
        cnt["v_init"] = smv.n

    for c in range(NC):
        p = c % 2

        @blk.sync
        def _(s, c=c, p=p):
            if c >= 2:
                wt(s, sms, ("a_s_xn", c - 2))
                wt(s, smv, ("a_v_cast", c - 2))
            d_ld.sem(c).inc(s.dma_start(out=xbuf[p][:, :],
                                        in_=x_h.ap()[c * 128:(c + 1) * 128, :]), 16)
            d_ld.sem(c).inc(s.dma_start(out=crepf[p][:, 0:512],
                                        in_=crep_h.ap()[c * 128:(c + 1) * 128, :]), 16)
            d_ld.sem(c).inc(s.dma_start(out=crepf[p][:, 512:1024],
                                        in_=srep_h.ap()[c * 128:(c + 1) * 128, :]), 16)
            d_ld.mark(c)

        @blk.vector
        def _(v, c=c, p=p):
            d_ld.wait(v, c)
            v.reduce_sum(lnsum[p][:, :], xbuf[p][:, :], AX)
            smv.inc(v.drain())
            cnt[("a_v_sum", c)] = smv.n

        @blk.scalar
        def _(sc, c=c, p=p):
            d_ld.wait(sc, c)
            sms.inc(sc.activation(sqscr[p][:, :], xbuf[p][:, :], AF.Square,
                                  accum_out=lnss[p][:, 0:1]))
            cnt[("a_s_ss", c)] = sms.n

        @blk.vector
        def _(v, c=c, p=p):
            wt(v, sms, ("a_s_ss", c))
            v.tensor_scalar_mul(lnm[p][:, :], lnsum[p][:, :], 1.0 / D)
            v.drain()
            v.tensor_mul(lnmsq[p][:, :], lnm[p][:, :], lnm[p][:, :])
            v.drain()
            v.scalar_tensor_tensor(lnvar[p][:, :], lnss[p][:, :], 1.0 / D,
                                   lnmsq[p][:, :], ALU.mult, ALU.subtract)
            smv.inc(v.drain())
            cnt[("a_v_var", c)] = smv.n

        @blk.scalar
        def _(sc, c=c, p=p):
            wt(sc, smv, ("a_v_var", c))
            sms.inc(sc.activation(lnvar[p][:, :], lnvar[p][:, :], AF.Sqrt, bias=1e-5))
            cnt[("a_s_rstd", c)] = sms.n

        @blk.vector
        def _(v, c=c, p=p):
            wt(v, sms, ("a_s_rstd", c))
            v.reciprocal(lnrstd[p][:, :], lnvar[p][:, :])
            v.drain()
            v.scalar_tensor_tensor(lnnm[p][:, :], lnm[p][:, :], -1.0,
                                   lnrstd[p][:, :], ALU.mult, ALU.mult)
            smv.inc(v.drain())
            cnt[("a_v_nm", c)] = smv.n

        @blk.scalar
        def _(sc, c=c, p=p):
            wt(sc, smv, ("a_v_nm", c))
            sms.inc(sc.activation(xn32[p][:, :], xbuf[p][:, :], AF.Identity,
                                  scale=lnrstd[p][:, 0:1], bias=lnnm[p][:, 0:1]))
            cnt[("a_s_xn", c)] = sms.n

        @blk.vector
        def _(v, c=c, p=p):
            wt(v, sms, ("a_s_xn", c))
            v.reduce_max(tmx[p][:, :], xn32[p][:, :], AX, apply_absolute_value=True)
            v.drain()
            v.tensor_max(mxacc[:, :], mxacc[:, :], tmx[p][:, :])
            v.drain()
            v.tensor_copy(xntm[p][:, :], xn32[p][:, :])
            v.tensor_copy(crepb[p][:, :], crepf[p][:, :])
            smv.inc(v.drain())
            cnt[("a_v_cast", c)] = smv.n

        @blk.sync
        def _(s, c=c, p=p):
            wt(s, smv, ("a_v_cast", c))
            if c >= 2:
                wt(s, smt, ("a_t_proj", c - 2))
                w_a.wait(s, c - 2)
            for j in range(NDCH):
                d_xt.sem(c).inc(s.dma_start(out=xnfmb[p][:, j * 128:(j + 1) * 128],
                                            in_=xntm[p][:, j * 128:(j + 1) * 128],
                                            transpose=True), 16)
            d_xt.mark(c)


        @blk.tensor
        def _(t, c=c, p=p):
            d_xt.wait(t, c)
            if c == 0:
                t.wait_ge(s_gw.h, s_gw.n)
                t.wait_ge(s_cn.h, s_cn.n)
            if c >= 1:
                wt(t, sms, ("a_s_evkv", c - 1))
            for (ps, woff) in ((psA, WV), (psB, WK)):
                for i in range(2):
                    for j in range(NDCH):
                        t.matmul(ps[i][:, :], xnfmb[p][:, j * 128:(j + 1) * 128],
                                 war[:, woff + j * D + i * 512: woff + j * D + i * 512 + 512],
                                 start=(j == 0), stop=(j == NDCH - 1))
            smt.inc(t.drain())
            cnt[("a_t_proj", c)] = smt.n

        @blk.scalar
        def _(sc, c=c, p=p):
            wt(sc, smt, ("a_t_proj", c))
            if c >= 2:
                w_a.wait(sc, c - 2)
            sc.copy(vbuf[p][:, 0:512], psA[0][:, :])
            sc.copy(vbuf[p][:, 512:1024], psA[1][:, :])
            sc.copy(ktmp[p][:, 0:512], psB[0][:, :])
            sms.inc(sc.copy(ktmp[p][:, 512:1024], psB[1][:, :]))
            cnt[("a_s_evkv", c)] = sms.n

        @blk.vector
        def _(v, c=c, p=p):
            wt(v, sms, ("a_s_evkv", c))
            if c >= 2:
                wt(v, smt, ("a_t_upd", c - 2))
                d_kt.wait(v, c - 2)
                w_a.wait(v, c - 2)
            k1 = r3(ktmp[p])[:, :, 0:32]
            k2 = r3(ktmp[p])[:, :, 32:64]
            o1 = r3(krop[p])[:, :, 0:32]
            o2 = r3(krop[p])[:, :, 32:64]
            cc_ = c3(crepb[p][:, 0:512])
            ss_ = c3(crepb[p][:, 512:1024])
            sA, sB2 = c3(ropes[:, 0:512]), c3(ropes[:, 512:1024])
            sC, sD = c3(ropes[:, 1024:1536]), c3(ropes[:, 1536:2048])
            v.drain()
            v.tensor_tensor(sA, k1, cc_, ALU.mult)
            v.tensor_tensor(sB2, k2, ss_, ALU.mult)
            v.tensor_tensor(sC, k1, ss_, ALU.mult)
            v.tensor_tensor(sD, k2, cc_, ALU.mult)
            v.drain()
            v.tensor_tensor(o1, sA, sB2, ALU.subtract)
            v.tensor_tensor(o2, sC, sD, ALU.add)
            smv.inc(v.drain())
            cnt[("a_v_rope", c)] = smv.n

        @blk.scalar
        def _(sc, c=c, p=p):
            wt(sc, smv, ("a_v_rope", c))
            sc.activation(phi1[p][:, :], krop[p][:, :], AF.Relu, scale=-1.0)
            sc.drain()
            sms.inc(sc.activation(phi2[p][:, :], phi1[p][:, :], AF.Exp, scale=-1.0))
            cnt[("a_s_phi", c)] = sms.n

        @blk.vector
        def _(v, c=c, p=p):
            wt(v, sms, ("a_s_phi", c))
            v.scalar_tensor_tensor(ktil[p][:, :], krop[p][:, :], 0.0,
                                   phi2[p][:, :], ALU.max, ALU.add)
            smv.inc(v.drain())
            cnt[("a_v_ktil", c)] = smv.n
            if c >= 1:
                wt(v, smt, ("a_t_upd", c - 1))

                smv.inc(v.tensor_copy(pkvst[p][:, :], kvps[:, :]))
                cnt[("a_v_pref", c)] = smv.n

        @blk.scalar
        def _(sc, c=c, p=p):
            if c >= 1:
                wt(sc, smt, ("a_t_upd", c - 1))

                sms.inc(sc.copy(pksst[p][:, :], ksps[:, :]))
                cnt[("a_s_prefks", c)] = sms.n


        @blk.sync
        def _(s, c=c, p=p):
            wt(s, smv, ("a_v_ktil", c))
            if c >= 2:
                w_a.wait(s, c - 2)
            for j in range(NDCH):
                d_kt.sem(c).inc(s.dma_start(out=kfmb[p][:, j * 128:(j + 1) * 128],
                                            in_=ktil[p][:, j * 128:(j + 1) * 128],
                                            transpose=True), 16)
            d_kt.mark(c)

        @blk.gpsimd
        def _(g, c=c, p=p):
            d_xt.wait(g, c)
            w_a.sem(c).inc(g.dma_start(
                out=xn_sp.ap()[:, :, c * 128:(c + 1) * 128].transpose([1, 0, 2]),
                in_=xnfmb[p][:, :].rearrange("p (j t) -> p j t", t=128)), 16)
            d_kt.wait(g, c)
            w_a.sem(c).inc(g.dma_start(out=kfm_sp.ap()[c * 128:(c + 1) * 128, :],
                                       in_=kfmb[p][:, :]), 16)
            w_a.sem(c).inc(g.dma_start(out=ktm_sp.ap()[c * 128:(c + 1) * 128, :],
                                       in_=ktil[p][:, :]), 16)
            wt(g, sms, ("a_s_evkv", c))
            w_a.sem(c).inc(g.dma_start(out=v_sp.ap()[c * 128:(c + 1) * 128, :],
                                       in_=vbuf[p][:, :]), 16)
            if c >= 1:
                wt(g, smv, ("a_v_pref", c))
                w_a.sem(c).inc(g.dma_start(out=pkv_sp.ap()[c * 128:(c + 1) * 128, :],
                                           in_=pkvst[p][:, :]), 16)
                wt(g, sms, ("a_s_prefks", c))
                w_a.sem(c).inc(g.dma_start(out=pks_sp.ap()[c:c + 1, :],
                                           in_=pksst[p][:, :]), 16)
            w_a.mark(c)

        @blk.tensor
        def _(t, c=c, p=p):
            wt(t, smv, ("a_v_ktil", c))
            if c == 0:
                wt(t, smv, "v_init")
                t.matmul(kvps[:, :], ltri[:, :], zeros_b[:, :], start=True, stop=False,
                         skip_group_check=True)
            else:
                wt(t, smv, ("a_v_pref", c))
                wt(t, sms, ("a_s_prefks", c))
            for h in range(H):
                hp, hh = h % 2, h // 2
                t.matmul(kvps[64 * hp:64 * hp + 64, 64 * hh:64 * hh + 64],
                         ktil[p][:, h * 64:(h + 1) * 64], vbuf[p][:, h * 64:(h + 1) * 64],
                         start=False, stop=False, tile_position=(0, 64 * hp),
                         skip_group_check=True)
            for i in range(2):
                t.matmul(ksps[:, i * 512:(i + 1) * 512], ones_c[:, :],
                         ktil[p][:, i * 512:(i + 1) * 512],
                         start=(c == 0), stop=(c == NC - 1), skip_group_check=True)
            smt.inc(t.drain())
            cnt[("a_t_upd", c)] = smt.n

    if PHASES < 2:
        return _finish()

    @blk.vector
    def _(v):
        wt(v, smt, ("a_t_upd", NC - 1))
        v.tensor_scalar_mul(arst[:, :], kvps[:, :], mke[:, 0:1])
        smv.inc(v.drain())
        cnt["ar_st"] = smv.n

    @blk.scalar
    def _(sc):
        wt(sc, smt, ("a_t_upd", NC - 1))
        sms.inc(sc.activation(arks[:, :], ksps[:, :], AF.Copy, scale=mke[0:1, 0:1]))
        cnt["ar_ks"] = sms.n

    @blk.tensor
    def _(t):
        wt(t, smv, ("a_v_cast", NC - 1))
        smt.inc(t.transpose(smps, mxacc[:, :], identf[:, :]))
        cnt["t_mxtp"] = smt.n

    @blk.vector
    def _(v):
        wt(v, smt, "t_mxtp")
        v.reduce_max(g1[0:1, 0:1], smps, AX)
        smv.inc(v.drain())
        cnt["v_mxg"] = smv.n

    @blk.gpsimd
    def _(g):
        w_a.wait_all(g)
        wt(g, smv, "ar_st")
        wt(g, sms, "ar_ks")
        kvv = ar3i.ap()[0:1, 0:65536].rearrange("o (p c) -> o p c", p=128).squeeze(0)
        s_gb.inc(g.dma_start(out=kvv, in_=arst[:, :]), 16)
        s_gb.inc(g.dma_start(out=ar3i.ap()[0:1, 65536:65536 + D], in_=arks[:, :]), 16)
        g.wait_ge(s_gb.h, s_gb.n)
        grp3 = [[2 * i, 2 * i + 1] for i in range(ncores // 2)] if ncores > 1 else [[0]]
        s_cc.inc(g.collective_compute("AllReduce", ALU.add, replica_groups=grp3,
                                      ins=[ar3i.ap().opt()], outs=[ar3o.ap().opt()]))
        g.wait_ge(s_cc.h, s_cc.n)
        kvo = ar3o.ap()[0:1, 0:65536].rearrange("o (p c) -> o p c", p=128).squeeze(0)
        s_gb.inc(g.dma_start(out=kvinf[:, :], in_=kvo), 16)
        s_gb.inc(g.dma_start(out=ksuminf[:, :], in_=ar3o.ap()[0:1, 65536:65536 + D]), 16)
        cnt["g_ar3"] = s_gb.n
        g.wait_ge(s_gb.h, s_gb.n)
        wt(g, smv, "v_mxg")
        s_gb.inc(g.dma_start(out=ar1i.ap(), in_=g1[0:1, 0:1]), 16)
        g.wait_ge(s_gb.h, s_gb.n)
        s_cc.inc(g.collective_compute("AllReduce", ALU.max,
                                      replica_groups=[list(range(ncores))],
                                      ins=[ar1i.ap().opt()], outs=[ar1o.ap().opt()]))
        g.wait_ge(s_cc.h, s_cc.n)
        s_gb.inc(g.dma_start(out=g1[0:1, 1:2], in_=ar1o.ap()), 16)
        cnt["g_ar1"] = s_gb.n
        g.wait_ge(s_gb.h, s_gb.n)

    @blk.vector
    def _(v):
        wt(v, s_gb, "g_ar3")
        v.tensor_scalar_mul(kvin_b[:, :], kvinf[:, :], mko[:, 0:1])
        v.tensor_scalar_mul(ksumin_b[:, :], ksuminf[:, :], mko[0:1, 0:1])
        smv.inc(v.drain())
        cnt["v_kvin"] = smv.n
        wt(v, s_gb, "g_ar1")
        v.tensor_scalar_max(g1[0:1, 2:3], g1[0:1, 1:2], 1e-6)
        v.drain()
        v.tensor_scalar_mul(g1[0:1, 3:4], g1[0:1, 2:3], 1.0 / 127.0)
        v.drain()
        v.reciprocal(scrow[0:1, SC_RSX:SC_RSX + 1], g1[0:1, 3:4])
        smv.inc(v.drain())
        cnt["v_sc1a"] = smv.n

    if PHASES < 3:
        return _finish()

    def emit_w1_max(d):
        @blk.sync
        def _(s, d=d):
            if d >= 1:
                wt(s, smv, ("w1mx", d - 1))
            d_w1.sem(d).inc(s.dma_start(out=w1f[:, :],
                                        in_=w1_h.ap()[d * 128:(d + 1) * 128, :]), 16)
            d_w1.mark(d)

        @blk.vector
        def _(v, d=d):
            d_w1.wait(v, d)
            v.reduce_max(tmx[d % 2][:, :], w1f[:, :], AX, apply_absolute_value=True)
            v.drain()
            v.tensor_max(w1mx[:, :], w1mx[:, :], tmx[d % 2][:, :])
            smv.inc(v.drain())
            cnt[("w1mx", d)] = smv.n

    def emit_w1_scale():
        @blk.tensor
        def _(t):
            wt(t, smv, ("w1mx", NDCH - 1))
            smt.inc(t.transpose(smps, w1mx[:, :], identf[:, :]))
            cnt["t_w1tp"] = smt.n

        @blk.vector
        def _(v):
            wt(v, smt, "t_w1tp")
            v.reduce_max(g1[0:1, 4:5], smps, AX)
            v.drain()
            v.tensor_scalar_max(g1[0:1, 5:6], g1[0:1, 4:5], 1e-6)
            v.drain()
            v.tensor_scalar_mul(g1[0:1, 6:7], g1[0:1, 5:6], 1.0 / 127.0)
            v.drain()
            v.reciprocal(scrow[0:1, SC_RSW1:SC_RSW1 + 1], g1[0:1, 6:7])
            v.drain()
            wt(v, smv, "v_sc1a")
            v.tensor_mul(scrow[0:1, SC_SWS1:SC_SWS1 + 1], g1[0:1, 3:4], g1[0:1, 6:7])
            smv.inc(v.drain())
            cnt["v_sc1"] = smv.n

        @blk.tensor
        def _(t):
            wt(t, smv, "v_sc1")
            smt.inc(t.matmul(bcps, ones_rf[:, :], scrow[:, :], start=True, stop=True))
            cnt["t_bc1"] = smt.n

        @blk.vector
        def _(v):
            wt(v, smt, "t_bc1")
            v.tensor_copy(scb[:, :], bcps)
            v.drain()
            v.tensor_scalar_mul(b1s[:, :], b1fm[:, :], scb[:, SC_SWS1:SC_SWS1 + 1])
            smv.inc(v.drain())
            cnt["v_bc1"] = smv.n

    def emit_w1_quant(d, guard):
        @blk.sync
        def _(s, d=d):
            if d >= 1:
                wt(s, sms, ("w1q", d - 1))
            else:
                wt(s, smv, ("w1mx", NDCH - 1))
            d_w1.sem(NDCH + d).inc(s.dma_start(out=w1f[:, :],
                                               in_=w1_h.ap()[d * 128:(d + 1) * 128, :]), 16)
            d_w1.mark(NDCH + d)

        @blk.scalar
        def _(sc, d=d, guard=guard):
            d_w1.wait(sc, NDCH + d)
            wt(sc, smv, "v_bc1")
            if guard is not None:
                sc.wait_ge(smt.h, guard)
            sc.activation(wqscr[:, :], w1f[:, :], AF.Copy,
                          scale=scb[:, SC_RSW1:SC_RSW1 + 1], bias=MAGIC)
            sc.drain()
            sms.inc(sc.add(war[:, d * FF:(d + 1) * FF], wqscr[:, :], -MAGIC))
            cnt[("w1q", d)] = sms.n

    def emit_w2_max(jw):
        @blk.sync
        def _(s, jw=jw):
            if jw >= 2:
                wt(s, smv, ("w2mx", jw - 2))
            else:
                wt(s, sms, ("w1q", NDCH - 1))
            d_w2.sem(jw).inc(s.dma_start(out=w2f[jw % 2][:, :],
                                         in_=w2_h.ap()[jw * 128:(jw + 1) * 128, :]), 16)
            d_w2.mark(jw)

        @blk.vector
        def _(v, jw=jw):
            d_w2.wait(v, jw)
            v.reduce_max(tmx[jw % 2][:, :], w2f[jw % 2][:, :], AX, apply_absolute_value=True)
            v.drain()
            v.tensor_max(w2mx[:, :], w2mx[:, :], tmx[jw % 2][:, :])
            smv.inc(v.drain())
            cnt[("w2mx", jw)] = smv.n

    def emit_w2_scale():
        @blk.tensor
        def _(t):
            wt(t, smv, ("w2mx", NJCH - 1))
            smt.inc(t.transpose(smps, w2mx[:, :], identf[:, :]))
            cnt["t_w2tp"] = smt.n

        @blk.vector
        def _(v):
            wt(v, smt, "t_w2tp")
            v.reduce_max(g1[0:1, 7:8], smps, AX)
            v.drain()
            v.tensor_scalar_max(g1[0:1, 8:9], g1[0:1, 7:8], 1e-6)
            v.drain()
            v.tensor_scalar_mul(g1[0:1, 9:10], g1[0:1, 8:9], 1.0 / 127.0)
            v.drain()
            v.reciprocal(scrow2[0:1, SC2_RSW2:SC2_RSW2 + 1], g1[0:1, 9:10])
            smv.inc(v.drain())
            cnt["v_sc2w"] = smv.n

        @blk.tensor
        def _(t):
            wt(t, smv, "v_sc2w")
            smt.inc(t.matmul(bcps, ones_rf[:, :], scrow2[:, :], start=True, stop=True))
            cnt["t_bc2w"] = smt.n

        @blk.vector
        def _(v):
            wt(v, smt, "t_bc2w")
            v.tensor_copy(scb2[:, :], bcps)
            smv.inc(v.drain())
            cnt["v_bc2w"] = smv.n

    def emit_w2_quant(jw):
        @blk.sync
        def _(s, jw=jw):
            if jw >= 2:
                w_w2.wait(s, jw - 2)
                wt(s, sms, ("w2q", jw - 2))
            else:
                wt(s, smv, ("w2mx", NJCH - 2 + jw))
            d_w2.sem(NJCH + jw).inc(s.dma_start(out=w2f[jw % 2][:, :],
                                                in_=w2_h.ap()[jw * 128:(jw + 1) * 128, :]), 16)
            d_w2.mark(NJCH + jw)

        @blk.scalar
        def _(sc, jw=jw):
            d_w2.wait(sc, NJCH + jw)
            wt(sc, smv, "v_bc2w")
            sc.activation(w2f[jw % 2][:, :], w2f[jw % 2][:, :], AF.Copy,
                          scale=scb2[:, SC2_RSW2:SC2_RSW2 + 1], bias=MAGIC)
            sc.drain()
            sms.inc(sc.add(wq2c[jw % 2][:, :], w2f[jw % 2][:, :], -MAGIC))
            cnt[("w2q", jw)] = sms.n
            sc.drain()
            w_w2.sem(jw).inc(sc.dma_start(out=wq2_sp.ap()[jw * 128:(jw + 1) * 128, :],
                                          in_=wq2c[jw % 2][:, :]), 16)
            w_w2.mark(jw)

    w1max_sched = {4 + d: d for d in range(NDCH)} if NC >= 24 else {}
    w1scale_at = 13 if NC >= 24 else None
    w1q_sched = {14 + d: d for d in range(4)} if NC >= 24 else {}

    for c in (range(0) if NOB else range(NC)):
        p = c % 2

        @blk.sync
        def _(s, c=c, p=p):
            if c >= 2:
                wt(s, smt, ("b_t_y", c - 2))
                wt(s, smt, ("b_t_kc", c - 2))
                wt(s, smt, ("b_t_q", c - 2))
                wt(s, smt, ("b_t_st", c - 2, 1))
                wt(s, smv, ("b_v_opart", c - 2))
                wt(s, smv, ("b_v_ptot", c - 2))
            if c == 0:
                w_a.wait_all(s)
                s.wait_ge(s_gb.h, 96)
            for dst, sp in ((kfmB[p], kfm_sp), (ktmB[p], ktm_sp), (vB[p], v_sp)):
                d_rl.sem(c).inc(s.dma_start(out=dst[:, :],
                                            in_=sp.ap()[c * 128:(c + 1) * 128, :]), 16)
            d_rl.sem(c).inc(s.dma_start(out=crepf[p][:, 0:512],
                                        in_=crep_h.ap()[c * 128:(c + 1) * 128, :]), 16)
            d_rl.sem(c).inc(s.dma_start(out=crepf[p][:, 512:1024],
                                        in_=srep_h.ap()[c * 128:(c + 1) * 128, :]), 16)
            d_rl.sem(c).inc(s.dma_start(out=xnB[p][:, :],
                                        in_=xn_sp.ap()[:, :, c * 128:(c + 1) * 128]
                                        .transpose([1, 0, 2])), 16)
            if c >= 1:
                d_rl.sem(c).inc(s.dma_start(out=ptot[p][:, :],
                                            in_=pkv_sp.ap()[c * 128:(c + 1) * 128, :]), 16)
                d_rl.sem(c).inc(s.dma_start(out=prow[p][:, :],
                                            in_=pks_sp.ap()[c:c + 1, :]), 16)
            d_rl.mark(c)
            d_xr.sem(c).inc(s.dma_start(out=xrel[p][:, :],
                                        in_=x_h.ap()[c * 128:(c + 1) * 128, :]), 16)
            d_xr.mark(c)

        @blk.tensor
        def _(t, c=c, p=p):
            d_rl.wait(t, c)
            if c >= 1:
                wt(t, sms, ("b_s_evq", c - 1))
                wt(t, smv, ("b_v_opart", c - 1))
            if c == 0:
                wt(t, sms, ("a_s_evkv", NC - 1))
            for i in range(2):
                for j in range(NDCH):
                    t.matmul(psA[i][:, :], xnB[p][:, j * 128:(j + 1) * 128],
                             war[:, WQ + j * D + i * 512: WQ + j * D + i * 512 + 512],
                             start=(j == 0), stop=(j == NDCH - 1))
            smt.inc(t.drain())
            cnt[("b_t_q", c)] = smt.n

        @blk.scalar
        def _(sc, c=c, p=p):
            wt(sc, smt, ("b_t_q", c))
            sc.copy(qtmp[p][:, 0:512], psA[0][:, :])
            sms.inc(sc.copy(qtmp[p][:, 512:1024], psA[1][:, :]))
            cnt[("b_s_evq", c)] = sms.n

        @blk.vector
        def _(v, c=c, p=p):
            d_rl.wait(v, c)
            v.tensor_copy(crepb[p][:, :], crepf[p][:, :])
            wt(v, sms, ("b_s_evq", c))
            if c >= 2:
                d_qt.wait(v, c - 2)
            q1 = r3(qtmp[p])[:, :, 0:32]
            q2 = r3(qtmp[p])[:, :, 32:64]
            o1 = r3(qrop[p])[:, :, 0:32]
            o2 = r3(qrop[p])[:, :, 32:64]
            cc_ = c3(crepb[p][:, 0:512])
            ss_ = c3(crepb[p][:, 512:1024])
            sA, sB2 = c3(ropes[:, 0:512]), c3(ropes[:, 512:1024])
            sC, sD = c3(ropes[:, 1024:1536]), c3(ropes[:, 1536:2048])
            v.drain()
            v.tensor_tensor(sA, q1, cc_, ALU.mult)
            v.tensor_tensor(sB2, q2, ss_, ALU.mult)
            v.tensor_tensor(sC, q1, ss_, ALU.mult)
            v.tensor_tensor(sD, q2, cc_, ALU.mult)
            v.drain()
            v.tensor_tensor(o1, sA, sB2, ALU.subtract)
            v.tensor_tensor(o2, sC, sD, ALU.add)
            smv.inc(v.drain())
            cnt[("b_v_rope", c)] = smv.n

        @blk.scalar
        def _(sc, c=c, p=p):
            wt(sc, smv, ("b_v_rope", c))
            sc.activation(phi1[p][:, :], qrop[p][:, :], AF.Relu, scale=-1.0)
            sc.drain()
            sms.inc(sc.activation(phi2[p][:, :], phi1[p][:, :], AF.Exp, scale=-1.0))
            cnt[("b_s_phi", c)] = sms.n

        @blk.vector
        def _(v, c=c, p=p):
            wt(v, sms, ("b_s_phi", c))
            v.scalar_tensor_tensor(qtil[p][:, :], qrop[p][:, :], 0.0,
                                   phi2[p][:, :], ALU.max, ALU.add)
            smv.inc(v.drain())
            cnt[("b_v_qtil", c)] = smv.n
            if c == 0:
                wt(v, smv, "v_kvin")
            if c >= 2:
                wt(v, smt, ("b_t_y", c - 2))
                wt(v, smt, ("b_t_kc", c - 2))
            d_rl.wait(v, c)
            if c >= 1:
                v.tensor_add(ptot[p][:, :], ptot[p][:, :], kvin_b[:, :])
                v.tensor_add(prow[p][:, :], prow[p][:, :], ksumin_b[:, :])
            else:
                v.tensor_copy(ptot[p][:, :], kvin_b[:, :])
                v.tensor_copy(prow[p][:, :], ksumin_b[:, :])
            smv.inc(v.drain())
            cnt[("b_v_ptot", c)] = smv.n

        @blk.tensor
        def _(t, c=c, p=p):
            wt(t, smv, ("b_v_ptot", c))
            if c >= 1:
                wt(t, sms, ("b_s_evy", c - 1))
            for i in range(2):
                t.matmul(psB[i][:, :], ltri[:, :], ktmB[p][:, i * 512:(i + 1) * 512],
                         start=True, stop=False)
                t.matmul(psB[i][:, :], ones_r[:, :], prow[p][0:1, i * 512:(i + 1) * 512],
                         start=False, stop=True)
            smt.inc(t.drain())
            cnt[("b_t_kc", c)] = smt.n

        @blk.scalar
        def _(sc, c=c, p=p):
            wt(sc, smt, ("b_t_kc", c))
            sc.copy(kcb[p][:, 0:512], psB[0][:, :])
            sms.inc(sc.copy(kcb[p][:, 512:1024], psB[1][:, :]))
            cnt[("b_s_kcb", c)] = sms.n

        @blk.vector
        def _(v, c=c, p=p):
            wt(v, sms, ("b_s_kcb", c))
            v.tensor_mul(kcb[p][:, :], qtil[p][:, :], kcb[p][:, :])
            v.drain()
            v.tensor_reduce(den_s[p][:, :].rearrange("p (h o) -> p h o", o=1),
                            kcb[p][:, :].rearrange("p (h d) -> p h d", d=64),
                            AX, ALU.add)
            v.drain()
            v.tensor_scalar_max(den_s[p][:, :], den_s[p][:, :], 1e-6)
            v.drain()
            v.reciprocal(den_s[p][:, :], den_s[p][:, :])
            v.drain()
            v.tensor_copy(rdrep[:, :].rearrange("p (h d) -> p h d", d=64),
                          den_s[p][:, 0:H].unsqueeze(2).broadcast_to([128, H, 64]))
            v.drain()
            v.tensor_mul(qhat[p][:, :], qtil[p][:, :], rdrep[:, :])
            smv.inc(v.drain())
            cnt[("b_v_qhat", c)] = smv.n

        @blk.sync
        def _(s, c=c, p=p):
            wt(s, smv, ("b_v_qhat", c))
            if c >= 2:
                wt(s, smt, ("b_t_y", c - 2))
                wt(s, smt, ("b_t_st", c - 2, 1))
            for j in range(NDCH):
                d_qt.sem(c).inc(s.dma_start(out=qfmB[p][:, j * 128:(j + 1) * 128],
                                            in_=qhat[p][:, j * 128:(j + 1) * 128],
                                            transpose=True), 16)
            d_qt.mark(c)

        for r in range(2):
            @blk.tensor
            def _(t, c=c, p=p, r=r):
                d_qt.wait(t, c)
                if r == 0 and c >= 1:
                    wt(t, smv, ("b_v_mask", c - 1, 1))
                if r == 1:
                    wt(t, smv, ("b_v_mask", c, 0))
                for hh in (-1, 0, 2, 4, 6, -1, 1, 3, 5, 7):
                    if hh < 0:
                        t.matmul(kvps[0:8, 0:8], ltri[:, 0:8], zeros_b[:, 0:8],
                                 start=True, stop=True, skip_group_check=True)
                        continue
                    h = r * 8 + hh
                    hp, pr = h % 2, h // 2
                    t.matmul(psA[hh // 4][:, (hh % 4) * 128:(hh % 4) * 128 + 128],
                             kfmB[p][64 * hp:64 * hp + 64, pr * 128:(pr + 1) * 128],
                             qfmB[p][64 * hp:64 * hp + 64, pr * 128:(pr + 1) * 128],
                             start=(hh in (0, 4)), stop=(hh in (3, 7)),
                             tile_position=(64 * hp, 0), skip_group_check=True)
                smt.inc(t.drain())
                cnt[("b_t_st", c, r)] = smt.n

            @blk.vector
            def _(v, c=c, p=p, r=r):
                wt(v, smt, ("b_t_st", c, r))
                if r == 0 and c >= 1:
                    wt(v, smt, ("b_t_y", c - 1))
                for b in range(2):
                    v.tensor_mul(stmask[:, r * 1024 + b * 512: r * 1024 + b * 512 + 512],
                                 psA[b][:, :], mtri[:, :])
                smv.inc(v.drain())
                cnt[("b_v_mask", c, r)] = smv.n

            @blk.tensor
            def _(t, c=c, p=p, r=r):
                wt(t, smv, ("b_v_mask", c, r))
                for hh in range(8):
                    h = r * 8 + hh
                    hp, pr = h % 2, h // 2
                    yb = psB[(h * 64) // 512]
                    ycol = (h * 64) % 512
                    t.matmul(yb[:, ycol:ycol + 64],
                             stmask[:, h * 128:(h + 1) * 128],
                             vB[p][:, h * 64:(h + 1) * 64],
                             start=(h % 8 == 0), stop=False, skip_group_check=True)
                    t.matmul(yb[:, ycol:ycol + 64],
                             qfmB[p][64 * hp:64 * hp + 64, pr * 128:(pr + 1) * 128],
                             ptot[p][64 * hp:64 * hp + 64, 64 * pr:64 * pr + 64],
                             start=False, stop=(h % 8 == 7),
                             tile_position=(64 * hp, 0), skip_group_check=True)
                smt.inc(t.drain())
                cnt[("b_t_y", c) if r == 1 else ("b_t_y0", c)] = smt.n

        @blk.scalar
        def _(sc, c=c, p=p):
            wt(sc, smt, ("b_t_y", c))
            sc.copy(ytm[p][:, 0:512], psB[0][:, :])
            sms.inc(sc.copy(ytm[p][:, 512:1024], psB[1][:, :]))
            cnt[("b_s_evy", c)] = sms.n

        @blk.sync
        def _(s, c=c, p=p):
            wt(s, sms, ("b_s_evy", c))
            if c >= 2:
                wt(s, smt, ("b_t_wo", c - 2))
            for j in range(NDCH):
                d_yt.sem(c).inc(s.dma_start(out=yfm[p][:, j * 128:(j + 1) * 128],
                                            in_=ytm[p][:, j * 128:(j + 1) * 128],
                                            transpose=True), 16)
            d_yt.mark(c)

        @blk.tensor
        def _(t, c=c, p=p):
            d_yt.wait(t, c)
            if c >= 1:
                wt(t, smv, ("b_v_opart", c - 1))
            for i in range(2):
                for j in range(NDCH):
                    t.matmul(psA[i][:, :], yfm[p][:, j * 128:(j + 1) * 128],
                             war[:, WO + j * D + i * 512: WO + j * D + i * 512 + 512],
                             start=(j == 0), stop=(j == NDCH - 1))
            smt.inc(t.drain())
            cnt[("b_t_wo", c)] = smt.n

        @blk.vector
        def _(v, c=c, p=p):
            wt(v, smt, ("b_t_wo", c))
            d_xr.wait(v, c)
            if c >= 2:
                w_pt.wait(v, c - 2)
            for i in range(2):
                v.tensor_add(opart[p][:, i * 512:(i + 1) * 512], psA[i][:, :],
                             xrel[p][:, i * 512:(i + 1) * 512])
            smv.inc(v.drain())
            cnt[("b_v_opart", c)] = smv.n

        @blk.scalar
        def _(sc, c=c, p=p):
            wt(sc, smv, ("b_v_opart", c))
            w_pt.sem(c).inc(sc.dma_start(out=part_sp.ap()[c * 128:(c + 1) * 128, :],
                                         in_=opart[p][:, :]), 16)
            w_pt.mark(c)

        if c in w1max_sched:
            emit_w1_max(w1max_sched[c])
        if w1scale_at == c:
            emit_w1_scale()
        if c in w1q_sched:
            emit_w1_quant(w1q_sched[c], None)

    if NC < 24 or NOB:
        for d in range(NDCH):
            emit_w1_max(d)
        emit_w1_scale()
        for d in range(4):
            emit_w1_quant(d, cnt.get(("b_t_wo", NC - 1)))
    for d in range(4, NDCH):
        emit_w1_quant(d, cnt.get(("b_t_wo", NC - 1)))

    if PHASES < 3.5:
        return _finish()

    w2max_sched = {}
    w2q_sched = {}
    w2scale_at = None
    if NTT >= 8:
        for tt in range(4):
            w2max_sched[tt] = [tt * 8 + jj for jj in range(8)]
        w2scale_at = 4
        for tt in (5, 6):
            w2q_sched[tt] = [(tt - 5) * 16 + jj for jj in range(16)]

    for tt in range(NTT):
        p = tt % 2

        @blk.sync
        def _(s, tt=tt, p=p):
            if tt >= 2:
                wt(s, smv, ("m1_xq", tt - 2))
            if tt == 0:
                wt(s, smt, ("b_t_y", NC - 1))
                wt(s, smt, ("b_t_wo", NC - 1))
                wt(s, smv, ("b_v_opart", NC - 1))
                wt(s, sms, ("b_s_evy", NC - 1))
            if not M1NOXM:
                d_xm.sem(tt).inc(s.dma_start(out=xnm1[p][:, :],
                                             in_=xn_sp.ap()[:, :, tt * 512:(tt + 1) * 512]
                                             .transpose([1, 0, 2])), 16)
            d_xm.mark(tt)

        @blk.vector
        def _(v, tt=tt, p=p):
            d_xm.wait(v, tt)
            if tt == 0:
                wt(v, sms, ("w1q", NDCH - 1))
                w_pt.wait(v, NC - 1)
                w_pt.wait(v, NC - 2)
            if tt >= 2:
                wt(v, smt, ("m1_t", tt - 2, NJCH - 1))
            if not M1NOXQ:
                v.tensor_scalar(qscr[:, :], xnm1[p][:, :], scb[:, SC_RSX:SC_RSX + 1],
                                MAGIC, ALU.mult, ALU.add)
                v.drain()
                v.tensor_scalar(xqb[p][:, :], qscr[:, :], MAGIC, None, ALU.subtract)
            smv.inc(v.drain())
            cnt[("m1_xq", tt)] = smv.n

        for j in range(NJCH):
            bank = j % (2 if M1B2 else 4)

            @blk.tensor
            def _(t, tt=tt, p=p, j=j, bank=bank):
                wt(t, smv, ("m1_xq", tt))
                idx = tt * NJCH + j
                nb = 2 if M1B2 else 4
                if idx == 0:
                    wt(t, sms, ("w1q", NDCH - 1))
                if idx >= nb:
                    wt(t, sms, ("m1_h", idx - nb))
                tgt = psA[bank] if bank < 2 else psB[bank - 2]
                for d in range(1 if M1LITE else NDCH):
                    t.matmul(tgt[:, :], war[:, d * FF + j * 128: d * FF + j * 128 + 128],
                             xqb[p][:, d * 512:(d + 1) * 512],
                             start=(d == 0), stop=(d == (0 if M1LITE else NDCH - 1)))
                smt.inc(t.drain())
                cnt[("m1_t", tt, j)] = smt.n

            @blk.scalar
            def _(sc, tt=tt, p=p, j=j, bank=bank):
                wt(sc, smt, ("m1_t", tt, j))
                idx = tt * NJCH + j
                nb = 2 if M1B2 else 4
                if idx >= nb:
                    w_h.wait(sc, idx // 4 - 1)
                    wt(sc, smv, ("m1_hmx", idx - nb))
                tgt = psA[bank] if bank < 2 else psB[bank - 2]
                fn = AF.Identity if NOGELU else AF.Gelu
                if M1EVC:
                    sms.inc(sc.copy(hb[idx % 4][:, :], tgt[:, :]))
                else:
                    sms.inc(sc.activation(hb[idx % 4][:, :], tgt[:, :], fn,
                                          scale=scb[:, SC_SWS1:SC_SWS1 + 1],
                                          bias=b1s[:, j:j + 1]))
                cnt[("m1_h", idx)] = sms.n
                if j % 4 == 3:
                    gidx = idx // 4
                    sc.drain()
                    if not M1NOSP:
                        for jj in range(j - 3, j + 1):
                            w_h.sem(gidx).inc(sc.dma_start(
                                out=h_sp.ap()[4 * tt:4 * tt + 4, jj * 128:(jj + 1) * 128, :]
                                .transpose([1, 0, 2]),
                                in_=hb[jj % 4][:, :].rearrange("p (b t) -> p b t", t=128)), 16)
                    w_h.mark(gidx)

            @blk.vector
            def _(v, tt=tt, j=j):
                idx = tt * NJCH + j
                wt(v, sms, ("m1_h", idx))
                if not M1NOHM:
                    v.reduce_max(thm[:, idx % 4:idx % 4 + 1], hb[idx % 4][:, :], AX,
                                 apply_absolute_value=True)
                    if idx % 4 == 3:
                        v.drain()
                        v.reduce_max(thm4[:, :], thm[:, :], AX)
                        v.drain()
                        v.tensor_max(hmx[:, :], hmx[:, :], thm4[:, :])
                smv.inc(v.drain())
                cnt[("m1_hmx", idx)] = smv.n

        if tt in w2max_sched:
            for jw in w2max_sched[tt]:
                emit_w2_max(jw)
        if w2scale_at == tt:
            emit_w2_scale()
        if tt in w2q_sched:
            for jw in w2q_sched[tt]:
                emit_w2_quant(jw)

    if PHASES < 3.8:
        return _finish()

    if w2scale_at is None:
        for jw in range(NJCH):
            emit_w2_max(jw)
        emit_w2_scale()
        for jw in range(NJCH):
            emit_w2_quant(jw)

    @blk.tensor
    def _(t):
        wt(t, smv, ("m1_hmx", NTT * NJCH - 1))
        smt.inc(t.transpose(smps, hmx[:, :], identf[:, :]))
        cnt["t_htp"] = smt.n

    @blk.vector
    def _(v):
        wt(v, smt, "t_htp")
        v.reduce_max(g1[0:1, 10:11], smps, AX)
        smv.inc(v.drain())
        cnt["v_hg"] = smv.n

    @blk.gpsimd
    def _(g):
        w_h.wait_all(g)
        w_pt.wait_all(g)
        w_w2.wait_all(g)
        wt(g, smv, "v_hg")
        s_gb.inc(g.dma_start(out=ar2i.ap(), in_=g1[0:1, 10:11]), 16)
        g.wait_ge(s_gb.h, s_gb.n)
        s_cc.inc(g.collective_compute("AllReduce", ALU.max,
                                      replica_groups=[list(range(ncores))],
                                      ins=[ar2i.ap().opt()], outs=[ar2o.ap().opt()]))
        g.wait_ge(s_cc.h, s_cc.n)
        s_gb.inc(g.dma_start(out=g1[0:1, 11:12], in_=ar2o.ap()), 16)
        cnt["g_ar2"] = s_gb.n
        g.wait_ge(s_gb.h, s_gb.n)

    @blk.vector
    def _(v):
        wt(v, s_gb, "g_ar2")
        v.tensor_scalar_max(g1[0:1, 12:13], g1[0:1, 11:12], 1e-6)
        v.drain()
        v.tensor_scalar_mul(g1[0:1, 13:14], g1[0:1, 12:13], 1.0 / 127.0)
        v.drain()
        v.reciprocal(scrow2[0:1, SC2_RSH:SC2_RSH + 1], g1[0:1, 13:14])
        v.drain()
        v.tensor_mul(scrow2[0:1, SC2_S2B:SC2_S2B + 1], g1[0:1, 13:14], g1[0:1, 9:10])
        smv.inc(v.drain())
        cnt["v_sc2"] = smv.n

    @blk.tensor
    def _(t):
        wt(t, smv, "v_sc2")
        smt.inc(t.matmul(bcps, ones_rf[:, :], scrow2[:, :], start=True, stop=True))
        cnt["t_bc2"] = smt.n

    @blk.vector
    def _(v):
        wt(v, smt, "t_bc2")
        v.tensor_copy(scb2[:, :], bcps)
        smv.inc(v.drain())
        cnt["v_bc2"] = smv.n

    @blk.sync
    def _(s):
        wt(s, smt, ("m1_t", NTT - 1, NJCH - 1))
        wt(s, sms, ("w2q", NJCH - 1))
        s.wait_ge(s_gb.h, 128)
        w_w2.wait_all(s)
        for part in range(4):
            d_q2.sem(0).inc(s.dma_start(
                out=war[:, part * 8192:(part + 1) * 8192],
                in_=wq2_sp.ap()[part * 1024:(part + 1) * 1024, :]
                .rearrange("(j p) i -> p j i", p=128)), 16)
        d_q2.mark(0)

    if PHASES < 5:
        return _finish()

    for tc in range(NC):
        p = tc % 2

        @blk.sync
        def _(s, tc=tc, p=p):
            if tc >= 2:
                wt(s, smv, ("m2_hq", tc - 2))
                wt(s, smv, ("m2_fin", tc - 2))
            if tc == 0:
                w_h.wait_all(s)
                w_pt.wait_all(s)
                wt(s, sms, ("w2q", NJCH - 1))
                wt(s, smv, ("m1_hmx", NTT * NJCH - 1))
                wt(s, smv, ("m1_xq", NTT - 1))
            d_hb.sem(tc).inc(s.dma_start(out=hblf[p][:, :],
                                         in_=h_sp.ap()[tc, :, :]
                                         .rearrange("(j p) t -> p j t", p=128)), 16)
            d_hb.sem(tc).inc(s.dma_start(out=oprel[p][:, :],
                                         in_=part_sp.ap()[tc * 128:(tc + 1) * 128, :]), 16)
            d_hb.mark(tc)

        @blk.vector
        def _(v, tc=tc, p=p):
            d_hb.wait(v, tc)
            if tc == 0:
                wt(v, smv, "v_bc2")
            if tc >= 2:
                wt(v, smt, ("m2_t", tc - 2))
            v.tensor_scalar(hblf[p][:, :], hblf[p][:, :], scb2[:, SC2_RSH:SC2_RSH + 1],
                            MAGIC, ALU.mult, ALU.add)
            v.drain()
            v.tensor_scalar(hqb[p][:, :], hblf[p][:, :], MAGIC, None, ALU.subtract)
            smv.inc(v.drain())
            cnt[("m2_hq", tc)] = smv.n

        @blk.tensor
        def _(t, tc=tc, p=p):
            wt(t, smv, ("m2_hq", tc))
            if tc == 0:
                d_q2.wait(t, 0)
                wt(t, sms, ("m1_h", NTT * NJCH - 1))
            if tc >= 1:
                wt(t, smv, ("m2_fin", tc - 1))
            for i in range(2):
                for j in range(1 if M2LITE else NJCH):
                    t.matmul(psA[i][:, :], hqb[p][:, j * 128:(j + 1) * 128],
                             war[:, j * D + i * 512: j * D + i * 512 + 512],
                             start=(j == 0), stop=False)
                t.matmul(psA[i][:, :], ones_r[:, :], b2b[0:1, i * 512:(i + 1) * 512],
                         start=False, stop=True)
            smt.inc(t.drain())
            cnt[("m2_t", tc)] = smt.n

        @blk.vector
        def _(v, tc=tc, p=p):
            wt(v, smt, ("m2_t", tc))
            if tc >= 2:
                w_out.wait(v, tc - 2)
            for i in range(2):
                v.scalar_tensor_tensor(fin[p][:, i * 512:(i + 1) * 512], psA[i][:, :],
                                       scb2[:, SC2_S2B:SC2_S2B + 1],
                                       oprel[p][:, i * 512:(i + 1) * 512],
                                       ALU.mult, ALU.add)
            smv.inc(v.drain())
            cnt[("m2_fin", tc)] = smv.n

        @blk.scalar
        def _(sc, tc=tc, p=p):
            wt(sc, smv, ("m2_fin", tc))
            w_out.sem(tc).inc(sc.dma_start(out=out_h.ap()[tc * 128:(tc + 1) * 128, :],
                                           in_=fin[p][:, :]), 16)
            w_out.mark(tc)

    @blk.gpsimd
    def _(g):
        w_out.wait_all(g)

    es.close()
    return nc


def _host_tables(NT, pos0):
    pos = np.arange(pos0, pos0 + NT, dtype=np.float64)
    inv = 10000.0 ** (-np.arange(32, dtype=np.float64) / 32.0)
    ang = pos[:, None] * inv[None, :]
    crep = np.tile(np.cos(ang).astype(np.float32), (1, H))
    srep = np.tile(np.sin(ang).astype(np.float32), (1, H))
    return crep, srep


def make_in_maps(x, Wq, Wk, Wv, Wo, w1, b1, w2, b2, ncores, NC):
    NT = NC * 128
    wqT = np.ascontiguousarray(np.asarray(Wq, np.float32).T)
    wkT = np.ascontiguousarray(np.asarray(Wk, np.float32).T)
    wvT = np.ascontiguousarray(np.asarray(Wv, np.float32).T)
    woT = np.ascontiguousarray(np.asarray(Wo, np.float32).T)
    w1T = np.ascontiguousarray(np.asarray(w1, np.float32).T)
    w2T = np.ascontiguousarray(np.asarray(w2, np.float32).T)
    b1fm = np.ascontiguousarray(np.asarray(b1, np.float32).reshape(NJCH, 128).T)
    b2row = np.ascontiguousarray(np.asarray(b2, np.float32).reshape(1, D))
    tri = np.triu(np.ones((128, 128), np.float32))
    consts = {
        "wqT": wqT, "wkT": wkT, "wvT": wvT, "woT": woT, "w1T": w1T, "w2T": w2T,
        "b1fm": b1fm, "b2row": b2row,
        "ltri": tri.astype(BF), "mtri": np.tile(tri, (1, 4)).astype(np.float32),
        "ones_r": np.ones((1, 128), BF), "ones_c": np.ones((128, 1), BF),
        "ones_rf": np.ones((1, 128), np.float32),
        "identf": np.eye(128, dtype=np.float32),
    }
    in_maps = []
    for core in range(ncores):
        b, half = core // 2, core % 2
        xs = np.ascontiguousarray(np.asarray(x, np.float32)[b, half * NT:(half + 1) * NT, :])
        crep, srep = _host_tables(NT, half * NT)
        m = dict(consts)
        m.update({
            "x": xs, "crep": crep, "srep": srep,
            "mke": np.full((128, 1), 1.0 - half, np.float32),
            "mko": np.full((128, 1), float(half), np.float32),
        })
        in_maps.append(m)
    return in_maps


def kernel(L0_attn_x, L0_attn_Wq, L0_attn_Wk, L0_attn_Wv, L0_attn_Wo,
           L0_ffn_w1, L0_ffn_b1, L0_ffn_w2, L0_ffn_b2):
    NCORES = 8
    NC = 32
    NT = NC * 128
    x = np.asarray(L0_attn_x, dtype=np.float32)
    B, T, _ = x.shape
    nc = build_core_program(NC=NC, ncores=NCORES)
    in_maps = make_in_maps(x, L0_attn_Wq, L0_attn_Wk, L0_attn_Wv, L0_attn_Wo,
                           L0_ffn_w1, L0_ffn_b1, L0_ffn_w2, L0_ffn_b2, NCORES, NC)
    import time as _time
    last = None
    for _attempt in range(6):
        try:
            res = run_bass_kernel_spmd(nc, in_maps, core_ids=list(range(NCORES)),
                                       trace=True)
            break
        except Exception as e:
            last = e
            if _attempt == 5:
                raise
            _time.sleep(10.0)
    out = np.empty((B, T, D), np.float32)
    for core in range(NCORES):
        b, half = core // 2, core % 2
        out[b, half * NT:(half + 1) * NT, :] = res.results[core]["out"]
    return out
